# version 14
# baseline (speedup 1.0000x reference)
import os
import numpy as np
import ml_dtypes
from contextlib import ExitStack

import concourse.bass as bass
import concourse.mybir as mybir
from concourse.bass_utils import run_bass_kernel_spmd

F32 = mybir.dt.float32
BF16 = mybir.dt.bfloat16
I32 = mybir.dt.int32
AF = mybir.ActivationFunctionType
ALU = mybir.AluOpType
AX = mybir.AxisListType
BF = ml_dtypes.bfloat16

CH = 128
FC = 512
EPS = 1e-5

SERIAL_PHASES = set(
    p for p in os.environ.get("KSERIAL", "").split(",") if p)
SERIAL_ALL = os.environ.get("KSERIAL", "") == "all"


class Buf:
    __slots__ = ("name", "w", "r")

    def __init__(self, name):
        self.name = name
        self.w = {}
        self.r = {}


class Planner:
    DMA_STREAMS = ("syncq", "gq", "scq")
    BODY = {"syncq": "sync", "gq": "gpsimd", "scq": "scalar",
            "pe": "tensor", "dve": "vector", "act": "scalar"}

    def __init__(self):
        self.entries = []
        self.counts = {}
        self.phase = "P0"
        self.prev_any = None

    def set_phase(self, name):
        self.phase = name

    def op(self, stream, fn, reads=(), writes=(), sync_self=False):
        idx = self.counts.get(stream, 0)
        deps = {}

        def add(tok, raw=True):
            if tok is None:
                return
            s, i = tok
            if s == stream and i >= idx:
                return
            if s == stream and not sync_self:
                if not (raw and stream in ("dve", "act")):
                    return
            key = (s, i % self.NLANES) if s in self.DMA_STREAMS else s
            if deps.get(key, -1) < i:
                deps[key] = i
        serial = SERIAL_ALL or (self.phase in SERIAL_PHASES)
        if serial and self.prev_any is not None:
            add(self.prev_any)
        for b in reads:
            for (ws, _), i in b.w.items():
                add((ws, i), raw=True)
        for b in writes:
            for (ws, _), i in b.w.items():
                add((ws, i), raw=False)
            for (rs, _), i in b.r.items():
                add((rs, i), raw=False)
        tok = (stream, idx)
        key = ((stream, idx % self.NLANES)
               if stream in self.DMA_STREAMS else (stream, 0))
        for b in reads:
            if b.r.get(key, -1) < idx:
                b.r[key] = idx
        for b in writes:
            if b.w.get(key, -1) < idx:
                b.w[key] = idx
            b.r = {}
        self.entries.append((stream, fn, deps, self.phase))
        self.counts[stream] = idx + 1
        self.prev_any = tok
        return tok

    def dma(self, queue, fn, reads=(), writes=()):
        return self.op(queue, fn, reads, writes)

    NLANES = 16

    def finalize(self, nc, block, sems):
        NL = self.NLANES
        awaited = set()
        for stream, fn, deps, ph in self.entries:
            for key, i in deps.items():
                st = key[0] if isinstance(key, tuple) else key
                awaited.add((st, i))
        idx_ctr = {s: 0 for s in self.counts}
        incs = []
        cum = {s: 0 for s in self.counts}
        val_after = {}
        last_pos = {}
        for i, (stream, fn, deps, ph) in enumerate(self.entries):
            idx = idx_ctr[stream]
            idx_ctr[stream] += 1
            last_pos[stream] = (i, idx)
            if stream in self.DMA_STREAMS:
                incs.append(16)
            else:
                incs.append(1 if (stream, idx) in awaited else 0)
        for stream, (i, idx) in last_pos.items():
            if incs[i] == 0:
                incs[i] = 1
        idx_ctr = {s: 0 for s in self.counts}
        for i, (stream, fn, deps, ph) in enumerate(self.entries):
            idx = idx_ctr[stream]
            idx_ctr[stream] += 1
            if stream in self.DMA_STREAMS:
                val_after[(stream, idx)] = (idx % NL, 16 * (idx // NL + 1))
            else:
                cum[stream] += incs[i]
                val_after[(stream, idx)] = (None, cum[stream])

        bodies = {b: [] for b in ("sync", "gpsimd", "scalar", "tensor", "vector")}
        idx_ctr = {s: 0 for s in self.counts}
        for i, (stream, fn, deps, ph) in enumerate(self.entries):
            idx = idx_ctr[stream]
            idx_ctr[stream] += 1
            waits = {}
            for key, j in deps.items():
                st = key[0] if isinstance(key, tuple) else key
                lane, v = val_after[(st, j)]
                k2 = (st, lane)
                if waits.get(k2, -1) < v:
                    waits[k2] = v
            if stream in self.DMA_STREAMS:
                inc_sem = sems[stream][idx % NL]
                if idx >= NL:
                    k2 = (stream, idx % NL)
                    if waits.get(k2, -1) < 16 * (idx // NL):
                        waits[k2] = 16 * (idx // NL)
            else:
                inc_sem = sems[stream]
            bodies[self.BODY[stream]].append(
                (stream, fn, list(waits.items()), incs[i], inc_sem))

        totals = []
        for stream, cnt in self.counts.items():
            if stream in self.DMA_STREAMS:
                for lane in range(min(self.NLANES, cnt)):
                    n_lane = (cnt - lane - 1) // NL + 1
                    totals.append((sems[stream][lane], 16 * n_lane))
            else:
                totals.append((sems[stream], cum[stream]))

        def mk_body(body_name):
            def body(eng):
                hw = {}
                for stream, fn, waits, inc, inc_sem in bodies[body_name]:
                    for (s, lane), v in waits:
                        if hw.get((s, lane), -1) >= v:
                            continue
                        hw[(s, lane)] = v
                        sem = sems[s][lane] if lane is not None else sems[s]
                        eng.wait_ge(sem, v)
                    inst = fn()
                    if inc:
                        if stream == "pe":
                            eng.drain(fusable=False).then_inc(inc_sem, inc)
                        else:
                            inst.then_inc(inc_sem, inc)
                if body_name == "gpsimd":
                    for sem, v in totals:
                        eng.wait_ge(sem, v)
            return body

        block.gpsimd(mk_body("gpsimd"))
        block.sync(mk_body("sync"))
        block.vector(mk_body("vector"))
        block.scalar(mk_body("scalar"))
        block.tensor(mk_body("tensor"))
        return totals


def build_program(D, I, T, TP, debug_taps=False):
    H = D // CH
    HCH = D // 128
    NCH = T // CH
    NPCH = TP // CH
    NIB = I // 128
    NQF = D // FC
    TBS = T // FC
    NP6 = T // 256
    HPG = min(H, 8)
    NG = (H + HPG - 1) // HPG
    assert D % FC == 0 and T % 512 == 0 and TP == T and I % 128 == 0

    nc = bass.Bass("TRN2", target_bir_lowering=False)

    def inp(name, shape, dt=F32):
        return nc.dram_tensor(name, shape, dt, kind="ExternalInput").ap()

    xm = inp("xm", [T, D])
    xp = inp("xp", [TP, D])
    maskc = inp("maskc", [128, 1])
    wqkvT = inp("wqkvT", [D, 3 * D])
    bqkv = inp("bqkv", [1, 3 * D])
    woutT = inp("woutT", [D, D])
    bout = inp("bout", [1, D])
    gT = inp("gT", [D, I], I32)
    uT = inp("uT", [D, I], I32)
    dT = inp("dT", [I, D], I32)
    gs = inp("gs", [128, NIB])
    us = inp("us", [128, NIB])
    dsc = inp("dsc", [1, D])
    slope_sq = inp("slope_sq", [1, H])
    posp1t = inp("posp1t", [1, H * CH])
    poskr = inp("poskr", [1, 128])
    diffcT = inp("diffcT", [128, 128])
    causT = inp("causT", [128, 128])
    identf = inp("identf", [128, 128])

    out = nc.dram_tensor("out", [T, D], F32, kind="ExternalOutput").ap()

    sk = "ExternalOutput" if debug_taps else "Internal"

    def scratch(name, shape, dt):
        return nc.dram_tensor(name, shape, dt, kind=sk).ap()

    q_s = scratch("q_s", [T, D], BF16)
    k_s = scratch("k_s", [T, D], BF16)
    v_s = scratch("v_s", [T, D], BF16)
    xn_s = scratch("xn_s", [128, HCH * T], BF16)
    ap_s = scratch("ap_s", [T, D], F32)
    hsi_s = scratch("hsi_s", [I, T], BF16)
    wd_s = scratch("wd_s", [I, D], BF16)
    g_bf = scratch("g_bf", [D, I], BF16)
    u_bf = scratch("u_bf", [D, I], BF16)
    st_s = scratch("st_s", [128, 1024], F32)
    xr_s = scratch("xr_s", [128, 64 * 1024], BF16)
    wg_s = scratch("wg_s", [128, 8 * 1024], BF16)

    with ExitStack() as ctx:
        def sbt(name, shape, dtype):
            return ctx.enter_context(nc.sbuf_tensor(name, shape, dtype))

        BIGW = max(HCH * T, 2 * NIB * 256)
        big8 = sbt("big8", [128, BIGW], BF16)
        wstream = sbt("wstream", [128, HCH * FC], BF16)
        xtile = sbt("xtile", [128, D], F32)
        S = sbt("S", [128, D], F32)
        lamcr = sbt("lamcr", [128, D], F32)
        dqf = sbt("dqf", [128, D], BF16)
        dmaskT = sbt("dmaskT", [128, D], BF16)
        dkh = sbt("dkh", [128, H], F32)
        dscr = sbt("dscr", [128, D], F32)
        scT = sbt("scT", [128, D], BF16)
        attnsb = sbt("attnsb", [128, D], BF16)
        qp = sbt("qp", [128, D], BF16)
        Sbf = sbt("Sbf", [128, D], BF16)
        kc = sbt("kc", [128, D], BF16)
        vc = sbt("vc", [128, D], BF16)
        kpc = sbt("kpc", [128, D], BF16)
        kfm = sbt("kfm", [128, D], BF16)
        qch = sbt("qch", [128, D], BF16)
        rowpool = sbt("rowpool", [1, 4 * D], BF16)
        onesf = sbt("onesf", [1, 128], F32)
        onesb = sbt("onesb", [1, FC], BF16)
        slp = sbt("slp", [1, H], F32)
        l16 = sbt("l16", [1, H], F32)
        t1 = sbt("t1", [128, D], F32)
        t1b = sbt("t1b", [128, D], F32)
        ap8 = sbt("ap8", [128, D], F32)
        stg = sbt("stg", [128, 2048], BF16)
        sg = sbt("sg", [128, 2 * FC], BF16)
        uu = sbt("uu", [128, 2 * FC], BF16)
        st = sbt("st", [128, 10 * 2 * NCH], F32)
        epsc = sbt("epsc", [128, 1], F32)
        mcol = sbt("mcol", [128, 1], F32)
        gssb = sbt("gssb", [128, NIB], F32)
        ussb = sbt("ussb", [128, NIB], F32)
        identb = sbt("identb", [128, 128], BF16)

        PAt = ctx.enter_context(nc.psum_tensor("PA", [128, 1024], F32))
        PBt = ctx.enter_context(nc.psum_tensor("PB", [128, 1024], F32))
        PCt = ctx.enter_context(nc.psum_tensor("PC", [128, 1024], F32))
        PDt = ctx.enter_context(nc.psum_tensor("PD", [128, 1024], F32))
        PT = PDt[:].bitcast(BF16)

        sems = {}
        for s in ("pe", "dve", "act"):
            sems[s] = ctx.enter_context(nc.semaphore("sem_" + s))
        for s in ("syncq", "gq", "scq"):
            sems[s] = [ctx.enter_context(nc.semaphore(f"sem_{s}_{l}"))
                       for l in range(Planner.NLANES)]
        block = ctx.enter_context(nc.Block())
        P = Planner()

        B = {}
        for nm in ("P0B", "wq0", "wq1", "wq2", "wq3",
                   "xt0", "xt1", "st0", "st1", "qch", "kpc", "kfm", "qp",
                   "Sbf", "scT", "attnsb", "S", "t1", "ap8L", "ap8R", "stg", "stgA", "stgB",
                   "kc", "vc", "sg0", "sg1", "uu0", "uu1",
                   "PA0", "PA1", "PB0", "PB1", "PC0", "PC1", "PD0", "PD1", "t1bB",
                   "xn_sB", "ap_sB", "hsi_sB",
                   "wd_sB", "outB", "g_bfB", "u_bfB"):
            B[nm] = Buf(nm)
        NRG = NCH // 2
        QKV_RG = {nm: [Buf(f"{nm}_{g}") for g in range(NRG)]
                  for nm in ("q_sB", "k_sB", "v_sB")}
        BIG8T = [Buf(f"b8_{i}") for i in range(max(NCH, 16))]
        BIG8 = tuple(BIG8T)
        WS_ALL = (B["wq0"], B["wq1"], B["wq2"], B["wq3"])
        AP8 = (B["ap8L"], B["ap8R"])
        PA_ALL = (B["PA0"], B["PA1"])
        PB_ALL = (B["PB0"], B["PB1"])
        PC_ALL = (B["PC0"], B["PC1"])
        PD_ALL = (B["PD0"], B["PD1"])
        C = B["P0B"]

        big8v = big8[:, 0:HCH * T].rearrange("p (c t) -> p c t", t=T)
        wsv = wstream[:].rearrange("p (c f) -> p c f", f=FC)
        rowA = rowpool[:, 0:2 * D].bitcast(F32)
        rowB = rowpool[:, 2 * D:4 * D].bitcast(F32)
        bqkv_bf = rowpool[:, 0:3 * D]
        bout_bf = rowpool[:, 3 * D:4 * D]
        xt_ap = (xtile[:], ap8[:])
        xt_buf = ((B["xt0"],), AP8)
        kc2 = ap8[:].bitcast(BF16)[:, 0:D]
        vc2 = ap8[:].bitcast(BF16)[:, D:2 * D]

        PHASE = P.set_phase

        PHASE("P0")
        P.dma("syncq", lambda: nc.sync.dma_start(out=slp[:], in_=slope_sq),
              writes=[C])
        P.dma("syncq", lambda: nc.sync.dma_start(out=dscr[0:1, :], in_=dsc),
              writes=[C])
        P.dma("syncq", lambda: nc.sync.dma_start(out=xtile[0:128, 0:128],
                                                 in_=diffcT),
              writes=[C, B["xt0"]])
        P.dma("syncq", lambda: nc.sync.dma_start(out=ap8[0:128, 0:128],
                                                 in_=causT),
              writes=[C, *AP8])
        P.dma("syncq", lambda: nc.sync.dma_start(out=rowA[:, 0:128], in_=poskr),
              writes=[C])
        P.dma("syncq", lambda: nc.sync.dma_start(out=rowB[:, 0:D], in_=posp1t),
              writes=[C])
        P.dma("syncq", lambda: nc.sync.dma_start(out=gssb[:], in_=gs), writes=[C])
        P.dma("syncq", lambda: nc.sync.dma_start(out=ussb[:], in_=us), writes=[C])
        P.dma("syncq", lambda: nc.sync.dma_start(out=mcol[:], in_=maskc),
              writes=[C])
        P.dma("gq", lambda: nc.gpsimd.dma_start(out=identb[:], in_=identf),
              writes=[C])
        P.op("dve", lambda: nc.vector.memset(onesf[:], 1.0), reads=[C], writes=[C])
        P.op("dve", lambda: nc.vector.memset(onesb[:], 1.0), reads=[C], writes=[C])
        P.op("dve", lambda: nc.vector.memset(S[:], 0.0), writes=[B["S"]])
        P.op("dve", lambda: nc.vector.memset(epsc[:], float(EPS)), writes=[C])
        P.op("act", lambda: nc.scalar.activation(slp[:], slp[:], AF.Sqrt),
             reads=[C], writes=[C])
        P.op("pe", lambda: nc.tensor.matmul(PCt[:, 0:H], rowA[:, 0:128], slp[:],
                                            start=True, stop=True),
             reads=[C], writes=[B["PC0"], C])
        P.op("act", lambda: nc.scalar.activation(dkh[:], PCt[:, 0:H], AF.Exp,
                                                 scale=-1.0),
             reads=[C], writes=[C])
        P.op("act", lambda: nc.scalar.activation(l16[:], slp[:], AF.Exp,
                                                 scale=-float(CH)),
             reads=[C], writes=[C])
        P.op("dve", lambda: nc.vector.tensor_copy(
            rowA[:, 0:D].rearrange("o (h c) -> o h c", h=H),
            slp[:].unsqueeze(2).broadcast_to([1, H, CH])),
            reads=[C], writes=[C])
        for i in range(D // FC):
            P.op("pe", lambda i=i: nc.tensor.matmul(
                PAt[:, 0:FC], onesf[:], rowA[:, i * FC:(i + 1) * FC],
                start=True, stop=True), reads=[C], writes=[B["PA0"], C])
            P.op("dve", lambda i=i: nc.vector.tensor_copy(
                t1[:, i * FC:(i + 1) * FC], PAt[:, 0:FC]),
                reads=[C], writes=[B["t1"], C])
        P.op("dve", lambda: nc.vector.tensor_tensor(
            t1[:].rearrange("p (h c) -> p h c", h=H),
            xtile[0:128, 0:128].unsqueeze(1).broadcast_to([128, H, CH]),
            t1[:].rearrange("p (h c) -> p h c", h=H), ALU.mult),
            reads=[C, B["t1"], B["xt0"]], writes=[B["t1"], C])
        P.op("act", lambda: nc.scalar.activation(t1[:], t1[:], AF.Exp,
                                                 scale=-1.0),
             reads=[C, B["t1"]], writes=[B["t1"], C])
        P.op("dve", lambda: nc.vector.tensor_tensor(
            dmaskT[:].rearrange("p (h c) -> p h c", h=H),
            t1[:].rearrange("p (h c) -> p h c", h=H),
            ap8[0:128, 0:128].unsqueeze(1).broadcast_to([128, H, CH]),
            ALU.mult), reads=[C, B["t1"], *AP8], writes=[C])
        P.op("dve", lambda: nc.vector.tensor_tensor(
            rowB[:, 0:D], rowB[:, 0:D], rowA[:, 0:D], ALU.mult),
            reads=[C], writes=[C])
        P.op("act", lambda: nc.scalar.activation(rowB[:, 0:D], rowB[:, 0:D],
                                                 AF.Exp, scale=-1.0),
             reads=[C], writes=[C])
        for i in range(D // FC):
            P.op("pe", lambda i=i: nc.tensor.matmul(
                PAt[:, 0:FC], onesf[:], rowB[:, i * FC:(i + 1) * FC],
                start=True, stop=True), reads=[C], writes=[B["PA0"], C])
            P.op("dve", lambda i=i: nc.vector.tensor_copy(
                dqf[:, i * FC:(i + 1) * FC], PAt[:, 0:FC]),
                reads=[C], writes=[C])
        P.op("dve", lambda: nc.vector.tensor_copy(
            rowA[:, 0:D].rearrange("o (h c) -> o h c", h=H),
            l16[:].unsqueeze(2).broadcast_to([1, H, CH])),
            reads=[C], writes=[C])
        for i in range(D // FC):
            P.op("pe", lambda i=i: nc.tensor.matmul(
                PAt[:, 0:FC], onesf[:], rowA[:, i * FC:(i + 1) * FC],
                start=True, stop=True), reads=[C], writes=[B["PA0"], C])
            P.op("dve", lambda i=i: nc.vector.tensor_copy(
                lamcr[:, i * FC:(i + 1) * FC], PAt[:, 0:FC]),
                reads=[C], writes=[C])
        P.op("dve", lambda: nc.vector.tensor_copy(rowA[:, 0:D], dscr[0:1, :]),
             reads=[C], writes=[C])
        for i in range(D // FC):
            P.op("pe", lambda i=i: nc.tensor.matmul(
                PAt[:, 0:FC], onesf[:], rowA[:, i * FC:(i + 1) * FC],
                start=True, stop=True), reads=[C], writes=[B["PA0"], C])
            P.op("dve", lambda i=i: nc.vector.tensor_copy(
                dscr[:, i * FC:(i + 1) * FC], PAt[:, 0:FC]),
                reads=[C], writes=[C])
        P.dma("gq", lambda: nc.gpsimd.dma_start(out=bqkv_bf, in_=bqkv),
              reads=[C], writes=[C])
        P.dma("gq", lambda: nc.gpsimd.dma_start(out=bout_bf, in_=bout),
              reads=[C], writes=[C])

        def emit_ln_phase(src_dram, slot0):
            STB = [B["st0"], B["st1"]]
            nparts = (D + 1023) // 1024
            dump = (PCt, PCt)
            dumpb = (PC_ALL, PC_ALL)
            for tc in range(NCH):
                par = tc % 2
                xa = xt_ap[par]
                slot = slot0 + tc
                sa = st[:, slot * 10:slot * 10 + 10]
                P.dma("syncq", lambda tc=tc, xa=xa: nc.sync.dma_start(
                    out=xa, in_=src_dram[tc * 128:(tc + 1) * 128, :]),
                    writes=list(xt_buf[par]))
                P.op("dve", lambda xa=xa, sa=sa: nc.vector.reduce_sum(
                    sa[:, 0:1], xa, axis=AX.X),
                    reads=list(xt_buf[par]), writes=STB)
                for ip in range(nparts):
                    w = min(1024, D - ip * 1024)
                    P.op("act", lambda ip=ip, w=w, xa=xa, sa=sa:
                         nc.scalar.activation(
                             dump[ip][:, 0:w], xa[:, ip * 1024:ip * 1024 + w],
                             AF.Square, accum_out=sa[:, 2 + ip:3 + ip]),
                         reads=list(xt_buf[par]), writes=[*dumpb[ip], *STB])
            def sv(col):
                return st[:, slot0 * 10 + col: (slot0 + NCH) * 10: 10]
            if nparts == 2:
                P.op("dve", lambda: nc.vector.tensor_tensor(
                    sv(4), sv(2), sv(3), ALU.add), reads=STB, writes=STB)
            else:
                P.op("dve", lambda: nc.vector.tensor_copy(sv(4), sv(2)),
                     reads=STB, writes=STB)
            P.op("dve", lambda: nc.vector.tensor_scalar(
                sv(1), sv(0), 1.0 / D, None, ALU.mult), reads=STB, writes=STB)
            P.op("dve", lambda: nc.vector.tensor_tensor(
                sv(5), sv(1), sv(1), ALU.mult), reads=STB, writes=STB)
            P.op("dve", lambda: nc.vector.tensor_scalar(
                sv(6), sv(4), 1.0 / D, None, ALU.mult), reads=STB, writes=STB)
            P.op("dve", lambda: nc.vector.tensor_sub(
                sv(7), sv(6), sv(5)), reads=STB, writes=STB)
            P.op("act", lambda: nc.scalar.activation(
                sv(8), sv(7), AF.Sqrt, bias=epsc[:]),
                reads=[C, *STB], writes=STB)
            P.op("dve", lambda: nc.vector.reciprocal(sv(9), sv(8)),
                 reads=STB, writes=STB)
            for tc in range(NCH):
                par = tc % 2
                xa = xt_ap[par]
                slot = slot0 + tc
                sa = st[:, slot * 10:slot * 10 + 10]
                P.dma("syncq", lambda tc=tc, xa=xa: nc.sync.dma_start(
                    out=xa, in_=src_dram[tc * 128:(tc + 1) * 128, :]),
                    writes=list(xt_buf[par]))
                dst = (qch, kpc)[par]
                dstb = (B["qch"], B["kpc"])[par]
                P.op("dve", lambda xa=xa, sa=sa, dst=dst: nc.vector.tensor_scalar(
                    dst[:], xa, sa[:, 1:2], sa[:, 9:10], ALU.subtract, ALU.mult),
                    reads=[*xt_buf[par], *STB], writes=[dstb], sync_self=True)
                for half in range(2 if HCH > 8 else 1):
                    h0 = half * 8
                    h1 = min(HCH, h0 + 8)
                    for hch in range(h0, h1):
                        P.op("pe", lambda hch=hch, dst=dst: nc.tensor.transpose(
                            PT[:, hch * 128:(hch + 1) * 128],
                            dst[:, hch * 128:(hch + 1) * 128], identb[:]),
                            reads=[C, dstb], writes=[PD_ALL[half]])
                    P.op("dve", lambda tc=tc, h0=h0, h1=h1: nc.vector.tensor_copy(
                        big8v[:, h0:h1, tc * 128:(tc + 1) * 128],
                        PT[:, h0 * 128:h1 * 128].rearrange(
                            "p (c t) -> p c t", t=128)),
                        reads=[PD_ALL[half]], writes=[BIG8T[tc]])
                emit_p3b(1)

        def emit_wslab(fcol):
            hh = max(HCH // 2, 1)
            P.dma("gq", lambda fcol=fcol, hh=hh: nc.gpsimd.dma_start(
                out=wstream[:, 0:hh * FC].rearrange("p (c f) -> p c f", f=FC),
                in_=wqkvT.rearrange("(c p) f -> p c f", p=128)[
                    :, 0:hh, fcol * FC:(fcol + 1) * FC]),
                writes=[B["wq0"], B["wq1"]])
            if hh < HCH:
                P.dma("gq", lambda fcol=fcol, hh=hh: nc.gpsimd.dma_start(
                    out=wstream[:, hh * FC:HCH * FC].rearrange(
                        "p (c f) -> p c f", f=FC),
                    in_=wqkvT.rearrange("(c p) f -> p c f", p=128)[
                        :, hh:HCH, fcol * FC:(fcol + 1) * FC]),
                    writes=[B["wq2"], B["wq3"]])

        def ws_bufs(hch):
            return (B["wq0"], B["wq1"]) if hch < max(HCH // 2, 1) \
                else (B["wq2"], B["wq3"])

        def emit_kv_proj(fcols):
            for fcol in fcols:
                emit_wslab(fcol)
                emit_p3b(1)
                if fcol < NQF:
                    dst_s, dstb, foff = q_s, QKV_RG["q_sB"], fcol * FC
                elif fcol < 2 * NQF:
                    dst_s, dstb, foff = k_s, QKV_RG["k_sB"], (fcol - NQF) * FC
                else:
                    dst_s, dstb, foff = v_s, QKV_RG["v_sB"], (fcol - 2 * NQF) * FC
                for tcq in range(NCH // 4):
                    for tci in range(4):
                        tc = tcq * 4 + tci
                        half = tci // 2
                        hb = (B["stgA"], B["stgB"])[half]
                        pab = PA_ALL[tci % 2]
                        pav = PAt[:, (tci % 2) * FC:(tci % 2 + 1) * FC]
                        for hch in range(HCH):
                            P.op("pe", lambda tc=tc, hch=hch, pav=pav:
                                 nc.tensor.matmul(
                                     pav, big8v[:, hch, tc * 128:(tc + 1) * 128],
                                     wsv[:, hch, :],
                                     start=(hch == 0), stop=False),
                                 reads=[BIG8T[tc], *ws_bufs(hch)], writes=[pab])
                        P.op("pe", lambda fcol=fcol, pav=pav: nc.tensor.matmul(
                            pav, onesb[:, 0:128],
                            bqkv_bf[:, fcol * FC:(fcol + 1) * FC],
                            start=False, stop=True),
                            reads=[C], writes=[pab])
                        P.op("act", lambda tci=tci, pav=pav: nc.scalar.copy(
                            stg[:, tci * FC:(tci + 1) * FC], pav),
                            reads=[pab], writes=[hb])
                        if tci % 2 == 1:
                            P.dma("scq", lambda dst_s=dst_s, foff=foff,
                                  tcq=tcq, half=half:
                                  nc.scalar.dma_start(
                                      out=dst_s.rearrange(
                                          "(c p) f -> p c f", p=128)[
                                          :, tcq * 4 + 2 * half:
                                          tcq * 4 + 2 * half + 2,
                                          foff:foff + FC],
                                      in_=stg[:, half * 2 * FC:
                                              (half + 1) * 2 * FC].rearrange(
                                          "p (c f) -> p c f", f=FC)),
                                  reads=[hb], writes=[dstb[tcq * 2 + half]])

        def emit_chunk_loads(j, kdst, kdb, vdst, vdb):
            P.dma("syncq", lambda j=j, kdst=kdst: nc.sync.dma_start(
                out=kdst, in_=k_s[j * 128:(j + 1) * 128, :]),
                reads=[QKV_RG["k_sB"][j // 2]], writes=[kdb])
            P.dma("syncq", lambda j=j, vdst=vdst: nc.sync.dma_start(
                out=vdst, in_=v_s[j * 128:(j + 1) * 128, :]),
                reads=[QKV_RG["v_sB"][j // 2]], writes=[vdb])

        def emit_kdecay(ksrc, kb):
            P.op("dve", lambda ksrc=ksrc: nc.vector.tensor_tensor(
                kpc[:].rearrange("p (h c) -> p h c", h=H),
                ksrc.rearrange("p (h c) -> p h c", h=H),
                dkh[:].unsqueeze(2).broadcast_to([128, H, CH]), ALU.mult),
                reads=[C, kb], writes=[B["kpc"]])

        def emit_A_supdate(vsrc, vb):
            for g in range(NG):
                h0 = g * HPG
                nh = min(HPG, H - h0)
                for i in range(nh):
                    P.op("pe", lambda i=i, h0=h0, vsrc=vsrc: nc.tensor.matmul(
                        PBt[:, i * 128:(i + 1) * 128],
                        kpc[:, (h0 + i) * 128:(h0 + i + 1) * 128],
                        vsrc[:, (h0 + i) * 128:(h0 + i + 1) * 128],
                        start=True, stop=True),
                        reads=[B["kpc"], vb], writes=list(PB_ALL))
                sl = slice(h0 * 128, (h0 + nh) * 128)
                w = nh * 128
                P.op("dve", lambda sl=sl: nc.vector.tensor_tensor(
                    S[:, sl], S[:, sl], lamcr[:, sl], ALU.mult),
                    reads=[C, B["S"]], writes=[B["S"]])
                P.op("dve", lambda sl=sl, w=w: nc.vector.tensor_tensor(
                    S[:, sl], S[:, sl], PBt[:, 0:w], ALU.add),
                    reads=[B["S"], *PB_ALL], writes=[B["S"]])

        def emit_p3b_item(ib):
            PH = P.phase
            P.set_phase("P3b")
            P.dma("gq", lambda ib=ib: nc.gpsimd.dma_start(
                out=attnsb[:, 0:D], in_=dT[ib * 128:(ib + 1) * 128, :]),
                writes=[B["attnsb"]])
            P.op("dve", lambda: nc.vector.tensor_scalar(
                attnsb[:, 0:D], attnsb[:, 0:D], 128.0, None, ALU.subtract),
                reads=[B["attnsb"]], writes=[B["attnsb"]])
            P.op("dve", lambda: nc.vector.tensor_tensor(
                attnsb[:, 0:D], attnsb[:, 0:D], dscr[:, 0:D], ALU.mult),
                reads=[C, B["attnsb"]], writes=[B["attnsb"]])
            P.dma("scq", lambda ib=ib: nc.scalar.dma_start(
                out=wd_s[ib * 128:(ib + 1) * 128, :], in_=attnsb[:, 0:D]),
                reads=[B["attnsb"]], writes=[B["wd_sB"]])
            P.set_phase(PH)
        p3b_next = [0]
        NGU = 16
        gu_next = [0]

        def emit_gu_item(i):
            src, dst, bb = ((gT, g_bf, B["g_bfB"]) if i % 2 == 0
                            else (uT, u_bf, B["u_bfB"]))
            r = (i // 2) * (D // (NGU // 2))
            r1 = r + D // (NGU // 2)
            P.dma("gq", lambda src=src, dst=dst, r=r, r1=r1:
                  nc.gpsimd.dma_start(out=dst[r:r1, :], in_=src[r:r1, :]),
                  writes=[bb])

        def emit_p3b(n):
            while n > 0 and p3b_next[0] < NIB:
                emit_p3b_item(p3b_next[0])
                p3b_next[0] += 1
                n -= 1
            while n > 0 and gu_next[0] < NGU:
                emit_gu_item(gu_next[0])
                gu_next[0] += 1
                n -= 1

        def emit_gu(n):
            while n > 0 and gu_next[0] < NGU:
                emit_gu_item(gu_next[0])
                gu_next[0] += 1
                n -= 1

        PHASE("P1a")
        emit_ln_phase(xp, 0)
        PHASE("P1b")
        emit_kv_proj(range(NQF, 3 * NQF))
        PHASE("P2")
        emit_ln_phase(xm, NCH)
        P.dma("scq", lambda: nc.scalar.dma_start(out=xn_s, in_=big8[:, 0:HCH * T]),
              reads=list(BIG8), writes=[B["xn_sB"]])

        PHASE("P1c")
        for j in range(NPCH):
            par = j % 2
            kdst, kdb = (kc[:], B["kc"]) if par == 0 else (kc2, B["ap8L"])
            vdst, vdb = (vc[:], B["vc"]) if par == 0 else (vc2, B["ap8R"])
            emit_chunk_loads(j, kdst, kdb, vdst, vdb)
            emit_kdecay(kdst, kdb)
            emit_A_supdate(vdst, vdb)
        P.op("dve", lambda: nc.vector.tensor_scalar(
            S[:], S[:], mcol[:], None, ALU.mult),
            reads=[C, B["S"]], writes=[B["S"]])

        PHASE("P3")
        emit_kv_proj(range(3 * NQF))

        emit_p3b(NIB)
        emit_gu(NGU)

        PHASE("P4")
        P.dma("gq", lambda: nc.gpsimd.dma_start(
            out=big8[:, 0:HCH * D].rearrange("p (c f) -> p c f", f=D),
            in_=woutT.rearrange("(c p) f -> p c f", p=128)),
            writes=list(BIG8))
        wov = big8[:, 0:HCH * D].rearrange("p (c f) -> p c f", f=D)
        qfm = t1b[:].bitcast(BF16)[:, 0:D]
        for j in range(NCH):
            par = j % 2
            kdst, kdb = (kc[:], B["kc"]) if par == 0 else (kc2, B["ap8L"])
            vdst, vdb = (vc[:], B["vc"]) if par == 0 else (vc2, B["ap8R"])
            qdst, qdb = (qch[:], B["qch"]) if par == 0 else \
                        (stg[:, 0:D], B["stg"])
            emit_chunk_loads(j, kdst, kdb, vdst, vdb)
            P.dma("syncq", lambda j=j, qdst=qdst: nc.sync.dma_start(
                out=qdst, in_=q_s[j * 128:(j + 1) * 128, :]),
                reads=[QKV_RG["q_sB"][j // 2]], writes=[qdb])
            emit_kdecay(kdst, kdb)
            for half in range(2 if HCH > 8 else 1):
                h0 = half * 8
                h1 = min(HCH, h0 + 8)
                for hch in range(h0, h1):
                    P.op("pe", lambda hch=hch, kdst=kdst: nc.tensor.transpose(
                        PT[:, hch * 128:(hch + 1) * 128],
                        kdst[:, hch * 128:(hch + 1) * 128], identb[:]),
                        reads=[C, kdb], writes=[PD_ALL[half]])
                P.op("dve", lambda h0=h0, h1=h1: nc.vector.tensor_copy(
                    kfm[:, h0 * 128:h1 * 128], PT[:, h0 * 128:h1 * 128]),
                    reads=[PD_ALL[half]], writes=[B["kfm"]])
            for half in range(2 if HCH > 8 else 1):
                h0 = half * 8
                h1 = min(HCH, h0 + 8)
                for hch in range(h0, h1):
                    P.op("pe", lambda hch=hch, qdst=qdst: nc.tensor.transpose(
                        PT[:, hch * 128:(hch + 1) * 128],
                        qdst[:, hch * 128:(hch + 1) * 128], identb[:]),
                        reads=[C, qdb], writes=[PD_ALL[half]])
                P.op("dve", lambda h0=h0, h1=h1: nc.vector.tensor_copy(
                    qfm[:, h0 * 128:h1 * 128], PT[:, h0 * 128:h1 * 128]),
                    reads=[PD_ALL[half]], writes=[B["t1bB"]])
            P.op("dve", lambda: nc.vector.tensor_tensor(
                qp[:].rearrange("p (h c) -> p h c", h=H),
                qfm.rearrange("p (h c) -> p h c", h=H),
                dqf[:].rearrange("p (h c) -> p h c", h=H), ALU.mult),
                reads=[C, B["t1bB"]], writes=[B["qp"]])
            P.op("act", lambda: nc.scalar.copy(Sbf[:], S[:]),
                 reads=[B["S"]], writes=[B["Sbf"]])
            for g in range(NG):
                h0 = g * HPG
                nh = min(HPG, H - h0)
                gsl = slice(h0 * 128, (h0 + nh) * 128)
                gw = nh * 128
                for i in range(nh):
                    h = h0 + i
                    P.op("pe", lambda i=i, h=h: nc.tensor.matmul(
                        PAt[:, i * 128:(i + 1) * 128],
                        kfm[:, h * 128:(h + 1) * 128],
                        qfm[:, h * 128:(h + 1) * 128],
                        start=True, stop=True),
                        reads=[B["kfm"], B["t1bB"]], writes=list(PA_ALL))
                P.op("dve", lambda gsl=gsl, gw=gw: nc.vector.tensor_tensor(
                    scT[:, gsl], PAt[:, 0:gw], dmaskT[:, gsl], ALU.mult),
                    reads=[C, *PA_ALL], writes=[B["scT"]])
                for i in range(nh):
                    h = h0 + i
                    P.op("pe", lambda i=i, h=h: nc.tensor.matmul(
                        PBt[:, i * 128:(i + 1) * 128],
                        Sbf[:, h * 128:(h + 1) * 128],
                        qp[:, h * 128:(h + 1) * 128],
                        start=True, stop=False),
                        reads=[B["Sbf"], B["qp"]], writes=list(PB_ALL))
                    P.op("pe", lambda i=i, h=h, vdst=vdst: nc.tensor.matmul(
                        PBt[:, i * 128:(i + 1) * 128],
                        vdst[:, h * 128:(h + 1) * 128],
                        scT[:, h * 128:(h + 1) * 128],
                        start=False, stop=True),
                        reads=[vdb, B["scT"]], writes=list(PB_ALL))
                P.op("act", lambda gsl=gsl, gw=gw: nc.scalar.copy(
                    attnsb[:, gsl], PBt[:, 0:gw]),
                    reads=list(PB_ALL), writes=[B["attnsb"]])
                for i in range(nh):
                    h = h0 + i
                    P.op("pe", lambda i=i, h=h, vdst=vdst: nc.tensor.matmul(
                        PAt[:, i * 128:(i + 1) * 128],
                        kpc[:, h * 128:(h + 1) * 128],
                        vdst[:, h * 128:(h + 1) * 128],
                        start=True, stop=True),
                        reads=[B["kpc"], vdb], writes=list(PA_ALL))
                P.op("dve", lambda gsl=gsl: nc.vector.tensor_tensor(
                    S[:, gsl], S[:, gsl], lamcr[:, gsl], ALU.mult),
                    reads=[C, B["S"], B["Sbf"]], writes=[B["S"]])
                P.op("dve", lambda gsl=gsl, gw=gw: nc.vector.tensor_tensor(
                    S[:, gsl], S[:, gsl], PAt[:, 0:gw], ALU.add),
                    reads=[B["S"], *PA_ALL], writes=[B["S"]])
            for ph in range((D + 1023) // 1024):
                pw = min(1024, D - ph * 1024)
                pcb = PC_ALL[ph % 2]
                for fsub in range(pw // FC):
                    fo = ph * 1024 + fsub * FC
                    pcv = PCt[:, (ph % 2) * 0 + fsub * FC:fsub * FC + FC] \
                        if False else PCt[:, fsub * FC:(fsub + 1) * FC]
                    for dch in range(HCH):
                        P.op("pe", lambda fsub=fsub, fo=fo, dch=dch:
                             nc.tensor.matmul(
                                 PCt[:, fsub * FC:(fsub + 1) * FC],
                                 attnsb[:, dch * 128:(dch + 1) * 128],
                                 wov[:, dch, fo:fo + FC],
                                 start=(dch == 0), stop=False),
                             reads=[B["attnsb"], *BIG8], writes=list(PC_ALL))
                    P.op("pe", lambda fsub=fsub, fo=fo: nc.tensor.matmul(
                        PCt[:, fsub * FC:(fsub + 1) * FC],
                        onesb[:, 0:128], bout_bf[:, fo:fo + FC],
                        start=False, stop=True),
                        reads=[C], writes=list(PC_ALL))
                P.op("dve", lambda ph=ph, pw=pw: nc.vector.tensor_copy(
                    t1[:, ph * 1024:ph * 1024 + pw], PCt[:, 0:pw]),
                    reads=list(PC_ALL), writes=[B["t1"]])
            P.dma("scq", lambda j=j: nc.scalar.dma_start(
                out=ap_s[j * 128:(j + 1) * 128, :], in_=t1[:]),
                reads=[B["t1"]], writes=[B["ap_sB"]])

        PHASE("P5")
        P.dma("gq", lambda: nc.gpsimd.dma_start(out=big8[:, 0:HCH * T], in_=xn_s),
              reads=[B["xn_sB"]], writes=list(BIG8))
        for ib in range(NIB):
            par = ib % 2
            goff = par * 2 * HCH * 128
            uoff = goff + HCH * 128
            gb = B["wq0"] if par == 0 else B["wq2"]
            ub = B["wq1"] if par == 0 else B["wq3"]
            gq1, geng = (("syncq", nc.sync), ("scq", nc.scalar))[ib % 2]
            P.dma(gq1, lambda ib=ib, goff=goff, geng=geng: geng.dma_start(
                out=wstream[:, goff:goff + HCH * 128].rearrange(
                    "p (c f) -> p c f", f=128),
                in_=g_bf.rearrange("(c p) i -> p c i", p=128)[
                    :, :, ib * 128:(ib + 1) * 128]),
                reads=[B["g_bfB"]], writes=[gb])
            gq2, ueng = (("scq", nc.scalar), ("syncq", nc.sync))[ib % 2]
            P.dma(gq2, lambda ib=ib, uoff=uoff, ueng=ueng: ueng.dma_start(
                out=wstream[:, uoff:uoff + HCH * 128].rearrange(
                    "p (c f) -> p c f", f=128),
                in_=u_bf.rearrange("(c p) i -> p c i", p=128)[
                    :, :, ib * 128:(ib + 1) * 128]),
                reads=[B["u_bfB"]], writes=[ub])
            P.op("dve", lambda goff=goff: nc.vector.tensor_scalar(
                wstream[:, goff:goff + 2 * HCH * 128],
                wstream[:, goff:goff + 2 * HCH * 128],
                128.0, None, ALU.subtract),
                reads=[gb, ub], writes=[gb, ub])
            for tb in range(TBS):
                tpar = tb % 2
                pab = PA_ALL[tpar]
                pbb = PB_ALL[tpar]
                pav = PAt[:, tpar * FC:(tpar + 1) * FC]
                pbv = PBt[:, tpar * FC:(tpar + 1) * FC]
                for hch in range(HCH):
                    P.op("pe", lambda hch=hch, tb=tb, goff=goff, pav=pav:
                         nc.tensor.matmul(
                             pav, wstream[:, goff + hch * 128:
                                          goff + (hch + 1) * 128],
                             big8v[:, hch, tb * FC:(tb + 1) * FC],
                             start=(hch == 0), stop=(hch == HCH - 1)),
                         reads=[gb, *BIG8], writes=[pab])
                for hch in range(HCH):
                    P.op("pe", lambda hch=hch, tb=tb, uoff=uoff, pbv=pbv:
                         nc.tensor.matmul(
                             pbv, wstream[:, uoff + hch * 128:
                                          uoff + (hch + 1) * 128],
                             big8v[:, hch, tb * FC:(tb + 1) * FC],
                             start=(hch == 0), stop=(hch == HCH - 1)),
                         reads=[ub, *BIG8], writes=[pbb])
                sgb = (B["sg0"], B["sg1"])[tpar]
                uub = (B["uu0"], B["uu1"])[tpar]
                P.op("act", lambda ib=ib, tpar=tpar, pav=pav:
                     nc.scalar.activation(
                         sg[:, tpar * FC:(tpar + 1) * FC], pav, AF.Silu,
                         scale=gssb[:, ib:ib + 1]),
                     reads=[C, pab], writes=[sgb])
                P.op("act", lambda ib=ib, tpar=tpar, pbv=pbv:
                     nc.scalar.activation(
                         uu[:, tpar * FC:(tpar + 1) * FC], pbv, AF.Copy,
                         scale=ussb[:, ib:ib + 1]),
                     reads=[C, pbb], writes=[uub])
                P.op("dve", lambda tb=tb, tpar=tpar: nc.vector.tensor_tensor(
                    stg[:, tb * FC:(tb + 1) * FC],
                    sg[:, tpar * FC:(tpar + 1) * FC],
                    uu[:, tpar * FC:(tpar + 1) * FC],
                    ALU.mult), reads=[sgb, uub], writes=[(B["stg"], B["sg1"])[0]])
                P.dma("scq", lambda ib=ib, tb=tb: nc.scalar.dma_start(
                    out=hsi_s[ib * 128:(ib + 1) * 128, tb * FC:(tb + 1) * FC],
                    in_=stg[:, tb * FC:(tb + 1) * FC]),
                    reads=[B["stg"]], writes=[B["hsi_sB"]])

        PHASE("P6")
        PSUMS = [(PAt, PA_ALL), (PBt, PB_ALL), (PCt, PC_ALL), (PDt, PD_ALL)]

        def emit_slab(tp):
            spar = tp % 2
            soff = spar * NIB * 256
            slb = BIG8T[spar * 8:spar * 8 + 8]
            P.dma("syncq", lambda tp=tp, soff=soff: nc.sync.dma_start(
                out=big8[:, soff:soff + NIB * 256].rearrange(
                    "p (b t) -> p b t", t=256),
                in_=hsi_s.rearrange("(b p) t -> p b t", p=128)[
                    :, :, tp * 256:(tp + 1) * 256]),
                reads=[B["hsi_sB"]], writes=slb)

        emit_slab(0)
        for tp in range(NP6):
            PHASE("P6")
            spar = tp % 2
            soff = spar * NIB * 256
            slab = big8[:, soff:soff + NIB * 256].rearrange(
                "p (b t) -> p b t", t=256)
            slbufs = BIG8T[spar * 8:spar * 8 + 8]
            for ib in range(NIB):
                wpar = ib % 4
                wdst = (vc[:], kc[:], kpc[:], kfm[:])[wpar]
                wdb = (B["vc"], B["kc"], B["kpc"], B["kfm"])[wpar]
                wq = ("gq", "syncq")[ib % 2]
                dmafn = (nc.gpsimd, nc.sync)[ib % 2]
                P.dma(wq, lambda ib=ib, wdst=wdst, dmafn=dmafn: dmafn.dma_start(
                    out=wdst, in_=wd_s[ib * 128:(ib + 1) * 128, :]),
                    reads=[B["wd_sB"]], writes=[wdb])
                for tcs in range(2):
                    for fq in range(D // FC):
                        pi = tcs * 2 + fq // 2 if D > 1024 else tcs
                        pst, psb = PSUMS[pi]
                        po = (fq % 2) * FC if D > 1024 else fq * FC
                        P.op("pe", lambda ib=ib, tcs=tcs, fq=fq, pst=pst,
                             po=po, wdst=wdst, slab=slab: nc.tensor.matmul(
                                 pst[:, po:po + FC],
                                 slab[:, ib, tcs * 128:(tcs + 1) * 128],
                                 wdst[:, fq * FC:(fq + 1) * FC],
                                 start=(ib == 0), stop=(ib == NIB - 1)),
                             reads=slbufs + [wdb], writes=list(psb))
            if tp + 1 < NP6:
                emit_slab(tp + 1)
            PHASE("P6m")
            for tcs in range(2):
                j = tp * 2 + tcs
                ev = (t1, t1b)[tcs]
                evb = (B["t1"], B["t1bB"])[tcs]
                P.dma("syncq", lambda j=j: nc.sync.dma_start(
                    out=ap8[:], in_=ap_s[j * 128:(j + 1) * 128, :]),
                    reads=[B["ap_sB"]], writes=list(AP8))
                P.dma("syncq", lambda j=j: nc.sync.dma_start(
                    out=xtile[:], in_=xm[j * 128:(j + 1) * 128, :]),
                    writes=[B["xt0"]])
                nhal = (D + 1023) // 1024
                for ph in range(nhal):
                    pw = min(1024, D - ph * 1024)
                    pi = tcs * 2 + ph if D > 1024 else tcs
                    pst, psb = PSUMS[pi]
                    P.op("dve", lambda ph=ph, pw=pw, pst=pst, ev=ev:
                         nc.vector.tensor_copy(
                             ev[:, ph * 1024:ph * 1024 + pw], pst[:, 0:pw]),
                         reads=[*psb], writes=[evb])
                P.op("dve", lambda ev=ev: nc.vector.tensor_tensor(
                    ev[:], ev[:], ap8[:], ALU.add),
                    reads=[evb, *AP8], writes=[evb], sync_self=True)
                P.op("dve", lambda ev=ev: nc.vector.tensor_tensor(
                    ev[:], ev[:], xtile[:], ALU.add),
                    reads=[evb, B["xt0"]], writes=[evb])
                P.dma("scq", lambda j=j, ev=ev: nc.scalar.dma_start(
                    out=out[j * 128:(j + 1) * 128, :], in_=ev[:]),
                    reads=[evb], writes=[B["outB"]])


        P.finalize(nc, block, sems)
        rem = nc.sbuf_bytes_remaining
        print(f"[build] D={D} I={I} T={T}: {len(P.entries)} ops, "
              f"sbuf_remaining={rem}B/part")
    return nc


_CACHE = {}


def _get_program(D, I, T, TP, debug_taps=False):
    key = (D, I, T, TP, debug_taps)
    if key not in _CACHE:
        _CACHE[key] = build_program(D, I, T, TP, debug_taps)
    return _CACHE[key]


def make_const_inputs(D, H):
    pos = np.arange(CH, dtype=np.float32)
    return dict(
        slope_sq=np.ldexp(np.ones(H, np.float32), -(np.arange(H) + 1)).reshape(1, H),
        posp1t=np.tile(pos + 1.0, H).reshape(1, H * CH).astype(np.float32),
        poskr=(CH - 1.0 - pos).reshape(1, CH).astype(np.float32),
        diffcT=np.maximum(pos[None, :] - pos[:, None], 0.0).astype(np.float32),
        causT=(pos[None, :] >= pos[:, None]).astype(np.float32),
        identf=np.eye(128, dtype=np.float32),
    )


def make_core_inputs(x_full, b, h, weights, D, H, I, T, TP):
    d = dict(weights)
    d.update(make_const_inputs(D, H))
    d["xm"] = np.ascontiguousarray(x_full[b, h * T:(h + 1) * T, :])
    d["xp"] = np.ascontiguousarray(x_full[b, 0:TP, :])
    d["maskc"] = np.full((128, 1), 1.0 if h == 1 else 0.0, np.float32)
    return d


def prep_weights(w_qkv, b_qkv, w_out, b_out, gate_w, gate_scale, up_w, up_scale,
                 down_w, down_scale, D, I):
    NIB = I // 128
    return dict(
        wqkvT=np.ascontiguousarray(w_qkv.T.astype(np.float32, copy=False)),
        bqkv=np.ascontiguousarray(b_qkv.reshape(1, -1).astype(np.float32, copy=False)),
        woutT=np.ascontiguousarray(w_out.T.astype(np.float32, copy=False)),
        bout=np.ascontiguousarray(b_out.reshape(1, -1).astype(np.float32, copy=False)),
        gT=np.ascontiguousarray(np.asarray(gate_w).T.astype(np.int32, copy=False)),
        uT=np.ascontiguousarray(np.asarray(up_w).T.astype(np.int32, copy=False)),
        dT=np.ascontiguousarray(np.asarray(down_w).T.astype(np.int32, copy=False)),
        gs=np.ascontiguousarray(
            np.asarray(gate_scale, np.float32).reshape(NIB, 128).T),
        us=np.ascontiguousarray(
            np.asarray(up_scale, np.float32).reshape(NIB, 128).T),
        dsc=np.ascontiguousarray(
            np.asarray(down_scale, np.float32).reshape(1, -1)),
    )


def kernel(L0_attn_x, L0_attn_w_qkv, L0_attn_b_qkv, L0_attn_w_out, L0_attn_b_out,
           L0_ffn_gate_w, L0_ffn_gate_scale, L0_ffn_up_w, L0_ffn_up_scale,
           L0_ffn_down_w, L0_ffn_down_scale):
    x = np.asarray(L0_attn_x, dtype=np.float32)
    B, N, D = x.shape
    H = D // CH
    I = np.asarray(L0_ffn_gate_w).shape[0]
    T = N // 2
    TP = T
    weights = prep_weights(
        np.asarray(L0_attn_w_qkv, np.float32), np.asarray(L0_attn_b_qkv, np.float32),
        np.asarray(L0_attn_w_out, np.float32), np.asarray(L0_attn_b_out, np.float32),
        L0_ffn_gate_w, L0_ffn_gate_scale, L0_ffn_up_w, L0_ffn_up_scale,
        L0_ffn_down_w, L0_ffn_down_scale, D, I)

    nc = _get_program(D, I, T, TP)
    in_maps = []
    for c in range(8):
        b, h = c // 2, c % 2
        in_maps.append(make_core_inputs(x, b, h, weights, D, H, I, T, TP))
    res = run_bass_kernel_spmd(nc, in_maps, core_ids=list(range(8)))
    outp = np.empty((B, N, D), dtype=np.float32)
    for c in range(8):
        b, h = c // 2, c % 2
        outp[b, h * T:(h + 1) * T, :] = np.asarray(res.results[c]["out"])
    return outp


# version 15
# speedup vs baseline: 1.0203x; 1.0203x over previous
import os
import numpy as np
import ml_dtypes
from contextlib import ExitStack

import concourse.bass as bass
import concourse.mybir as mybir
from concourse.bass_utils import run_bass_kernel_spmd

F32 = mybir.dt.float32
BF16 = mybir.dt.bfloat16
I32 = mybir.dt.int32
AF = mybir.ActivationFunctionType
ALU = mybir.AluOpType
AX = mybir.AxisListType
BF = ml_dtypes.bfloat16

CH = 128
FC = 512
EPS = 1e-5

SERIAL_PHASES = set(
    p for p in os.environ.get("KSERIAL", "").split(",") if p)
SERIAL_ALL = os.environ.get("KSERIAL", "") == "all"


class Buf:
    __slots__ = ("name", "w", "r")

    def __init__(self, name):
        self.name = name
        self.w = {}
        self.r = {}


class Planner:
    DMA_STREAMS = ("syncq", "gq", "scq")
    BODY = {"syncq": "sync", "gq": "gpsimd", "scq": "scalar",
            "pe": "tensor", "dve": "vector", "act": "scalar"}

    def __init__(self):
        self.entries = []
        self.counts = {}
        self.phase = "P0"
        self.prev_any = None

    def set_phase(self, name):
        self.phase = name

    def op(self, stream, fn, reads=(), writes=(), sync_self=False):
        idx = self.counts.get(stream, 0)
        deps = {}

        def add(tok, raw=True):
            if tok is None:
                return
            s, i = tok
            if s == stream and i >= idx:
                return
            if s == stream and not sync_self:
                if not (raw and stream in ("dve", "act")):
                    return
            key = (s, i % self.NLANES) if s in self.DMA_STREAMS else s
            if deps.get(key, -1) < i:
                deps[key] = i
        serial = SERIAL_ALL or (self.phase in SERIAL_PHASES)
        if serial and self.prev_any is not None:
            add(self.prev_any)
        for b in reads:
            for (ws, _), i in b.w.items():
                add((ws, i), raw=True)
        for b in writes:
            for (ws, _), i in b.w.items():
                add((ws, i), raw=False)
            for (rs, _), i in b.r.items():
                add((rs, i), raw=False)
        tok = (stream, idx)
        key = ((stream, idx % self.NLANES)
               if stream in self.DMA_STREAMS else (stream, 0))
        for b in reads:
            if b.r.get(key, -1) < idx:
                b.r[key] = idx
        for b in writes:
            if b.w.get(key, -1) < idx:
                b.w[key] = idx
            b.r = {}
        self.entries.append((stream, fn, deps, self.phase))
        self.counts[stream] = idx + 1
        self.prev_any = tok
        return tok

    def dma(self, queue, fn, reads=(), writes=()):
        return self.op(queue, fn, reads, writes)

    NLANES = 16

    def finalize(self, nc, block, sems):
        NL = self.NLANES
        awaited = set()
        for stream, fn, deps, ph in self.entries:
            for key, i in deps.items():
                st = key[0] if isinstance(key, tuple) else key
                awaited.add((st, i))
        idx_ctr = {s: 0 for s in self.counts}
        incs = []
        cum = {s: 0 for s in self.counts}
        val_after = {}
        last_pos = {}
        for i, (stream, fn, deps, ph) in enumerate(self.entries):
            idx = idx_ctr[stream]
            idx_ctr[stream] += 1
            last_pos[stream] = (i, idx)
            if stream in self.DMA_STREAMS:
                incs.append(16)
            else:
                incs.append(1 if (stream, idx) in awaited else 0)
        for stream, (i, idx) in last_pos.items():
            if incs[i] == 0:
                incs[i] = 1
        idx_ctr = {s: 0 for s in self.counts}
        for i, (stream, fn, deps, ph) in enumerate(self.entries):
            idx = idx_ctr[stream]
            idx_ctr[stream] += 1
            if stream in self.DMA_STREAMS:
                val_after[(stream, idx)] = (idx % NL, 16 * (idx // NL + 1))
            else:
                cum[stream] += incs[i]
                val_after[(stream, idx)] = (None, cum[stream])

        bodies = {b: [] for b in ("sync", "gpsimd", "scalar", "tensor", "vector")}
        idx_ctr = {s: 0 for s in self.counts}
        for i, (stream, fn, deps, ph) in enumerate(self.entries):
            idx = idx_ctr[stream]
            idx_ctr[stream] += 1
            waits = {}
            for key, j in deps.items():
                st = key[0] if isinstance(key, tuple) else key
                lane, v = val_after[(st, j)]
                k2 = (st, lane)
                if waits.get(k2, -1) < v:
                    waits[k2] = v
            if stream in self.DMA_STREAMS:
                inc_sem = sems[stream][idx % NL]
                if idx >= NL:
                    k2 = (stream, idx % NL)
                    if waits.get(k2, -1) < 16 * (idx // NL):
                        waits[k2] = 16 * (idx // NL)
            else:
                inc_sem = sems[stream]
            bodies[self.BODY[stream]].append(
                (stream, fn, list(waits.items()), incs[i], inc_sem))

        totals = []
        for stream, cnt in self.counts.items():
            if stream in self.DMA_STREAMS:
                for lane in range(min(self.NLANES, cnt)):
                    n_lane = (cnt - lane - 1) // NL + 1
                    totals.append((sems[stream][lane], 16 * n_lane))
            else:
                totals.append((sems[stream], cum[stream]))

        def mk_body(body_name):
            def body(eng):
                hw = {}
                for stream, fn, waits, inc, inc_sem in bodies[body_name]:
                    for (s, lane), v in waits:
                        if hw.get((s, lane), -1) >= v:
                            continue
                        hw[(s, lane)] = v
                        sem = sems[s][lane] if lane is not None else sems[s]
                        eng.wait_ge(sem, v)
                    inst = fn()
                    if inc:
                        if stream == "pe":
                            eng.drain(fusable=False).then_inc(inc_sem, inc)
                        else:
                            inst.then_inc(inc_sem, inc)
                if body_name == "gpsimd":
                    for sem, v in totals:
                        eng.wait_ge(sem, v)
            return body

        block.gpsimd(mk_body("gpsimd"))
        block.sync(mk_body("sync"))
        block.vector(mk_body("vector"))
        block.scalar(mk_body("scalar"))
        block.tensor(mk_body("tensor"))
        return totals


def build_program(D, I, T, TP, debug_taps=False):
    H = D // CH
    HCH = D // 128
    NCH = T // CH
    NPCH = TP // CH
    NIB = I // 128
    NQF = D // FC
    TBS = T // FC
    NP6 = T // 256
    HPG = min(H, 8)
    NG = (H + HPG - 1) // HPG
    assert D % FC == 0 and T % 512 == 0 and TP == T and I % 128 == 0

    nc = bass.Bass("TRN2", target_bir_lowering=False)

    def inp(name, shape, dt=F32):
        return nc.dram_tensor(name, shape, dt, kind="ExternalInput").ap()

    xm = inp("xm", [T, D])
    xp = inp("xp", [TP, D])
    maskc = inp("maskc", [128, 1])
    wqkvT = inp("wqkvT", [D, 3 * D])
    bqkv = inp("bqkv", [1, 3 * D])
    woutT = inp("woutT", [D, D])
    bout = inp("bout", [1, D])
    gT = inp("gT", [D, I], I32)
    uT = inp("uT", [D, I], I32)
    dT = inp("dT", [I, D], I32)
    gs = inp("gs", [128, NIB])
    us = inp("us", [128, NIB])
    dsc = inp("dsc", [1, D])
    slope_sq = inp("slope_sq", [1, H])
    posp1t = inp("posp1t", [1, H * CH])
    poskr = inp("poskr", [1, 128])
    diffcT = inp("diffcT", [128, 128])
    causT = inp("causT", [128, 128])
    identf = inp("identf", [128, 128])

    out = nc.dram_tensor("out", [T, D], F32, kind="ExternalOutput").ap()

    sk = "ExternalOutput" if debug_taps else "Internal"

    def scratch(name, shape, dt):
        return nc.dram_tensor(name, shape, dt, kind=sk).ap()

    q_s = scratch("q_s", [T, D], BF16)
    k_s = scratch("k_s", [T, D], BF16)
    v_s = scratch("v_s", [T, D], BF16)
    xn_s = scratch("xn_s", [128, HCH * T], BF16)
    ap_s = scratch("ap_s", [T, D], F32)
    hsi_s = scratch("hsi_s", [I, T], BF16)
    wd_s = scratch("wd_s", [I, D], BF16)
    g_bf = scratch("g_bf", [D, I], BF16)
    u_bf = scratch("u_bf", [D, I], BF16)
    st_s = scratch("st_s", [128, 1024], F32)
    xr_s = scratch("xr_s", [128, 64 * 1024], BF16)
    wg_s = scratch("wg_s", [128, 8 * 1024], BF16)

    with ExitStack() as ctx:
        def sbt(name, shape, dtype):
            return ctx.enter_context(nc.sbuf_tensor(name, shape, dtype))

        BIGW = max(HCH * T, 2 * NIB * 256, HCH * D)
        big8 = sbt("big8", [128, BIGW], BF16)
        wstream = sbt("wstream", [128, HCH * FC], BF16)
        xtile = sbt("xtile", [128, D], F32)
        S = sbt("S", [128, D], F32)
        lamcr = sbt("lamcr", [128, D], F32)
        dqf = sbt("dqf", [128, D], BF16)
        dmaskT = sbt("dmaskT", [128, D], BF16)
        dkh = sbt("dkh", [128, H], F32)
        dscr = sbt("dscr", [128, D], F32)
        scT = sbt("scT", [128, D], BF16)
        attnsb = sbt("attnsb", [128, D], BF16)
        qp = sbt("qp", [128, D], BF16)
        Sbf = sbt("Sbf", [128, D], BF16)
        kc = sbt("kc", [128, D], BF16)
        vc = sbt("vc", [128, D], BF16)
        kpc = sbt("kpc", [128, D], BF16)
        kfm = sbt("kfm", [128, D], BF16)
        qch = sbt("qch", [128, D], BF16)
        rowpool = sbt("rowpool", [1, 4 * D], BF16)
        onesf = sbt("onesf", [1, 128], F32)
        onesb = sbt("onesb", [1, FC], BF16)
        slp = sbt("slp", [1, H], F32)
        l16 = sbt("l16", [1, H], F32)
        t1 = sbt("t1", [128, D], F32)
        t1b = sbt("t1b", [128, D], F32)
        ap8 = sbt("ap8", [128, D], F32)
        stg = sbt("stg", [128, 2048], BF16)
        sg = sbt("sg", [128, 2 * FC], BF16)
        uu = sbt("uu", [128, 2 * FC], BF16)
        st = sbt("st", [128, 10 * 2 * NCH], F32)
        epsc = sbt("epsc", [128, 1], F32)
        mcol = sbt("mcol", [128, 1], F32)
        gssb = sbt("gssb", [128, NIB], F32)
        ussb = sbt("ussb", [128, NIB], F32)
        identb = sbt("identb", [128, 128], BF16)

        PAt = ctx.enter_context(nc.psum_tensor("PA", [128, 1024], F32))
        PBt = ctx.enter_context(nc.psum_tensor("PB", [128, 1024], F32))
        PCt = ctx.enter_context(nc.psum_tensor("PC", [128, 1024], F32))
        PDt = ctx.enter_context(nc.psum_tensor("PD", [128, 1024], F32))
        PT = PDt[:].bitcast(BF16)

        sems = {}
        for s in ("pe", "dve", "act"):
            sems[s] = ctx.enter_context(nc.semaphore("sem_" + s))
        for s in ("syncq", "gq", "scq"):
            sems[s] = [ctx.enter_context(nc.semaphore(f"sem_{s}_{l}"))
                       for l in range(Planner.NLANES)]
        block = ctx.enter_context(nc.Block())
        P = Planner()

        B = {}
        for nm in ("P0B", "wq0", "wq1", "wq2", "wq3",
                   "xt0", "xt1", "st0", "st1", "qch", "kpc", "kfm", "qp",
                   "Sbf", "scT", "attnsb", "S", "t1", "ap8L", "ap8R", "stg", "stgA", "stgB",
                   "kc", "vc", "sg0", "sg1", "uu0", "uu1",
                   "PA0", "PA1", "PB0", "PB1", "PC0", "PC1", "PD0", "PD1", "t1bB",
                   "xn_sB", "ap_sB", "hsi_sB",
                   "wd_sB", "outB", "g_bfB", "u_bfB"):
            B[nm] = Buf(nm)
        NRG = NCH // 2
        QKV_RG = {nm: [Buf(f"{nm}_{g}") for g in range(NRG)]
                  for nm in ("q_sB", "k_sB", "v_sB")}
        BIG8T = [Buf(f"b8_{i}") for i in range(max(NCH, 16))]
        BIG8 = tuple(BIG8T)
        WS_ALL = (B["wq0"], B["wq1"], B["wq2"], B["wq3"])
        AP8 = (B["ap8L"], B["ap8R"])
        PA_ALL = (B["PA0"], B["PA1"])
        PB_ALL = (B["PB0"], B["PB1"])
        PC_ALL = (B["PC0"], B["PC1"])
        PD_ALL = (B["PD0"], B["PD1"])
        C = B["P0B"]

        big8v = big8[:, 0:HCH * T].rearrange("p (c t) -> p c t", t=T)
        wsv = wstream[:].rearrange("p (c f) -> p c f", f=FC)
        rowA = rowpool[:, 0:2 * D].bitcast(F32)
        rowB = rowpool[:, 2 * D:4 * D].bitcast(F32)
        bqkv_bf = rowpool[:, 0:3 * D]
        bout_bf = rowpool[:, 3 * D:4 * D]
        xt_ap = (xtile[:], ap8[:])
        xt_buf = ((B["xt0"],), AP8)
        kc2 = ap8[:].bitcast(BF16)[:, 0:D]
        vc2 = ap8[:].bitcast(BF16)[:, D:2 * D]

        PHASE = P.set_phase

        PHASE("P0")
        P.dma("syncq", lambda: nc.sync.dma_start(out=slp[:], in_=slope_sq),
              writes=[C])
        P.dma("syncq", lambda: nc.sync.dma_start(out=dscr[0:1, :], in_=dsc),
              writes=[C])
        P.dma("syncq", lambda: nc.sync.dma_start(out=xtile[0:128, 0:128],
                                                 in_=diffcT),
              writes=[C, B["xt0"]])
        P.dma("syncq", lambda: nc.sync.dma_start(out=ap8[0:128, 0:128],
                                                 in_=causT),
              writes=[C, *AP8])
        P.dma("syncq", lambda: nc.sync.dma_start(out=rowA[:, 0:128], in_=poskr),
              writes=[C])
        P.dma("syncq", lambda: nc.sync.dma_start(out=rowB[:, 0:D], in_=posp1t),
              writes=[C])
        P.dma("syncq", lambda: nc.sync.dma_start(out=gssb[:], in_=gs), writes=[C])
        P.dma("syncq", lambda: nc.sync.dma_start(out=ussb[:], in_=us), writes=[C])
        P.dma("syncq", lambda: nc.sync.dma_start(out=mcol[:], in_=maskc),
              writes=[C])
        P.dma("gq", lambda: nc.gpsimd.dma_start(out=identb[:], in_=identf),
              writes=[C])
        P.op("dve", lambda: nc.vector.memset(onesf[:], 1.0), reads=[C], writes=[C])
        P.op("dve", lambda: nc.vector.memset(onesb[:], 1.0), reads=[C], writes=[C])
        P.op("dve", lambda: nc.vector.memset(S[:], 0.0), writes=[B["S"]])
        P.op("dve", lambda: nc.vector.memset(epsc[:], float(EPS)), writes=[C])
        P.op("act", lambda: nc.scalar.activation(slp[:], slp[:], AF.Sqrt),
             reads=[C], writes=[C])
        P.op("pe", lambda: nc.tensor.matmul(PCt[:, 0:H], rowA[:, 0:128], slp[:],
                                            start=True, stop=True),
             reads=[C], writes=[B["PC0"], C])
        P.op("act", lambda: nc.scalar.activation(dkh[:], PCt[:, 0:H], AF.Exp,
                                                 scale=-1.0),
             reads=[C], writes=[C])
        P.op("act", lambda: nc.scalar.activation(l16[:], slp[:], AF.Exp,
                                                 scale=-float(CH)),
             reads=[C], writes=[C])
        P.op("dve", lambda: nc.vector.tensor_copy(
            rowA[:, 0:D].rearrange("o (h c) -> o h c", h=H),
            slp[:].unsqueeze(2).broadcast_to([1, H, CH])),
            reads=[C], writes=[C])
        for i in range(D // FC):
            P.op("pe", lambda i=i: nc.tensor.matmul(
                PAt[:, 0:FC], onesf[:], rowA[:, i * FC:(i + 1) * FC],
                start=True, stop=True), reads=[C], writes=[B["PA0"], C])
            P.op("dve", lambda i=i: nc.vector.tensor_copy(
                t1[:, i * FC:(i + 1) * FC], PAt[:, 0:FC]),
                reads=[C], writes=[B["t1"], C])
        P.op("dve", lambda: nc.vector.tensor_tensor(
            t1[:].rearrange("p (h c) -> p h c", h=H),
            xtile[0:128, 0:128].unsqueeze(1).broadcast_to([128, H, CH]),
            t1[:].rearrange("p (h c) -> p h c", h=H), ALU.mult),
            reads=[C, B["t1"], B["xt0"]], writes=[B["t1"], C])
        P.op("act", lambda: nc.scalar.activation(t1[:], t1[:], AF.Exp,
                                                 scale=-1.0),
             reads=[C, B["t1"]], writes=[B["t1"], C])
        P.op("dve", lambda: nc.vector.tensor_tensor(
            dmaskT[:].rearrange("p (h c) -> p h c", h=H),
            t1[:].rearrange("p (h c) -> p h c", h=H),
            ap8[0:128, 0:128].unsqueeze(1).broadcast_to([128, H, CH]),
            ALU.mult), reads=[C, B["t1"], *AP8], writes=[C])
        P.op("dve", lambda: nc.vector.tensor_tensor(
            rowB[:, 0:D], rowB[:, 0:D], rowA[:, 0:D], ALU.mult),
            reads=[C], writes=[C])
        P.op("act", lambda: nc.scalar.activation(rowB[:, 0:D], rowB[:, 0:D],
                                                 AF.Exp, scale=-1.0),
             reads=[C], writes=[C])
        for i in range(D // FC):
            P.op("pe", lambda i=i: nc.tensor.matmul(
                PAt[:, 0:FC], onesf[:], rowB[:, i * FC:(i + 1) * FC],
                start=True, stop=True), reads=[C], writes=[B["PA0"], C])
            P.op("dve", lambda i=i: nc.vector.tensor_copy(
                dqf[:, i * FC:(i + 1) * FC], PAt[:, 0:FC]),
                reads=[C], writes=[C])
        P.op("dve", lambda: nc.vector.tensor_copy(
            rowA[:, 0:D].rearrange("o (h c) -> o h c", h=H),
            l16[:].unsqueeze(2).broadcast_to([1, H, CH])),
            reads=[C], writes=[C])
        for i in range(D // FC):
            P.op("pe", lambda i=i: nc.tensor.matmul(
                PAt[:, 0:FC], onesf[:], rowA[:, i * FC:(i + 1) * FC],
                start=True, stop=True), reads=[C], writes=[B["PA0"], C])
            P.op("dve", lambda i=i: nc.vector.tensor_copy(
                lamcr[:, i * FC:(i + 1) * FC], PAt[:, 0:FC]),
                reads=[C], writes=[C])
        P.op("dve", lambda: nc.vector.tensor_copy(rowA[:, 0:D], dscr[0:1, :]),
             reads=[C], writes=[C])
        for i in range(D // FC):
            P.op("pe", lambda i=i: nc.tensor.matmul(
                PAt[:, 0:FC], onesf[:], rowA[:, i * FC:(i + 1) * FC],
                start=True, stop=True), reads=[C], writes=[B["PA0"], C])
            P.op("dve", lambda i=i: nc.vector.tensor_copy(
                dscr[:, i * FC:(i + 1) * FC], PAt[:, 0:FC]),
                reads=[C], writes=[C])
        P.dma("gq", lambda: nc.gpsimd.dma_start(out=bqkv_bf, in_=bqkv),
              reads=[C], writes=[C])
        P.dma("gq", lambda: nc.gpsimd.dma_start(out=bout_bf, in_=bout),
              reads=[C], writes=[C])

        def emit_ln_phase(src_dram, slot0):
            STB = [B["st0"], B["st1"]]
            nparts = (D + 1023) // 1024
            dump = (PCt, PCt)
            dumpb = (PC_ALL, PC_ALL)
            for tc in range(NCH):
                par = tc % 2
                xa = xt_ap[par]
                slot = slot0 + tc
                sa = st[:, slot * 10:slot * 10 + 10]
                P.dma("syncq", lambda tc=tc, xa=xa: nc.sync.dma_start(
                    out=xa, in_=src_dram[tc * 128:(tc + 1) * 128, :]),
                    writes=list(xt_buf[par]))
                P.op("dve", lambda xa=xa, sa=sa: nc.vector.reduce_sum(
                    sa[:, 0:1], xa, axis=AX.X),
                    reads=list(xt_buf[par]), writes=STB)
                for ip in range(nparts):
                    w = min(1024, D - ip * 1024)
                    P.op("act", lambda ip=ip, w=w, xa=xa, sa=sa:
                         nc.scalar.activation(
                             dump[ip][:, 0:w], xa[:, ip * 1024:ip * 1024 + w],
                             AF.Square, accum_out=sa[:, 2 + ip:3 + ip]),
                         reads=list(xt_buf[par]), writes=[*dumpb[ip], *STB],
                         sync_self=(ip > 0))
            def sv(col):
                return st[:, slot0 * 10 + col: (slot0 + NCH) * 10: 10]
            if nparts == 2:
                P.op("dve", lambda: nc.vector.tensor_tensor(
                    sv(4), sv(2), sv(3), ALU.add), reads=STB, writes=STB)
            else:
                P.op("dve", lambda: nc.vector.tensor_copy(sv(4), sv(2)),
                     reads=STB, writes=STB)
            P.op("dve", lambda: nc.vector.tensor_scalar(
                sv(1), sv(0), 1.0 / D, None, ALU.mult), reads=STB, writes=STB)
            P.op("dve", lambda: nc.vector.tensor_tensor(
                sv(5), sv(1), sv(1), ALU.mult), reads=STB, writes=STB)
            P.op("dve", lambda: nc.vector.tensor_scalar(
                sv(6), sv(4), 1.0 / D, None, ALU.mult), reads=STB, writes=STB)
            P.op("dve", lambda: nc.vector.tensor_sub(
                sv(7), sv(6), sv(5)), reads=STB, writes=STB)
            P.op("act", lambda: nc.scalar.activation(
                sv(8), sv(7), AF.Sqrt, bias=epsc[:]),
                reads=[C, *STB], writes=STB)
            P.op("dve", lambda: nc.vector.reciprocal(sv(9), sv(8)),
                 reads=STB, writes=STB)
            for tc in range(NCH):
                par = tc % 2
                xa = xt_ap[par]
                slot = slot0 + tc
                sa = st[:, slot * 10:slot * 10 + 10]
                P.dma("syncq", lambda tc=tc, xa=xa: nc.sync.dma_start(
                    out=xa, in_=src_dram[tc * 128:(tc + 1) * 128, :]),
                    writes=list(xt_buf[par]))
                dst = (qch, kpc)[par]
                dstb = (B["qch"], B["kpc"])[par]
                P.op("dve", lambda xa=xa, sa=sa, dst=dst: nc.vector.tensor_scalar(
                    dst[:], xa, sa[:, 1:2], sa[:, 9:10], ALU.subtract, ALU.mult),
                    reads=[*xt_buf[par], *STB], writes=[dstb], sync_self=True)
                for half in range(2 if HCH > 8 else 1):
                    h0 = half * 8
                    h1 = min(HCH, h0 + 8)
                    for hch in range(h0, h1):
                        P.op("pe", lambda hch=hch, dst=dst: nc.tensor.transpose(
                            PT[:, hch * 128:(hch + 1) * 128],
                            dst[:, hch * 128:(hch + 1) * 128], identb[:]),
                            reads=[C, dstb], writes=[PD_ALL[half]])
                    P.op("dve", lambda tc=tc, h0=h0, h1=h1: nc.vector.tensor_copy(
                        big8v[:, h0:h1, tc * 128:(tc + 1) * 128],
                        PT[:, h0 * 128:h1 * 128].rearrange(
                            "p (c t) -> p c t", t=128)),
                        reads=[PD_ALL[half]], writes=[BIG8T[tc]])
                emit_p3b(1)

        def emit_wslab(fcol):
            hh = max(HCH // 2, 1)
            P.dma("gq", lambda fcol=fcol, hh=hh: nc.gpsimd.dma_start(
                out=wstream[:, 0:hh * FC].rearrange("p (c f) -> p c f", f=FC),
                in_=wqkvT.rearrange("(c p) f -> p c f", p=128)[
                    :, 0:hh, fcol * FC:(fcol + 1) * FC]),
                writes=[B["wq0"], B["wq1"]])
            if hh < HCH:
                P.dma("gq", lambda fcol=fcol, hh=hh: nc.gpsimd.dma_start(
                    out=wstream[:, hh * FC:HCH * FC].rearrange(
                        "p (c f) -> p c f", f=FC),
                    in_=wqkvT.rearrange("(c p) f -> p c f", p=128)[
                        :, hh:HCH, fcol * FC:(fcol + 1) * FC]),
                    writes=[B["wq2"], B["wq3"]])

        def ws_bufs(hch):
            return (B["wq0"], B["wq1"]) if hch < max(HCH // 2, 1) \
                else (B["wq2"], B["wq3"])

        def emit_kv_proj(fcols):
            for fcol in fcols:
                emit_wslab(fcol)
                emit_p3b(1)
                if fcol < NQF:
                    dst_s, dstb, foff = q_s, QKV_RG["q_sB"], fcol * FC
                elif fcol < 2 * NQF:
                    dst_s, dstb, foff = k_s, QKV_RG["k_sB"], (fcol - NQF) * FC
                else:
                    dst_s, dstb, foff = v_s, QKV_RG["v_sB"], (fcol - 2 * NQF) * FC
                for tcq in range(NCH // 4):
                    for tci in range(4):
                        tc = tcq * 4 + tci
                        half = tci // 2
                        hb = (B["stgA"], B["stgB"])[half]
                        pab = PA_ALL[tci % 2]
                        pav = PAt[:, (tci % 2) * FC:(tci % 2 + 1) * FC]
                        for hch in range(HCH):
                            P.op("pe", lambda tc=tc, hch=hch, pav=pav:
                                 nc.tensor.matmul(
                                     pav, big8v[:, hch, tc * 128:(tc + 1) * 128],
                                     wsv[:, hch, :],
                                     start=(hch == 0), stop=False),
                                 reads=[BIG8T[tc], *ws_bufs(hch)], writes=[pab])
                        P.op("pe", lambda fcol=fcol, pav=pav: nc.tensor.matmul(
                            pav, onesb[:, 0:128],
                            bqkv_bf[:, fcol * FC:(fcol + 1) * FC],
                            start=False, stop=True),
                            reads=[C], writes=[pab])
                        P.op("act", lambda tci=tci, pav=pav: nc.scalar.copy(
                            stg[:, tci * FC:(tci + 1) * FC], pav),
                            reads=[pab], writes=[hb])
                        if tci % 2 == 1:
                            P.dma("scq", lambda dst_s=dst_s, foff=foff,
                                  tcq=tcq, half=half:
                                  nc.scalar.dma_start(
                                      out=dst_s.rearrange(
                                          "(c p) f -> p c f", p=128)[
                                          :, tcq * 4 + 2 * half:
                                          tcq * 4 + 2 * half + 2,
                                          foff:foff + FC],
                                      in_=stg[:, half * 2 * FC:
                                              (half + 1) * 2 * FC].rearrange(
                                          "p (c f) -> p c f", f=FC)),
                                  reads=[hb], writes=[dstb[tcq * 2 + half]])

        def emit_chunk_loads(j, kdst, kdb, vdst, vdb):
            P.dma("syncq", lambda j=j, kdst=kdst: nc.sync.dma_start(
                out=kdst, in_=k_s[j * 128:(j + 1) * 128, :]),
                reads=[QKV_RG["k_sB"][j // 2]], writes=[kdb])
            P.dma("syncq", lambda j=j, vdst=vdst: nc.sync.dma_start(
                out=vdst, in_=v_s[j * 128:(j + 1) * 128, :]),
                reads=[QKV_RG["v_sB"][j // 2]], writes=[vdb])

        def emit_kdecay(ksrc, kb):
            P.op("dve", lambda ksrc=ksrc: nc.vector.tensor_tensor(
                kpc[:].rearrange("p (h c) -> p h c", h=H),
                ksrc.rearrange("p (h c) -> p h c", h=H),
                dkh[:].unsqueeze(2).broadcast_to([128, H, CH]), ALU.mult),
                reads=[C, kb], writes=[B["kpc"]])

        def emit_A_supdate(vsrc, vb):
            for g in range(NG):
                h0 = g * HPG
                nh = min(HPG, H - h0)
                for i in range(nh):
                    P.op("pe", lambda i=i, h0=h0, vsrc=vsrc: nc.tensor.matmul(
                        PBt[:, i * 128:(i + 1) * 128],
                        kpc[:, (h0 + i) * 128:(h0 + i + 1) * 128],
                        vsrc[:, (h0 + i) * 128:(h0 + i + 1) * 128],
                        start=True, stop=True),
                        reads=[B["kpc"], vb], writes=list(PB_ALL))
                sl = slice(h0 * 128, (h0 + nh) * 128)
                w = nh * 128
                P.op("dve", lambda sl=sl: nc.vector.tensor_tensor(
                    S[:, sl], S[:, sl], lamcr[:, sl], ALU.mult),
                    reads=[C, B["S"]], writes=[B["S"]])
                P.op("dve", lambda sl=sl, w=w: nc.vector.tensor_tensor(
                    S[:, sl], S[:, sl], PBt[:, 0:w], ALU.add),
                    reads=[B["S"], *PB_ALL], writes=[B["S"]])

        def emit_p3b_item(ib):
            PH = P.phase
            P.set_phase("P3b")
            P.dma("gq", lambda ib=ib: nc.gpsimd.dma_start(
                out=attnsb[:, 0:D], in_=dT[ib * 128:(ib + 1) * 128, :]),
                writes=[B["attnsb"]])
            P.op("dve", lambda: nc.vector.tensor_scalar(
                attnsb[:, 0:D], attnsb[:, 0:D], 128.0, None, ALU.subtract),
                reads=[B["attnsb"]], writes=[B["attnsb"]])
            P.op("dve", lambda: nc.vector.tensor_tensor(
                attnsb[:, 0:D], attnsb[:, 0:D], dscr[:, 0:D], ALU.mult),
                reads=[C, B["attnsb"]], writes=[B["attnsb"]])
            P.dma("scq", lambda ib=ib: nc.scalar.dma_start(
                out=wd_s[ib * 128:(ib + 1) * 128, :], in_=attnsb[:, 0:D]),
                reads=[B["attnsb"]], writes=[B["wd_sB"]])
            P.set_phase(PH)
        p3b_next = [0]
        NGU = 16
        gu_next = [0]

        def emit_gu_item(i):
            src, dst, bb = ((gT, g_bf, B["g_bfB"]) if i % 2 == 0
                            else (uT, u_bf, B["u_bfB"]))
            r = (i // 2) * (D // (NGU // 2))
            r1 = r + D // (NGU // 2)
            P.dma("gq", lambda src=src, dst=dst, r=r, r1=r1:
                  nc.gpsimd.dma_start(out=dst[r:r1, :], in_=src[r:r1, :]),
                  writes=[bb])

        def emit_p3b(n):
            while n > 0 and p3b_next[0] < NIB:
                emit_p3b_item(p3b_next[0])
                p3b_next[0] += 1
                n -= 1
            while n > 0 and gu_next[0] < NGU:
                emit_gu_item(gu_next[0])
                gu_next[0] += 1
                n -= 1

        def emit_gu(n):
            while n > 0 and gu_next[0] < NGU:
                emit_gu_item(gu_next[0])
                gu_next[0] += 1
                n -= 1

        PHASE("P1a")
        emit_ln_phase(xp, 0)
        PHASE("P1b")
        emit_kv_proj(range(NQF, 3 * NQF))
        PHASE("P2")
        emit_ln_phase(xm, NCH)
        P.dma("scq", lambda: nc.scalar.dma_start(out=xn_s, in_=big8[:, 0:HCH * T]),
              reads=list(BIG8), writes=[B["xn_sB"]])

        PHASE("P1c")
        for j in range(NPCH):
            par = j % 2
            kdst, kdb = (kc[:], B["kc"]) if par == 0 else (kc2, B["ap8L"])
            vdst, vdb = (vc[:], B["vc"]) if par == 0 else (vc2, B["ap8R"])
            emit_chunk_loads(j, kdst, kdb, vdst, vdb)
            emit_kdecay(kdst, kdb)
            emit_A_supdate(vdst, vdb)
        P.op("dve", lambda: nc.vector.tensor_scalar(
            S[:], S[:], mcol[:], None, ALU.mult),
            reads=[C, B["S"]], writes=[B["S"]])

        PHASE("P3")
        emit_kv_proj(range(3 * NQF))

        emit_p3b(NIB)
        emit_gu(NGU)

        PHASE("P4")
        P.dma("gq", lambda: nc.gpsimd.dma_start(
            out=big8[:, 0:HCH * D].rearrange("p (c f) -> p c f", f=D),
            in_=woutT.rearrange("(c p) f -> p c f", p=128)),
            writes=list(BIG8))
        wov = big8[:, 0:HCH * D].rearrange("p (c f) -> p c f", f=D)
        qfm = t1b[:].bitcast(BF16)[:, 0:D]
        for j in range(NCH):
            par = j % 2
            kdst, kdb = (kc[:], B["kc"]) if par == 0 else (kc2, B["ap8L"])
            vdst, vdb = (vc[:], B["vc"]) if par == 0 else (vc2, B["ap8R"])
            qdst = qch[:] if par == 0 else stg[:, 0:D]
            qdbs = [B["qch"]] if par == 0 else [B["stgA"], B["stgB"]]
            emit_chunk_loads(j, kdst, kdb, vdst, vdb)
            P.dma("syncq", lambda j=j, qdst=qdst: nc.sync.dma_start(
                out=qdst, in_=q_s[j * 128:(j + 1) * 128, :]),
                reads=[QKV_RG["q_sB"][j // 2]], writes=qdbs)
            emit_kdecay(kdst, kdb)
            for half in range(2 if HCH > 8 else 1):
                h0 = half * 8
                h1 = min(HCH, h0 + 8)
                for hch in range(h0, h1):
                    P.op("pe", lambda hch=hch, kdst=kdst: nc.tensor.transpose(
                        PT[:, hch * 128:(hch + 1) * 128],
                        kdst[:, hch * 128:(hch + 1) * 128], identb[:]),
                        reads=[C, kdb], writes=[PD_ALL[half]])
                P.op("dve", lambda h0=h0, h1=h1: nc.vector.tensor_copy(
                    kfm[:, h0 * 128:h1 * 128], PT[:, h0 * 128:h1 * 128]),
                    reads=[PD_ALL[half]], writes=[B["kfm"]])
            for half in range(2 if HCH > 8 else 1):
                h0 = half * 8
                h1 = min(HCH, h0 + 8)
                for hch in range(h0, h1):
                    P.op("pe", lambda hch=hch, qdst=qdst: nc.tensor.transpose(
                        PT[:, hch * 128:(hch + 1) * 128],
                        qdst[:, hch * 128:(hch + 1) * 128], identb[:]),
                        reads=[C, *qdbs], writes=[PD_ALL[half]])
                P.op("dve", lambda h0=h0, h1=h1: nc.vector.tensor_copy(
                    qfm[:, h0 * 128:h1 * 128], PT[:, h0 * 128:h1 * 128]),
                    reads=[PD_ALL[half]], writes=[B["t1bB"]])
            P.op("dve", lambda: nc.vector.tensor_tensor(
                qp[:].rearrange("p (h c) -> p h c", h=H),
                qfm.rearrange("p (h c) -> p h c", h=H),
                dqf[:].rearrange("p (h c) -> p h c", h=H), ALU.mult),
                reads=[C, B["t1bB"]], writes=[B["qp"]])
            P.op("act", lambda: nc.scalar.copy(Sbf[:], S[:]),
                 reads=[B["S"]], writes=[B["Sbf"]])
            for g in range(NG):
                h0 = g * HPG
                nh = min(HPG, H - h0)
                gsl = slice(h0 * 128, (h0 + nh) * 128)
                gw = nh * 128
                for i in range(nh):
                    h = h0 + i
                    P.op("pe", lambda i=i, h=h: nc.tensor.matmul(
                        PAt[:, i * 128:(i + 1) * 128],
                        kfm[:, h * 128:(h + 1) * 128],
                        qfm[:, h * 128:(h + 1) * 128],
                        start=True, stop=True),
                        reads=[B["kfm"], B["t1bB"]], writes=list(PA_ALL))
                P.op("dve", lambda gsl=gsl, gw=gw: nc.vector.tensor_tensor(
                    scT[:, gsl], PAt[:, 0:gw], dmaskT[:, gsl], ALU.mult),
                    reads=[C, *PA_ALL], writes=[B["scT"]])
                for i in range(nh):
                    h = h0 + i
                    P.op("pe", lambda i=i, h=h: nc.tensor.matmul(
                        PBt[:, i * 128:(i + 1) * 128],
                        Sbf[:, h * 128:(h + 1) * 128],
                        qp[:, h * 128:(h + 1) * 128],
                        start=True, stop=False),
                        reads=[B["Sbf"], B["qp"]], writes=list(PB_ALL))
                    P.op("pe", lambda i=i, h=h, vdst=vdst: nc.tensor.matmul(
                        PBt[:, i * 128:(i + 1) * 128],
                        vdst[:, h * 128:(h + 1) * 128],
                        scT[:, h * 128:(h + 1) * 128],
                        start=False, stop=True),
                        reads=[vdb, B["scT"]], writes=list(PB_ALL))
                P.op("act", lambda gsl=gsl, gw=gw: nc.scalar.copy(
                    attnsb[:, gsl], PBt[:, 0:gw]),
                    reads=list(PB_ALL), writes=[B["attnsb"]])
                for i in range(nh):
                    h = h0 + i
                    P.op("pe", lambda i=i, h=h, vdst=vdst: nc.tensor.matmul(
                        PAt[:, i * 128:(i + 1) * 128],
                        kpc[:, h * 128:(h + 1) * 128],
                        vdst[:, h * 128:(h + 1) * 128],
                        start=True, stop=True),
                        reads=[B["kpc"], vdb], writes=list(PA_ALL))
                P.op("dve", lambda gsl=gsl: nc.vector.tensor_tensor(
                    S[:, gsl], S[:, gsl], lamcr[:, gsl], ALU.mult),
                    reads=[C, B["S"], B["Sbf"]], writes=[B["S"]])
                P.op("dve", lambda gsl=gsl, gw=gw: nc.vector.tensor_tensor(
                    S[:, gsl], S[:, gsl], PAt[:, 0:gw], ALU.add),
                    reads=[B["S"], *PA_ALL], writes=[B["S"]])
            for ph in range((D + 1023) // 1024):
                pw = min(1024, D - ph * 1024)
                pcb = PC_ALL[ph % 2]
                for fsub in range(pw // FC):
                    fo = ph * 1024 + fsub * FC
                    pcv = PCt[:, (ph % 2) * 0 + fsub * FC:fsub * FC + FC] \
                        if False else PCt[:, fsub * FC:(fsub + 1) * FC]
                    for dch in range(HCH):
                        P.op("pe", lambda fsub=fsub, fo=fo, dch=dch:
                             nc.tensor.matmul(
                                 PCt[:, fsub * FC:(fsub + 1) * FC],
                                 attnsb[:, dch * 128:(dch + 1) * 128],
                                 wov[:, dch, fo:fo + FC],
                                 start=(dch == 0), stop=False),
                             reads=[B["attnsb"], *BIG8], writes=list(PC_ALL))
                    P.op("pe", lambda fsub=fsub, fo=fo: nc.tensor.matmul(
                        PCt[:, fsub * FC:(fsub + 1) * FC],
                        onesb[:, 0:128], bout_bf[:, fo:fo + FC],
                        start=False, stop=True),
                        reads=[C], writes=list(PC_ALL))
                P.op("dve", lambda ph=ph, pw=pw: nc.vector.tensor_copy(
                    t1[:, ph * 1024:ph * 1024 + pw], PCt[:, 0:pw]),
                    reads=list(PC_ALL), writes=[B["t1"]])
            P.dma("scq", lambda j=j: nc.scalar.dma_start(
                out=ap_s[j * 128:(j + 1) * 128, :], in_=t1[:]),
                reads=[B["t1"]], writes=[B["ap_sB"]])

        PHASE("P5")
        P.dma("gq", lambda: nc.gpsimd.dma_start(out=big8[:, 0:HCH * T], in_=xn_s),
              reads=[B["xn_sB"]], writes=list(BIG8))
        for ib in range(NIB):
            par = ib % 2
            goff = par * 2 * HCH * 128
            uoff = goff + HCH * 128
            gb = B["wq0"] if par == 0 else B["wq2"]
            ub = B["wq1"] if par == 0 else B["wq3"]
            gq1, geng = (("syncq", nc.sync), ("scq", nc.scalar))[ib % 2]
            P.dma(gq1, lambda ib=ib, goff=goff, geng=geng: geng.dma_start(
                out=wstream[:, goff:goff + HCH * 128].rearrange(
                    "p (c f) -> p c f", f=128),
                in_=g_bf.rearrange("(c p) i -> p c i", p=128)[
                    :, :, ib * 128:(ib + 1) * 128]),
                reads=[B["g_bfB"]], writes=[gb])
            gq2, ueng = (("scq", nc.scalar), ("syncq", nc.sync))[ib % 2]
            P.dma(gq2, lambda ib=ib, uoff=uoff, ueng=ueng: ueng.dma_start(
                out=wstream[:, uoff:uoff + HCH * 128].rearrange(
                    "p (c f) -> p c f", f=128),
                in_=u_bf.rearrange("(c p) i -> p c i", p=128)[
                    :, :, ib * 128:(ib + 1) * 128]),
                reads=[B["u_bfB"]], writes=[ub])
            P.op("dve", lambda goff=goff: nc.vector.tensor_scalar(
                wstream[:, goff:goff + 2 * HCH * 128],
                wstream[:, goff:goff + 2 * HCH * 128],
                128.0, None, ALU.subtract),
                reads=[gb, ub], writes=[gb, ub])
            for tb in range(TBS):
                tpar = tb % 2
                pab = PA_ALL[tpar]
                pbb = PB_ALL[tpar]
                pav = PAt[:, tpar * FC:(tpar + 1) * FC]
                pbv = PBt[:, tpar * FC:(tpar + 1) * FC]
                for hch in range(HCH):
                    P.op("pe", lambda hch=hch, tb=tb, goff=goff, pav=pav:
                         nc.tensor.matmul(
                             pav, wstream[:, goff + hch * 128:
                                          goff + (hch + 1) * 128],
                             big8v[:, hch, tb * FC:(tb + 1) * FC],
                             start=(hch == 0), stop=(hch == HCH - 1)),
                         reads=[gb, *BIG8], writes=[pab])
                for hch in range(HCH):
                    P.op("pe", lambda hch=hch, tb=tb, uoff=uoff, pbv=pbv:
                         nc.tensor.matmul(
                             pbv, wstream[:, uoff + hch * 128:
                                          uoff + (hch + 1) * 128],
                             big8v[:, hch, tb * FC:(tb + 1) * FC],
                             start=(hch == 0), stop=(hch == HCH - 1)),
                         reads=[ub, *BIG8], writes=[pbb])
                sgb = (B["sg0"], B["sg1"])[tpar]
                uub = (B["uu0"], B["uu1"])[tpar]
                P.op("act", lambda ib=ib, tpar=tpar, pav=pav:
                     nc.scalar.activation(
                         sg[:, tpar * FC:(tpar + 1) * FC], pav, AF.Silu,
                         scale=gssb[:, ib:ib + 1]),
                     reads=[C, pab], writes=[sgb])
                P.op("act", lambda ib=ib, tpar=tpar, pbv=pbv:
                     nc.scalar.activation(
                         uu[:, tpar * FC:(tpar + 1) * FC], pbv, AF.Copy,
                         scale=ussb[:, ib:ib + 1]),
                     reads=[C, pbb], writes=[uub])
                shb = (B["stgA"], B["stgB"])[tb // 2]
                P.op("dve", lambda tb=tb, tpar=tpar: nc.vector.tensor_tensor(
                    stg[:, tb * FC:(tb + 1) * FC],
                    sg[:, tpar * FC:(tpar + 1) * FC],
                    uu[:, tpar * FC:(tpar + 1) * FC],
                    ALU.mult), reads=[sgb, uub], writes=[shb])
                P.dma("scq", lambda ib=ib, tb=tb: nc.scalar.dma_start(
                    out=hsi_s[ib * 128:(ib + 1) * 128, tb * FC:(tb + 1) * FC],
                    in_=stg[:, tb * FC:(tb + 1) * FC]),
                    reads=[shb], writes=[B["hsi_sB"]])

        PHASE("P6")
        PSUMS = [(PAt, PA_ALL), (PBt, PB_ALL), (PCt, PC_ALL), (PDt, PD_ALL)]

        def emit_slab(tp):
            spar = tp % 2
            soff = spar * NIB * 256
            slb = BIG8T[spar * 8:spar * 8 + 8]
            P.dma("syncq", lambda tp=tp, soff=soff: nc.sync.dma_start(
                out=big8[:, soff:soff + NIB * 256].rearrange(
                    "p (b t) -> p b t", t=256),
                in_=hsi_s.rearrange("(b p) t -> p b t", p=128)[
                    :, :, tp * 256:(tp + 1) * 256]),
                reads=[B["hsi_sB"]], writes=slb)

        emit_slab(0)
        for tp in range(NP6):
            PHASE("P6")
            spar = tp % 2
            soff = spar * NIB * 256
            slab = big8[:, soff:soff + NIB * 256].rearrange(
                "p (b t) -> p b t", t=256)
            slbufs = BIG8T[spar * 8:spar * 8 + 8]
            for ib in range(NIB):
                wpar = ib % 4
                wdst = (vc[:], kc[:], kpc[:], kfm[:])[wpar]
                wdb = (B["vc"], B["kc"], B["kpc"], B["kfm"])[wpar]
                wq = ("gq", "syncq")[ib % 2]
                dmafn = (nc.gpsimd, nc.sync)[ib % 2]
                P.dma(wq, lambda ib=ib, wdst=wdst, dmafn=dmafn: dmafn.dma_start(
                    out=wdst, in_=wd_s[ib * 128:(ib + 1) * 128, :]),
                    reads=[B["wd_sB"]], writes=[wdb])
                for tcs in range(2):
                    for fq in range(D // FC):
                        pi = tcs * 2 + fq // 2 if D > 1024 else tcs
                        pst, psb = PSUMS[pi]
                        po = (fq % 2) * FC if D > 1024 else fq * FC
                        P.op("pe", lambda ib=ib, tcs=tcs, fq=fq, pst=pst,
                             po=po, wdst=wdst, slab=slab: nc.tensor.matmul(
                                 pst[:, po:po + FC],
                                 slab[:, ib, tcs * 128:(tcs + 1) * 128],
                                 wdst[:, fq * FC:(fq + 1) * FC],
                                 start=(ib == 0), stop=(ib == NIB - 1)),
                             reads=slbufs + [wdb], writes=list(psb))
            if tp + 1 < NP6:
                emit_slab(tp + 1)
            PHASE("P6m")
            for tcs in range(2):
                j = tp * 2 + tcs
                ev = (t1, t1b)[tcs]
                evb = (B["t1"], B["t1bB"])[tcs]
                P.dma("syncq", lambda j=j: nc.sync.dma_start(
                    out=ap8[:], in_=ap_s[j * 128:(j + 1) * 128, :]),
                    reads=[B["ap_sB"]], writes=list(AP8))
                P.dma("syncq", lambda j=j: nc.sync.dma_start(
                    out=xtile[:], in_=xm[j * 128:(j + 1) * 128, :]),
                    writes=[B["xt0"]])
                nhal = (D + 1023) // 1024
                for ph in range(nhal):
                    pw = min(1024, D - ph * 1024)
                    pi = tcs * 2 + ph if D > 1024 else tcs
                    pst, psb = PSUMS[pi]
                    P.op("dve", lambda ph=ph, pw=pw, pst=pst, ev=ev:
                         nc.vector.tensor_copy(
                             ev[:, ph * 1024:ph * 1024 + pw], pst[:, 0:pw]),
                         reads=[*psb], writes=[evb])
                P.op("dve", lambda ev=ev: nc.vector.tensor_tensor(
                    ev[:], ev[:], ap8[:], ALU.add),
                    reads=[evb, *AP8], writes=[evb], sync_self=True)
                P.op("dve", lambda ev=ev: nc.vector.tensor_tensor(
                    ev[:], ev[:], xtile[:], ALU.add),
                    reads=[evb, B["xt0"]], writes=[evb])
                P.dma("scq", lambda j=j, ev=ev: nc.scalar.dma_start(
                    out=out[j * 128:(j + 1) * 128, :], in_=ev[:]),
                    reads=[evb], writes=[B["outB"]])


        P.finalize(nc, block, sems)
        rem = nc.sbuf_bytes_remaining
        print(f"[build] D={D} I={I} T={T}: {len(P.entries)} ops, "
              f"sbuf_remaining={rem}B/part")
    return nc


_CACHE = {}


def _get_program(D, I, T, TP, debug_taps=False):
    key = (D, I, T, TP, debug_taps)
    if key not in _CACHE:
        _CACHE[key] = build_program(D, I, T, TP, debug_taps)
    return _CACHE[key]


def make_const_inputs(D, H):
    pos = np.arange(CH, dtype=np.float32)
    return dict(
        slope_sq=np.ldexp(np.ones(H, np.float32), -(np.arange(H) + 1)).reshape(1, H),
        posp1t=np.tile(pos + 1.0, H).reshape(1, H * CH).astype(np.float32),
        poskr=(CH - 1.0 - pos).reshape(1, CH).astype(np.float32),
        diffcT=np.maximum(pos[None, :] - pos[:, None], 0.0).astype(np.float32),
        causT=(pos[None, :] >= pos[:, None]).astype(np.float32),
        identf=np.eye(128, dtype=np.float32),
    )


def make_core_inputs(x_full, b, h, weights, D, H, I, T, TP):
    d = dict(weights)
    d.update(make_const_inputs(D, H))
    d["xm"] = np.ascontiguousarray(x_full[b, h * T:(h + 1) * T, :])
    d["xp"] = np.ascontiguousarray(x_full[b, 0:TP, :])
    d["maskc"] = np.full((128, 1), 1.0 if h == 1 else 0.0, np.float32)
    return d


def prep_weights(w_qkv, b_qkv, w_out, b_out, gate_w, gate_scale, up_w, up_scale,
                 down_w, down_scale, D, I):
    NIB = I // 128
    return dict(
        wqkvT=np.ascontiguousarray(w_qkv.T.astype(np.float32, copy=False)),
        bqkv=np.ascontiguousarray(b_qkv.reshape(1, -1).astype(np.float32, copy=False)),
        woutT=np.ascontiguousarray(w_out.T.astype(np.float32, copy=False)),
        bout=np.ascontiguousarray(b_out.reshape(1, -1).astype(np.float32, copy=False)),
        gT=np.ascontiguousarray(np.asarray(gate_w).T.astype(np.int32, copy=False)),
        uT=np.ascontiguousarray(np.asarray(up_w).T.astype(np.int32, copy=False)),
        dT=np.ascontiguousarray(np.asarray(down_w).T.astype(np.int32, copy=False)),
        gs=np.ascontiguousarray(
            np.asarray(gate_scale, np.float32).reshape(NIB, 128).T),
        us=np.ascontiguousarray(
            np.asarray(up_scale, np.float32).reshape(NIB, 128).T),
        dsc=np.ascontiguousarray(
            np.asarray(down_scale, np.float32).reshape(1, -1)),
    )


def kernel(L0_attn_x, L0_attn_w_qkv, L0_attn_b_qkv, L0_attn_w_out, L0_attn_b_out,
           L0_ffn_gate_w, L0_ffn_gate_scale, L0_ffn_up_w, L0_ffn_up_scale,
           L0_ffn_down_w, L0_ffn_down_scale):
    x = np.asarray(L0_attn_x, dtype=np.float32)
    B, N, D = x.shape
    H = D // CH
    I = np.asarray(L0_ffn_gate_w).shape[0]
    T = N // 2
    TP = T
    weights = prep_weights(
        np.asarray(L0_attn_w_qkv, np.float32), np.asarray(L0_attn_b_qkv, np.float32),
        np.asarray(L0_attn_w_out, np.float32), np.asarray(L0_attn_b_out, np.float32),
        L0_ffn_gate_w, L0_ffn_gate_scale, L0_ffn_up_w, L0_ffn_up_scale,
        L0_ffn_down_w, L0_ffn_down_scale, D, I)

    nc = _get_program(D, I, T, TP)
    in_maps = []
    for c in range(8):
        b, h = c // 2, c % 2
        in_maps.append(make_core_inputs(x, b, h, weights, D, H, I, T, TP))
    res = run_bass_kernel_spmd(nc, in_maps, core_ids=list(range(8)))
    outp = np.empty((B, N, D), dtype=np.float32)
    for c in range(8):
        b, h = c // 2, c % 2
        outp[b, h * T:(h + 1) * T, :] = np.asarray(res.results[c]["out"])
    return outp


# version 16
# speedup vs baseline: 1.0981x; 1.0762x over previous
import os
import numpy as np
import ml_dtypes
from contextlib import ExitStack

import concourse.bass as bass
import concourse.mybir as mybir
from concourse.bass_utils import run_bass_kernel_spmd

F32 = mybir.dt.float32
BF16 = mybir.dt.bfloat16
I32 = mybir.dt.int32
AF = mybir.ActivationFunctionType
ALU = mybir.AluOpType
AX = mybir.AxisListType
BF = ml_dtypes.bfloat16

CH = 128
FC = 512
EPS = 1e-5

SERIAL_PHASES = set(
    p for p in os.environ.get("KSERIAL", "").split(",") if p)
SERIAL_ALL = os.environ.get("KSERIAL", "") == "all"


class Buf:
    __slots__ = ("name", "w", "r")

    def __init__(self, name):
        self.name = name
        self.w = {}
        self.r = {}


class Planner:
    DMA_STREAMS = ("syncq", "gq", "scq")
    BODY = {"syncq": "sync", "gq": "gpsimd", "scq": "scalar",
            "pe": "tensor", "dve": "vector", "act": "scalar"}

    def __init__(self):
        self.entries = []
        self.counts = {}
        self.phase = "P0"
        self.prev_any = None

    def set_phase(self, name):
        self.phase = name

    def op(self, stream, fn, reads=(), writes=(), sync_self=False):
        idx = self.counts.get(stream, 0)
        deps = {}

        def add(tok, raw=True):
            if tok is None:
                return
            s, i = tok
            if s == stream and i >= idx:
                return
            if s == stream and not sync_self:
                if not (raw and stream in ("dve", "act")):
                    return
            key = (s, i % self.NLANES) if s in self.DMA_STREAMS else s
            if deps.get(key, -1) < i:
                deps[key] = i
        serial = SERIAL_ALL or (self.phase in SERIAL_PHASES)
        if serial and self.prev_any is not None:
            add(self.prev_any)
        for b in reads:
            for (ws, _), i in b.w.items():
                add((ws, i), raw=True)
        for b in writes:
            for (ws, _), i in b.w.items():
                add((ws, i), raw=False)
            for (rs, _), i in b.r.items():
                add((rs, i), raw=False)
        tok = (stream, idx)
        key = ((stream, idx % self.NLANES)
               if stream in self.DMA_STREAMS else (stream, 0))
        for b in reads:
            if b.r.get(key, -1) < idx:
                b.r[key] = idx
        for b in writes:
            if b.w.get(key, -1) < idx:
                b.w[key] = idx
            b.r = {}
        self.entries.append((stream, fn, deps, self.phase))
        self.counts[stream] = idx + 1
        self.prev_any = tok
        return tok

    def dma(self, queue, fn, reads=(), writes=()):
        return self.op(queue, fn, reads, writes)

    NLANES = 16

    def finalize(self, nc, block, sems):
        NL = self.NLANES
        awaited = set()
        for stream, fn, deps, ph in self.entries:
            for key, i in deps.items():
                st = key[0] if isinstance(key, tuple) else key
                awaited.add((st, i))
        idx_ctr = {s: 0 for s in self.counts}
        incs = []
        cum = {s: 0 for s in self.counts}
        val_after = {}
        last_pos = {}
        for i, (stream, fn, deps, ph) in enumerate(self.entries):
            idx = idx_ctr[stream]
            idx_ctr[stream] += 1
            last_pos[stream] = (i, idx)
            if stream in self.DMA_STREAMS:
                incs.append(16)
            else:
                incs.append(1 if (stream, idx) in awaited else 0)
        for stream, (i, idx) in last_pos.items():
            if incs[i] == 0:
                incs[i] = 1
        idx_ctr = {s: 0 for s in self.counts}
        for i, (stream, fn, deps, ph) in enumerate(self.entries):
            idx = idx_ctr[stream]
            idx_ctr[stream] += 1
            if stream in self.DMA_STREAMS:
                val_after[(stream, idx)] = (idx % NL, 16 * (idx // NL + 1))
            else:
                cum[stream] += incs[i]
                val_after[(stream, idx)] = (None, cum[stream])

        bodies = {b: [] for b in ("sync", "gpsimd", "scalar", "tensor", "vector")}
        idx_ctr = {s: 0 for s in self.counts}
        for i, (stream, fn, deps, ph) in enumerate(self.entries):
            idx = idx_ctr[stream]
            idx_ctr[stream] += 1
            waits = {}
            for key, j in deps.items():
                st = key[0] if isinstance(key, tuple) else key
                lane, v = val_after[(st, j)]
                k2 = (st, lane)
                if waits.get(k2, -1) < v:
                    waits[k2] = v
            if stream in self.DMA_STREAMS:
                inc_sem = sems[stream][idx % NL]
                if idx >= NL:
                    k2 = (stream, idx % NL)
                    if waits.get(k2, -1) < 16 * (idx // NL):
                        waits[k2] = 16 * (idx // NL)
            else:
                inc_sem = sems[stream]
            bodies[self.BODY[stream]].append(
                (stream, fn, list(waits.items()), incs[i], inc_sem))

        totals = []
        for stream, cnt in self.counts.items():
            if stream in self.DMA_STREAMS:
                for lane in range(min(self.NLANES, cnt)):
                    n_lane = (cnt - lane - 1) // NL + 1
                    totals.append((sems[stream][lane], 16 * n_lane))
            else:
                totals.append((sems[stream], cum[stream]))

        def mk_body(body_name):
            def body(eng):
                hw = {}
                for stream, fn, waits, inc, inc_sem in bodies[body_name]:
                    for (s, lane), v in waits:
                        if hw.get((s, lane), -1) >= v:
                            continue
                        hw[(s, lane)] = v
                        sem = sems[s][lane] if lane is not None else sems[s]
                        eng.wait_ge(sem, v)
                    inst = fn()
                    if inc:
                        if stream == "pe":
                            eng.drain(fusable=False).then_inc(inc_sem, inc)
                        else:
                            inst.then_inc(inc_sem, inc)
                if body_name == "gpsimd":
                    for sem, v in totals:
                        eng.wait_ge(sem, v)
            return body

        block.gpsimd(mk_body("gpsimd"))
        block.sync(mk_body("sync"))
        block.vector(mk_body("vector"))
        block.scalar(mk_body("scalar"))
        block.tensor(mk_body("tensor"))
        return totals


def build_program(D, I, T, TP, debug_taps=False):
    H = D // CH
    HCH = D // 128
    NCH = T // CH
    NPCH = TP // CH
    NIB = I // 128
    NQF = D // FC
    TBS = T // FC
    NP6 = T // 256
    HPG = min(H, 8)
    NG = (H + HPG - 1) // HPG
    assert D % FC == 0 and T % 512 == 0 and TP == T and I % 128 == 0

    nc = bass.Bass("TRN2", target_bir_lowering=False)

    def inp(name, shape, dt=F32):
        return nc.dram_tensor(name, shape, dt, kind="ExternalInput").ap()

    xm = inp("xm", [T, D])
    xp = inp("xp", [TP, D])
    maskc = inp("maskc", [128, 1])
    wqkvT = inp("wqkvT", [D, 3 * D])
    bqkv = inp("bqkv", [1, 3 * D])
    woutT = inp("woutT", [D, D])
    bout = inp("bout", [1, D])
    gT = inp("gT", [D, I], I32)
    uT = inp("uT", [D, I], I32)
    dT = inp("dT", [I, D], I32)
    gs = inp("gs", [128, NIB])
    us = inp("us", [128, NIB])
    dsc = inp("dsc", [1, D])
    slope_sq = inp("slope_sq", [1, H])
    posp1t = inp("posp1t", [1, H * CH])
    poskr = inp("poskr", [1, 128])
    diffcT = inp("diffcT", [128, 128])
    causT = inp("causT", [128, 128])
    identf = inp("identf", [128, 128])

    out = nc.dram_tensor("out", [T, D], F32, kind="ExternalOutput").ap()

    sk = "ExternalOutput" if debug_taps else "Internal"

    def scratch(name, shape, dt):
        return nc.dram_tensor(name, shape, dt, kind=sk).ap()

    q_s = scratch("q_s", [T, D], BF16)
    k_s = scratch("k_s", [T, D], BF16)
    v_s = scratch("v_s", [T, D], BF16)
    xn_s = scratch("xn_s", [128, HCH * T], BF16)
    ap_s = scratch("ap_s", [T, D], F32)
    hsi_s = scratch("hsi_s", [I, T], BF16)
    wd_s = scratch("wd_s", [I, D], BF16)
    g_bf = scratch("g_bf", [D, I], BF16)
    u_bf = scratch("u_bf", [D, I], BF16)
    st_s = scratch("st_s", [128, 1024], F32)
    xr_s = scratch("xr_s", [128, 64 * 1024], BF16)
    wg_s = scratch("wg_s", [128, 8 * 1024], BF16)

    with ExitStack() as ctx:
        def sbt(name, shape, dtype):
            return ctx.enter_context(nc.sbuf_tensor(name, shape, dtype))

        BIGW = max(HCH * T, 2 * NIB * 256, HCH * D)
        big8 = sbt("big8", [128, BIGW], BF16)
        wstream = sbt("wstream", [128, HCH * FC], BF16)
        xtile = sbt("xtile", [128, D], F32)
        S = sbt("S", [128, D], F32)
        lamcr = sbt("lamcr", [128, D], F32)
        dqf = sbt("dqf", [128, D], BF16)
        dmaskT = sbt("dmaskT", [128, D], BF16)
        dkh = sbt("dkh", [128, H], F32)
        dscr = sbt("dscr", [128, D], F32)
        scT = sbt("scT", [128, D], BF16)
        attnsb = sbt("attnsb", [128, D], BF16)
        qp = sbt("qp", [128, D], BF16)
        Sbf = sbt("Sbf", [128, D], BF16)
        kc = sbt("kc", [128, D], BF16)
        vc = sbt("vc", [128, D], BF16)
        kpc = sbt("kpc", [128, D], BF16)
        kfm = sbt("kfm", [128, D], BF16)
        qch = sbt("qch", [128, D], BF16)
        rowpool = sbt("rowpool", [1, 4 * D], BF16)
        onesf = sbt("onesf", [1, 128], F32)
        onesb = sbt("onesb", [1, FC], BF16)
        slp = sbt("slp", [1, H], F32)
        l16 = sbt("l16", [1, H], F32)
        t1 = sbt("t1", [128, D], F32)
        t1b = sbt("t1b", [128, D], F32)
        ap8 = sbt("ap8", [128, D], F32)
        stg = sbt("stg", [128, 2048], BF16)
        sg = sbt("sg", [128, 2 * FC], BF16)
        uu = sbt("uu", [128, 2 * FC], BF16)
        st = sbt("st", [128, 10 * 2 * NCH], F32)
        epsc = sbt("epsc", [128, 1], F32)
        mcol = sbt("mcol", [128, 1], F32)
        gssb = sbt("gssb", [128, NIB], F32)
        ussb = sbt("ussb", [128, NIB], F32)
        identb = sbt("identb", [128, 128], BF16)

        PAt = ctx.enter_context(nc.psum_tensor("PA", [128, 1024], F32))
        PBt = ctx.enter_context(nc.psum_tensor("PB", [128, 1024], F32))
        PCt = ctx.enter_context(nc.psum_tensor("PC", [128, 1024], F32))
        PDt = ctx.enter_context(nc.psum_tensor("PD", [128, 1024], F32))
        PT = PDt[:].bitcast(BF16)

        sems = {}
        for s in ("pe", "dve", "act"):
            sems[s] = ctx.enter_context(nc.semaphore("sem_" + s))
        for s in ("syncq", "gq", "scq"):
            sems[s] = [ctx.enter_context(nc.semaphore(f"sem_{s}_{l}"))
                       for l in range(Planner.NLANES)]
        block = ctx.enter_context(nc.Block())
        P = Planner()

        B = {}
        for nm in ("P0B", "wq0", "wq1", "wq2", "wq3",
                   "xt0", "xt1", "st0", "st1", "qch", "kpc", "kfm", "qp",
                   "Sbf", "scT", "attnsb", "S", "t1", "ap8L", "ap8R", "stg", "stgA", "stgB",
                   "kc", "vc", "sg0", "sg1", "uu0", "uu1",
                   "PA0", "PA1", "PB0", "PB1", "PC0", "PC1", "PD0", "PD1", "t1bB",
                   "xn_sB", "ap_sB", "hsi_sB",
                   "wd_sB", "outB", "g_bfB", "u_bfB"):
            B[nm] = Buf(nm)
        NRG = NCH // 2
        QKV_RG = {nm: [Buf(f"{nm}_{g}") for g in range(NRG)]
                  for nm in ("q_sB", "k_sB", "v_sB")}
        BIG8T = [Buf(f"b8_{i}") for i in range(max(NCH, 16))]
        BIG8 = tuple(BIG8T)
        WS_ALL = (B["wq0"], B["wq1"], B["wq2"], B["wq3"])
        AP8 = (B["ap8L"], B["ap8R"])
        PA_ALL = (B["PA0"], B["PA1"])
        PB_ALL = (B["PB0"], B["PB1"])
        PC_ALL = (B["PC0"], B["PC1"])
        PD_ALL = (B["PD0"], B["PD1"])
        C = B["P0B"]

        big8v = big8[:, 0:HCH * T].rearrange("p (c t) -> p c t", t=T)
        wsv = wstream[:].rearrange("p (c f) -> p c f", f=FC)
        rowA = rowpool[:, 0:2 * D].bitcast(F32)
        rowB = rowpool[:, 2 * D:4 * D].bitcast(F32)
        bqkv_bf = rowpool[:, 0:3 * D]
        bout_bf = rowpool[:, 3 * D:4 * D]
        xt_ap = (xtile[:], ap8[:])
        xt_buf = ((B["xt0"],), AP8)
        kc2 = ap8[:].bitcast(BF16)[:, 0:D]
        vc2 = ap8[:].bitcast(BF16)[:, D:2 * D]

        PHASE = P.set_phase

        PHASE("P0")
        P.dma("syncq", lambda: nc.sync.dma_start(out=slp[:], in_=slope_sq),
              writes=[C])
        P.dma("syncq", lambda: nc.sync.dma_start(out=dscr[0:1, :], in_=dsc),
              writes=[C])
        P.dma("syncq", lambda: nc.sync.dma_start(out=xtile[0:128, 0:128],
                                                 in_=diffcT),
              writes=[C, B["xt0"]])
        P.dma("syncq", lambda: nc.sync.dma_start(out=ap8[0:128, 0:128],
                                                 in_=causT),
              writes=[C, *AP8])
        P.dma("syncq", lambda: nc.sync.dma_start(out=rowA[:, 0:128], in_=poskr),
              writes=[C])
        P.dma("syncq", lambda: nc.sync.dma_start(out=rowB[:, 0:D], in_=posp1t),
              writes=[C])
        P.dma("syncq", lambda: nc.sync.dma_start(out=gssb[:], in_=gs), writes=[C])
        P.dma("syncq", lambda: nc.sync.dma_start(out=ussb[:], in_=us), writes=[C])
        P.dma("syncq", lambda: nc.sync.dma_start(out=mcol[:], in_=maskc),
              writes=[C])
        P.dma("gq", lambda: nc.gpsimd.dma_start(out=identb[:], in_=identf),
              writes=[C])
        P.op("dve", lambda: nc.vector.memset(onesf[:], 1.0), reads=[C], writes=[C])
        P.op("dve", lambda: nc.vector.memset(onesb[:], 1.0), reads=[C], writes=[C])
        P.op("dve", lambda: nc.vector.memset(S[:], 0.0), writes=[B["S"]])
        P.op("dve", lambda: nc.vector.memset(epsc[:], float(EPS)), writes=[C])
        P.op("act", lambda: nc.scalar.activation(slp[:], slp[:], AF.Sqrt),
             reads=[C], writes=[C])
        P.op("pe", lambda: nc.tensor.matmul(PCt[:, 0:H], rowA[:, 0:128], slp[:],
                                            start=True, stop=True),
             reads=[C], writes=[B["PC0"], C])
        P.op("act", lambda: nc.scalar.activation(dkh[:], PCt[:, 0:H], AF.Exp,
                                                 scale=-1.0),
             reads=[C], writes=[C])
        P.op("act", lambda: nc.scalar.activation(l16[:], slp[:], AF.Exp,
                                                 scale=-float(CH)),
             reads=[C], writes=[C])
        P.op("dve", lambda: nc.vector.tensor_copy(
            rowA[:, 0:D].rearrange("o (h c) -> o h c", h=H),
            slp[:].unsqueeze(2).broadcast_to([1, H, CH])),
            reads=[C], writes=[C])
        for i in range(D // FC):
            P.op("pe", lambda i=i: nc.tensor.matmul(
                PAt[:, 0:FC], onesf[:], rowA[:, i * FC:(i + 1) * FC],
                start=True, stop=True), reads=[C], writes=[B["PA0"], C])
            P.op("dve", lambda i=i: nc.vector.tensor_copy(
                t1[:, i * FC:(i + 1) * FC], PAt[:, 0:FC]),
                reads=[C], writes=[B["t1"], C])
        P.op("dve", lambda: nc.vector.tensor_tensor(
            t1[:].rearrange("p (h c) -> p h c", h=H),
            xtile[0:128, 0:128].unsqueeze(1).broadcast_to([128, H, CH]),
            t1[:].rearrange("p (h c) -> p h c", h=H), ALU.mult),
            reads=[C, B["t1"], B["xt0"]], writes=[B["t1"], C])
        P.op("act", lambda: nc.scalar.activation(t1[:], t1[:], AF.Exp,
                                                 scale=-1.0),
             reads=[C, B["t1"]], writes=[B["t1"], C])
        P.op("dve", lambda: nc.vector.tensor_tensor(
            dmaskT[:].rearrange("p (h c) -> p h c", h=H),
            t1[:].rearrange("p (h c) -> p h c", h=H),
            ap8[0:128, 0:128].unsqueeze(1).broadcast_to([128, H, CH]),
            ALU.mult), reads=[C, B["t1"], *AP8], writes=[C])
        P.op("dve", lambda: nc.vector.tensor_tensor(
            rowB[:, 0:D], rowB[:, 0:D], rowA[:, 0:D], ALU.mult),
            reads=[C], writes=[C])
        P.op("act", lambda: nc.scalar.activation(rowB[:, 0:D], rowB[:, 0:D],
                                                 AF.Exp, scale=-1.0),
             reads=[C], writes=[C])
        for i in range(D // FC):
            P.op("pe", lambda i=i: nc.tensor.matmul(
                PAt[:, 0:FC], onesf[:], rowB[:, i * FC:(i + 1) * FC],
                start=True, stop=True), reads=[C], writes=[B["PA0"], C])
            P.op("dve", lambda i=i: nc.vector.tensor_copy(
                dqf[:, i * FC:(i + 1) * FC], PAt[:, 0:FC]),
                reads=[C], writes=[C])
        P.op("dve", lambda: nc.vector.tensor_copy(
            rowA[:, 0:D].rearrange("o (h c) -> o h c", h=H),
            l16[:].unsqueeze(2).broadcast_to([1, H, CH])),
            reads=[C], writes=[C])
        for i in range(D // FC):
            P.op("pe", lambda i=i: nc.tensor.matmul(
                PAt[:, 0:FC], onesf[:], rowA[:, i * FC:(i + 1) * FC],
                start=True, stop=True), reads=[C], writes=[B["PA0"], C])
            P.op("dve", lambda i=i: nc.vector.tensor_copy(
                lamcr[:, i * FC:(i + 1) * FC], PAt[:, 0:FC]),
                reads=[C], writes=[C])
        P.op("dve", lambda: nc.vector.tensor_copy(rowA[:, 0:D], dscr[0:1, :]),
             reads=[C], writes=[C])
        for i in range(D // FC):
            P.op("pe", lambda i=i: nc.tensor.matmul(
                PAt[:, 0:FC], onesf[:], rowA[:, i * FC:(i + 1) * FC],
                start=True, stop=True), reads=[C], writes=[B["PA0"], C])
            P.op("dve", lambda i=i: nc.vector.tensor_copy(
                dscr[:, i * FC:(i + 1) * FC], PAt[:, 0:FC]),
                reads=[C], writes=[C])
        P.dma("gq", lambda: nc.gpsimd.dma_start(out=bqkv_bf, in_=bqkv),
              reads=[C], writes=[C])
        P.dma("gq", lambda: nc.gpsimd.dma_start(out=bout_bf, in_=bout),
              reads=[C], writes=[C])

        def emit_ln_phase(src_dram, slot0):
            STB = [B["st0"], B["st1"]]
            nparts = (D + 1023) // 1024
            dump = (PCt, PCt)
            dumpb = (PC_ALL, PC_ALL)
            for tc in range(NCH):
                par = tc % 2
                xa = xt_ap[par]
                slot = slot0 + tc
                sa = st[:, slot * 10:slot * 10 + 10]
                P.dma("syncq", lambda tc=tc, xa=xa: nc.sync.dma_start(
                    out=xa, in_=src_dram[tc * 128:(tc + 1) * 128, :]),
                    writes=list(xt_buf[par]))
                P.op("dve", lambda xa=xa, sa=sa: nc.vector.reduce_sum(
                    sa[:, 0:1], xa, axis=AX.X),
                    reads=list(xt_buf[par]), writes=STB)
                for ip in range(nparts):
                    w = min(1024, D - ip * 1024)
                    P.op("act", lambda ip=ip, w=w, xa=xa, sa=sa:
                         nc.scalar.activation(
                             dump[ip][:, 0:w], xa[:, ip * 1024:ip * 1024 + w],
                             AF.Square, accum_out=sa[:, 2 + ip:3 + ip]),
                         reads=list(xt_buf[par]), writes=[*dumpb[ip], *STB],
                         sync_self=(ip > 0))
            def sv(col):
                return st[:, slot0 * 10 + col: (slot0 + NCH) * 10: 10]
            if nparts == 2:
                P.op("dve", lambda: nc.vector.tensor_tensor(
                    sv(4), sv(2), sv(3), ALU.add), reads=STB, writes=STB)
            else:
                P.op("dve", lambda: nc.vector.tensor_copy(sv(4), sv(2)),
                     reads=STB, writes=STB)
            P.op("dve", lambda: nc.vector.tensor_scalar(
                sv(1), sv(0), 1.0 / D, None, ALU.mult), reads=STB, writes=STB)
            P.op("dve", lambda: nc.vector.tensor_tensor(
                sv(5), sv(1), sv(1), ALU.mult), reads=STB, writes=STB)
            P.op("dve", lambda: nc.vector.tensor_scalar(
                sv(6), sv(4), 1.0 / D, None, ALU.mult), reads=STB, writes=STB)
            P.op("dve", lambda: nc.vector.tensor_sub(
                sv(7), sv(6), sv(5)), reads=STB, writes=STB)
            P.op("act", lambda: nc.scalar.activation(
                sv(8), sv(7), AF.Sqrt, bias=epsc[:]),
                reads=[C, *STB], writes=STB)
            P.op("dve", lambda: nc.vector.reciprocal(sv(9), sv(8)),
                 reads=STB, writes=STB)
            for tc in range(NCH):
                par = tc % 2
                xa = xt_ap[par]
                slot = slot0 + tc
                sa = st[:, slot * 10:slot * 10 + 10]
                P.dma("syncq", lambda tc=tc, xa=xa: nc.sync.dma_start(
                    out=xa, in_=src_dram[tc * 128:(tc + 1) * 128, :]),
                    writes=list(xt_buf[par]))
                dst = (qch, kpc)[par]
                dstb = (B["qch"], B["kpc"])[par]
                P.op("dve", lambda xa=xa, sa=sa, dst=dst: nc.vector.tensor_scalar(
                    dst[:], xa, sa[:, 1:2], sa[:, 9:10], ALU.subtract, ALU.mult),
                    reads=[*xt_buf[par], *STB], writes=[dstb], sync_self=True)
                for half in range(2 if HCH > 8 else 1):
                    h0 = half * 8
                    h1 = min(HCH, h0 + 8)
                    for hch in range(h0, h1):
                        P.op("pe", lambda hch=hch, dst=dst: nc.tensor.transpose(
                            PT[:, hch * 128:(hch + 1) * 128],
                            dst[:, hch * 128:(hch + 1) * 128], identb[:]),
                            reads=[C, dstb], writes=[PD_ALL[half]])
                    P.op("dve", lambda tc=tc, h0=h0, h1=h1: nc.vector.tensor_copy(
                        big8v[:, h0:h1, tc * 128:(tc + 1) * 128],
                        PT[:, h0 * 128:h1 * 128].rearrange(
                            "p (c t) -> p c t", t=128)),
                        reads=[PD_ALL[half]], writes=[BIG8T[tc]])
                emit_p3b(1)

        def emit_wslab(fcol):
            hh = max(HCH // 2, 1)
            P.dma("gq", lambda fcol=fcol, hh=hh: nc.gpsimd.dma_start(
                out=wstream[:, 0:hh * FC].rearrange("p (c f) -> p c f", f=FC),
                in_=wqkvT.rearrange("(c p) f -> p c f", p=128)[
                    :, 0:hh, fcol * FC:(fcol + 1) * FC]),
                writes=[B["wq0"], B["wq1"]])
            if hh < HCH:
                P.dma("gq", lambda fcol=fcol, hh=hh: nc.gpsimd.dma_start(
                    out=wstream[:, hh * FC:HCH * FC].rearrange(
                        "p (c f) -> p c f", f=FC),
                    in_=wqkvT.rearrange("(c p) f -> p c f", p=128)[
                        :, hh:HCH, fcol * FC:(fcol + 1) * FC]),
                    writes=[B["wq2"], B["wq3"]])

        def ws_bufs(hch):
            return (B["wq0"], B["wq1"]) if hch < max(HCH // 2, 1) \
                else (B["wq2"], B["wq3"])

        def emit_kv_proj(fcols):
            for fcol in fcols:
                emit_wslab(fcol)
                emit_p3b(1)
                if fcol < NQF:
                    dst_s, dstb, foff = q_s, QKV_RG["q_sB"], fcol * FC
                elif fcol < 2 * NQF:
                    dst_s, dstb, foff = k_s, QKV_RG["k_sB"], (fcol - NQF) * FC
                else:
                    dst_s, dstb, foff = v_s, QKV_RG["v_sB"], (fcol - 2 * NQF) * FC
                for tcq in range(NCH // 4):
                    for tci in range(4):
                        tc = tcq * 4 + tci
                        half = tci // 2
                        hb = (B["stgA"], B["stgB"])[half]
                        pab = PA_ALL[tci % 2]
                        pav = PAt[:, (tci % 2) * FC:(tci % 2 + 1) * FC]
                        for hch in range(HCH):
                            P.op("pe", lambda tc=tc, hch=hch, pav=pav:
                                 nc.tensor.matmul(
                                     pav, big8v[:, hch, tc * 128:(tc + 1) * 128],
                                     wsv[:, hch, :],
                                     start=(hch == 0), stop=False),
                                 reads=[BIG8T[tc], *ws_bufs(hch)], writes=[pab])
                        P.op("pe", lambda fcol=fcol, pav=pav: nc.tensor.matmul(
                            pav, onesb[:, 0:128],
                            bqkv_bf[:, fcol * FC:(fcol + 1) * FC],
                            start=False, stop=True),
                            reads=[C], writes=[pab])
                        P.op("act", lambda tci=tci, pav=pav: nc.scalar.copy(
                            stg[:, tci * FC:(tci + 1) * FC], pav),
                            reads=[pab], writes=[hb])
                        if tci % 2 == 1:
                            P.dma("scq", lambda dst_s=dst_s, foff=foff,
                                  tcq=tcq, half=half:
                                  nc.scalar.dma_start(
                                      out=dst_s.rearrange(
                                          "(c p) f -> p c f", p=128)[
                                          :, tcq * 4 + 2 * half:
                                          tcq * 4 + 2 * half + 2,
                                          foff:foff + FC],
                                      in_=stg[:, half * 2 * FC:
                                              (half + 1) * 2 * FC].rearrange(
                                          "p (c f) -> p c f", f=FC)),
                                  reads=[hb], writes=[dstb[tcq * 2 + half]])

        def emit_chunk_loads(j, kdst, kdb, vdst, vdb):
            P.dma("syncq", lambda j=j, kdst=kdst: nc.sync.dma_start(
                out=kdst, in_=k_s[j * 128:(j + 1) * 128, :]),
                reads=[QKV_RG["k_sB"][j // 2]], writes=[kdb])
            P.dma("syncq", lambda j=j, vdst=vdst: nc.sync.dma_start(
                out=vdst, in_=v_s[j * 128:(j + 1) * 128, :]),
                reads=[QKV_RG["v_sB"][j // 2]], writes=[vdb])

        def emit_kdecay(ksrc, kb):
            P.op("dve", lambda ksrc=ksrc: nc.vector.tensor_tensor(
                kpc[:].rearrange("p (h c) -> p h c", h=H),
                ksrc.rearrange("p (h c) -> p h c", h=H),
                dkh[:].unsqueeze(2).broadcast_to([128, H, CH]), ALU.mult),
                reads=[C, kb], writes=[B["kpc"]])

        def emit_A_supdate(vsrc, vb):
            for g in range(NG):
                h0 = g * HPG
                nh = min(HPG, H - h0)
                for i in range(nh):
                    P.op("pe", lambda i=i, h0=h0, vsrc=vsrc: nc.tensor.matmul(
                        PBt[:, i * 128:(i + 1) * 128],
                        kpc[:, (h0 + i) * 128:(h0 + i + 1) * 128],
                        vsrc[:, (h0 + i) * 128:(h0 + i + 1) * 128],
                        start=True, stop=True),
                        reads=[B["kpc"], vb], writes=list(PB_ALL))
                sl = slice(h0 * 128, (h0 + nh) * 128)
                w = nh * 128
                P.op("dve", lambda sl=sl: nc.vector.tensor_tensor(
                    S[:, sl], S[:, sl], lamcr[:, sl], ALU.mult),
                    reads=[C, B["S"]], writes=[B["S"]])
                P.op("dve", lambda sl=sl, w=w: nc.vector.tensor_tensor(
                    S[:, sl], S[:, sl], PBt[:, 0:w], ALU.add),
                    reads=[B["S"], *PB_ALL], writes=[B["S"]])

        def emit_p3b_item(ib):
            PH = P.phase
            P.set_phase("P3b")
            P.dma("gq", lambda ib=ib: nc.gpsimd.dma_start(
                out=attnsb[:, 0:D], in_=dT[ib * 128:(ib + 1) * 128, :]),
                writes=[B["attnsb"]])
            P.op("dve", lambda: nc.vector.tensor_scalar(
                attnsb[:, 0:D], attnsb[:, 0:D], 128.0, None, ALU.subtract),
                reads=[B["attnsb"]], writes=[B["attnsb"]])
            P.op("dve", lambda: nc.vector.tensor_tensor(
                attnsb[:, 0:D], attnsb[:, 0:D], dscr[:, 0:D], ALU.mult),
                reads=[C, B["attnsb"]], writes=[B["attnsb"]])
            P.dma("scq", lambda ib=ib: nc.scalar.dma_start(
                out=wd_s[ib * 128:(ib + 1) * 128, :], in_=attnsb[:, 0:D]),
                reads=[B["attnsb"]], writes=[B["wd_sB"]])
            P.set_phase(PH)
        p3b_next = [0]
        NGU = 16
        gu_next = [0]

        def emit_gu_item(i):
            src, dst, bb = ((gT, g_bf, B["g_bfB"]) if i % 2 == 0
                            else (uT, u_bf, B["u_bfB"]))
            r = (i // 2) * (D // (NGU // 2))
            r1 = r + D // (NGU // 2)
            P.dma("gq", lambda src=src, dst=dst, r=r, r1=r1:
                  nc.gpsimd.dma_start(out=dst[r:r1, :], in_=src[r:r1, :]),
                  writes=[bb])

        def emit_p3b(n):
            while n > 0 and p3b_next[0] < NIB:
                emit_p3b_item(p3b_next[0])
                p3b_next[0] += 1
                n -= 1

        def emit_gu(n):
            pass

        PHASE("P1a")
        emit_ln_phase(xp, 0)
        PHASE("P1b")
        emit_kv_proj(range(NQF, 3 * NQF))
        PHASE("P2")
        emit_ln_phase(xm, NCH)
        P.dma("scq", lambda: nc.scalar.dma_start(out=xn_s, in_=big8[:, 0:HCH * T]),
              reads=list(BIG8), writes=[B["xn_sB"]])

        PHASE("P1c")
        for j in range(NPCH):
            par = j % 2
            kdst, kdb = (kc[:], B["kc"]) if par == 0 else (kc2, B["ap8L"])
            vdst, vdb = (vc[:], B["vc"]) if par == 0 else (vc2, B["ap8R"])
            emit_chunk_loads(j, kdst, kdb, vdst, vdb)
            emit_kdecay(kdst, kdb)
            emit_A_supdate(vdst, vdb)
        P.op("dve", lambda: nc.vector.tensor_scalar(
            S[:], S[:], mcol[:], None, ALU.mult),
            reads=[C, B["S"]], writes=[B["S"]])

        PHASE("P3")
        emit_kv_proj(range(3 * NQF))

        emit_p3b(NIB)

        PHASE("P4")
        P.dma("gq", lambda: nc.gpsimd.dma_start(
            out=big8[:, 0:HCH * D].rearrange("p (c f) -> p c f", f=D),
            in_=woutT.rearrange("(c p) f -> p c f", p=128)),
            writes=list(BIG8))
        wov = big8[:, 0:HCH * D].rearrange("p (c f) -> p c f", f=D)
        qfm = t1b[:].bitcast(BF16)[:, 0:D]
        for j in range(NCH):
            par = j % 2
            kdst, kdb = (kc[:], B["kc"]) if par == 0 else (kc2, B["ap8L"])
            vdst, vdb = (vc[:], B["vc"]) if par == 0 else (vc2, B["ap8R"])
            qdst = qch[:] if par == 0 else stg[:, 0:D]
            qdbs = [B["qch"]] if par == 0 else [B["stgA"], B["stgB"]]
            emit_chunk_loads(j, kdst, kdb, vdst, vdb)
            P.dma("syncq", lambda j=j, qdst=qdst: nc.sync.dma_start(
                out=qdst, in_=q_s[j * 128:(j + 1) * 128, :]),
                reads=[QKV_RG["q_sB"][j // 2]], writes=qdbs)
            emit_kdecay(kdst, kdb)
            for half in range(2 if HCH > 8 else 1):
                h0 = half * 8
                h1 = min(HCH, h0 + 8)
                for hch in range(h0, h1):
                    P.op("pe", lambda hch=hch, kdst=kdst: nc.tensor.transpose(
                        PT[:, hch * 128:(hch + 1) * 128],
                        kdst[:, hch * 128:(hch + 1) * 128], identb[:]),
                        reads=[C, kdb], writes=[PD_ALL[half]])
                P.op("dve", lambda h0=h0, h1=h1: nc.vector.tensor_copy(
                    kfm[:, h0 * 128:h1 * 128], PT[:, h0 * 128:h1 * 128]),
                    reads=[PD_ALL[half]], writes=[B["kfm"]])
            for half in range(2 if HCH > 8 else 1):
                h0 = half * 8
                h1 = min(HCH, h0 + 8)
                for hch in range(h0, h1):
                    P.op("pe", lambda hch=hch, qdst=qdst: nc.tensor.transpose(
                        PT[:, hch * 128:(hch + 1) * 128],
                        qdst[:, hch * 128:(hch + 1) * 128], identb[:]),
                        reads=[C, *qdbs], writes=[PD_ALL[half]])
                P.op("dve", lambda h0=h0, h1=h1: nc.vector.tensor_copy(
                    qfm[:, h0 * 128:h1 * 128], PT[:, h0 * 128:h1 * 128]),
                    reads=[PD_ALL[half]], writes=[B["t1bB"]])
            P.op("dve", lambda: nc.vector.tensor_tensor(
                qp[:].rearrange("p (h c) -> p h c", h=H),
                qfm.rearrange("p (h c) -> p h c", h=H),
                dqf[:].rearrange("p (h c) -> p h c", h=H), ALU.mult),
                reads=[C, B["t1bB"]], writes=[B["qp"]])
            P.op("act", lambda: nc.scalar.copy(Sbf[:], S[:]),
                 reads=[B["S"]], writes=[B["Sbf"]])
            for g in range(NG):
                h0 = g * HPG
                nh = min(HPG, H - h0)
                gsl = slice(h0 * 128, (h0 + nh) * 128)
                gw = nh * 128
                for i in range(nh):
                    h = h0 + i
                    P.op("pe", lambda i=i, h=h: nc.tensor.matmul(
                        PAt[:, i * 128:(i + 1) * 128],
                        kfm[:, h * 128:(h + 1) * 128],
                        qfm[:, h * 128:(h + 1) * 128],
                        start=True, stop=True),
                        reads=[B["kfm"], B["t1bB"]], writes=list(PA_ALL))
                P.op("dve", lambda gsl=gsl, gw=gw: nc.vector.tensor_tensor(
                    scT[:, gsl], PAt[:, 0:gw], dmaskT[:, gsl], ALU.mult),
                    reads=[C, *PA_ALL], writes=[B["scT"]])
                for i in range(nh):
                    h = h0 + i
                    P.op("pe", lambda i=i, h=h: nc.tensor.matmul(
                        PBt[:, i * 128:(i + 1) * 128],
                        Sbf[:, h * 128:(h + 1) * 128],
                        qp[:, h * 128:(h + 1) * 128],
                        start=True, stop=False),
                        reads=[B["Sbf"], B["qp"]], writes=list(PB_ALL))
                    P.op("pe", lambda i=i, h=h, vdst=vdst: nc.tensor.matmul(
                        PBt[:, i * 128:(i + 1) * 128],
                        vdst[:, h * 128:(h + 1) * 128],
                        scT[:, h * 128:(h + 1) * 128],
                        start=False, stop=True),
                        reads=[vdb, B["scT"]], writes=list(PB_ALL))
                P.op("act", lambda gsl=gsl, gw=gw: nc.scalar.copy(
                    attnsb[:, gsl], PBt[:, 0:gw]),
                    reads=list(PB_ALL), writes=[B["attnsb"]])
                for i in range(nh):
                    h = h0 + i
                    P.op("pe", lambda i=i, h=h, vdst=vdst: nc.tensor.matmul(
                        PAt[:, i * 128:(i + 1) * 128],
                        kpc[:, h * 128:(h + 1) * 128],
                        vdst[:, h * 128:(h + 1) * 128],
                        start=True, stop=True),
                        reads=[B["kpc"], vdb], writes=list(PA_ALL))
                P.op("dve", lambda gsl=gsl: nc.vector.tensor_tensor(
                    S[:, gsl], S[:, gsl], lamcr[:, gsl], ALU.mult),
                    reads=[C, B["S"], B["Sbf"]], writes=[B["S"]])
                P.op("dve", lambda gsl=gsl, gw=gw: nc.vector.tensor_tensor(
                    S[:, gsl], S[:, gsl], PAt[:, 0:gw], ALU.add),
                    reads=[B["S"], *PA_ALL], writes=[B["S"]])
            for ph in range((D + 1023) // 1024):
                pw = min(1024, D - ph * 1024)
                pcb = PC_ALL[ph % 2]
                for fsub in range(pw // FC):
                    fo = ph * 1024 + fsub * FC
                    pcv = PCt[:, (ph % 2) * 0 + fsub * FC:fsub * FC + FC] \
                        if False else PCt[:, fsub * FC:(fsub + 1) * FC]
                    for dch in range(HCH):
                        P.op("pe", lambda fsub=fsub, fo=fo, dch=dch:
                             nc.tensor.matmul(
                                 PCt[:, fsub * FC:(fsub + 1) * FC],
                                 attnsb[:, dch * 128:(dch + 1) * 128],
                                 wov[:, dch, fo:fo + FC],
                                 start=(dch == 0), stop=False),
                             reads=[B["attnsb"], *BIG8], writes=list(PC_ALL))
                    P.op("pe", lambda fsub=fsub, fo=fo: nc.tensor.matmul(
                        PCt[:, fsub * FC:(fsub + 1) * FC],
                        onesb[:, 0:128], bout_bf[:, fo:fo + FC],
                        start=False, stop=True),
                        reads=[C], writes=list(PC_ALL))
                P.op("dve", lambda ph=ph, pw=pw: nc.vector.tensor_copy(
                    t1[:, ph * 1024:ph * 1024 + pw], PCt[:, 0:pw]),
                    reads=list(PC_ALL), writes=[B["t1"]])
            P.dma("scq", lambda j=j: nc.scalar.dma_start(
                out=ap_s[j * 128:(j + 1) * 128, :], in_=t1[:]),
                reads=[B["t1"]], writes=[B["ap_sB"]])

        PHASE("P5")
        P.dma("gq", lambda: nc.gpsimd.dma_start(out=big8[:, 0:HCH * T], in_=xn_s),
              reads=[B["xn_sB"]], writes=list(BIG8))
        for ib in range(NIB):
            par = ib % 2
            goff = par * 2 * HCH * 128
            uoff = goff + HCH * 128
            gb = B["wq0"] if par == 0 else B["wq2"]
            ub = B["wq1"] if par == 0 else B["wq3"]
            P.dma("gq", lambda ib=ib, goff=goff: nc.gpsimd.dma_start(
                out=wstream[:, goff:goff + HCH * 128].rearrange(
                    "p (c f) -> p c f", f=128),
                in_=gT.rearrange("(c p) i -> p c i", p=128)[
                    :, :, ib * 128:(ib + 1) * 128]),
                writes=[gb])
            P.dma("gq", lambda ib=ib, uoff=uoff: nc.gpsimd.dma_start(
                out=wstream[:, uoff:uoff + HCH * 128].rearrange(
                    "p (c f) -> p c f", f=128),
                in_=uT.rearrange("(c p) i -> p c i", p=128)[
                    :, :, ib * 128:(ib + 1) * 128]),
                writes=[ub])
            P.op("dve", lambda goff=goff: nc.vector.tensor_scalar(
                wstream[:, goff:goff + 2 * HCH * 128],
                wstream[:, goff:goff + 2 * HCH * 128],
                128.0, None, ALU.subtract),
                reads=[gb, ub], writes=[gb, ub])
            for tb in range(TBS):
                tpar = tb % 2
                pab = PA_ALL[tpar]
                pbb = PB_ALL[tpar]
                pav = PAt[:, tpar * FC:(tpar + 1) * FC]
                pbv = PBt[:, tpar * FC:(tpar + 1) * FC]
                for hch in range(HCH):
                    P.op("pe", lambda hch=hch, tb=tb, goff=goff, pav=pav:
                         nc.tensor.matmul(
                             pav, wstream[:, goff + hch * 128:
                                          goff + (hch + 1) * 128],
                             big8v[:, hch, tb * FC:(tb + 1) * FC],
                             start=(hch == 0), stop=(hch == HCH - 1)),
                         reads=[gb, *BIG8], writes=[pab])
                for hch in range(HCH):
                    P.op("pe", lambda hch=hch, tb=tb, uoff=uoff, pbv=pbv:
                         nc.tensor.matmul(
                             pbv, wstream[:, uoff + hch * 128:
                                          uoff + (hch + 1) * 128],
                             big8v[:, hch, tb * FC:(tb + 1) * FC],
                             start=(hch == 0), stop=(hch == HCH - 1)),
                         reads=[ub, *BIG8], writes=[pbb])
                sgb = (B["sg0"], B["sg1"])[tpar]
                uub = (B["uu0"], B["uu1"])[tpar]
                P.op("act", lambda ib=ib, tpar=tpar, pav=pav:
                     nc.scalar.activation(
                         sg[:, tpar * FC:(tpar + 1) * FC], pav, AF.Silu,
                         scale=gssb[:, ib:ib + 1]),
                     reads=[C, pab], writes=[sgb])
                P.op("act", lambda ib=ib, tpar=tpar, pbv=pbv:
                     nc.scalar.activation(
                         uu[:, tpar * FC:(tpar + 1) * FC], pbv, AF.Copy,
                         scale=ussb[:, ib:ib + 1]),
                     reads=[C, pbb], writes=[uub])
                shb = (B["stgA"], B["stgB"])[tb // 2]
                P.op("dve", lambda tb=tb, tpar=tpar: nc.vector.tensor_tensor(
                    stg[:, tb * FC:(tb + 1) * FC],
                    sg[:, tpar * FC:(tpar + 1) * FC],
                    uu[:, tpar * FC:(tpar + 1) * FC],
                    ALU.mult), reads=[sgb, uub], writes=[shb])
                P.dma("scq", lambda ib=ib, tb=tb: nc.scalar.dma_start(
                    out=hsi_s[ib * 128:(ib + 1) * 128, tb * FC:(tb + 1) * FC],
                    in_=stg[:, tb * FC:(tb + 1) * FC]),
                    reads=[shb], writes=[B["hsi_sB"]])

        PHASE("P6")
        PSUMS = [(PAt, PA_ALL), (PBt, PB_ALL), (PCt, PC_ALL), (PDt, PD_ALL)]

        def emit_slab(tp):
            spar = tp % 2
            soff = spar * NIB * 256
            slb = BIG8T[spar * 8:spar * 8 + 8]
            P.dma("syncq", lambda tp=tp, soff=soff: nc.sync.dma_start(
                out=big8[:, soff:soff + NIB * 256].rearrange(
                    "p (b t) -> p b t", t=256),
                in_=hsi_s.rearrange("(b p) t -> p b t", p=128)[
                    :, :, tp * 256:(tp + 1) * 256]),
                reads=[B["hsi_sB"]], writes=slb)

        emit_slab(0)
        for tp in range(NP6):
            PHASE("P6")
            spar = tp % 2
            soff = spar * NIB * 256
            slab = big8[:, soff:soff + NIB * 256].rearrange(
                "p (b t) -> p b t", t=256)
            slbufs = BIG8T[spar * 8:spar * 8 + 8]
            for ib in range(NIB):
                wpar = ib % 4
                wdst = (vc[:], kc[:], kpc[:], kfm[:])[wpar]
                wdb = (B["vc"], B["kc"], B["kpc"], B["kfm"])[wpar]
                wq = ("gq", "syncq")[ib % 2]
                dmafn = (nc.gpsimd, nc.sync)[ib % 2]
                P.dma(wq, lambda ib=ib, wdst=wdst, dmafn=dmafn: dmafn.dma_start(
                    out=wdst, in_=wd_s[ib * 128:(ib + 1) * 128, :]),
                    reads=[B["wd_sB"]], writes=[wdb])
                for tcs in range(2):
                    for fq in range(D // FC):
                        pi = tcs * 2 + fq // 2 if D > 1024 else tcs
                        pst, psb = PSUMS[pi]
                        po = (fq % 2) * FC if D > 1024 else fq * FC
                        P.op("pe", lambda ib=ib, tcs=tcs, fq=fq, pst=pst,
                             po=po, wdst=wdst, slab=slab: nc.tensor.matmul(
                                 pst[:, po:po + FC],
                                 slab[:, ib, tcs * 128:(tcs + 1) * 128],
                                 wdst[:, fq * FC:(fq + 1) * FC],
                                 start=(ib == 0), stop=(ib == NIB - 1)),
                             reads=slbufs + [wdb], writes=list(psb))
            if tp + 1 < NP6:
                emit_slab(tp + 1)
            PHASE("P6m")
            for tcs in range(2):
                j = tp * 2 + tcs
                ev = (t1, t1b)[tcs]
                evb = (B["t1"], B["t1bB"])[tcs]
                P.dma("syncq", lambda j=j: nc.sync.dma_start(
                    out=ap8[:], in_=ap_s[j * 128:(j + 1) * 128, :]),
                    reads=[B["ap_sB"]], writes=list(AP8))
                P.dma("syncq", lambda j=j: nc.sync.dma_start(
                    out=xtile[:], in_=xm[j * 128:(j + 1) * 128, :]),
                    writes=[B["xt0"]])
                nhal = (D + 1023) // 1024
                for ph in range(nhal):
                    pw = min(1024, D - ph * 1024)
                    pi = tcs * 2 + ph if D > 1024 else tcs
                    pst, psb = PSUMS[pi]
                    P.op("dve", lambda ph=ph, pw=pw, pst=pst, ev=ev:
                         nc.vector.tensor_copy(
                             ev[:, ph * 1024:ph * 1024 + pw], pst[:, 0:pw]),
                         reads=[*psb], writes=[evb])
                P.op("dve", lambda ev=ev: nc.vector.tensor_tensor(
                    ev[:], ev[:], ap8[:], ALU.add),
                    reads=[evb, *AP8], writes=[evb], sync_self=True)
                P.op("dve", lambda ev=ev: nc.vector.tensor_tensor(
                    ev[:], ev[:], xtile[:], ALU.add),
                    reads=[evb, B["xt0"]], writes=[evb])
                P.dma("scq", lambda j=j, ev=ev: nc.scalar.dma_start(
                    out=out[j * 128:(j + 1) * 128, :], in_=ev[:]),
                    reads=[evb], writes=[B["outB"]])


        P.finalize(nc, block, sems)
        rem = nc.sbuf_bytes_remaining
        print(f"[build] D={D} I={I} T={T}: {len(P.entries)} ops, "
              f"sbuf_remaining={rem}B/part")
    return nc


_CACHE = {}


def _get_program(D, I, T, TP, debug_taps=False):
    key = (D, I, T, TP, debug_taps)
    if key not in _CACHE:
        _CACHE[key] = build_program(D, I, T, TP, debug_taps)
    return _CACHE[key]


def make_const_inputs(D, H):
    pos = np.arange(CH, dtype=np.float32)
    return dict(
        slope_sq=np.ldexp(np.ones(H, np.float32), -(np.arange(H) + 1)).reshape(1, H),
        posp1t=np.tile(pos + 1.0, H).reshape(1, H * CH).astype(np.float32),
        poskr=(CH - 1.0 - pos).reshape(1, CH).astype(np.float32),
        diffcT=np.maximum(pos[None, :] - pos[:, None], 0.0).astype(np.float32),
        causT=(pos[None, :] >= pos[:, None]).astype(np.float32),
        identf=np.eye(128, dtype=np.float32),
    )


def make_core_inputs(x_full, b, h, weights, D, H, I, T, TP):
    d = dict(weights)
    d.update(make_const_inputs(D, H))
    d["xm"] = np.ascontiguousarray(x_full[b, h * T:(h + 1) * T, :])
    d["xp"] = np.ascontiguousarray(x_full[b, 0:TP, :])
    d["maskc"] = np.full((128, 1), 1.0 if h == 1 else 0.0, np.float32)
    return d


def prep_weights(w_qkv, b_qkv, w_out, b_out, gate_w, gate_scale, up_w, up_scale,
                 down_w, down_scale, D, I):
    NIB = I // 128
    return dict(
        wqkvT=np.ascontiguousarray(w_qkv.T.astype(np.float32, copy=False)),
        bqkv=np.ascontiguousarray(b_qkv.reshape(1, -1).astype(np.float32, copy=False)),
        woutT=np.ascontiguousarray(w_out.T.astype(np.float32, copy=False)),
        bout=np.ascontiguousarray(b_out.reshape(1, -1).astype(np.float32, copy=False)),
        gT=np.ascontiguousarray(np.asarray(gate_w).T.astype(np.int32, copy=False)),
        uT=np.ascontiguousarray(np.asarray(up_w).T.astype(np.int32, copy=False)),
        dT=np.ascontiguousarray(np.asarray(down_w).T.astype(np.int32, copy=False)),
        gs=np.ascontiguousarray(
            np.asarray(gate_scale, np.float32).reshape(NIB, 128).T),
        us=np.ascontiguousarray(
            np.asarray(up_scale, np.float32).reshape(NIB, 128).T),
        dsc=np.ascontiguousarray(
            np.asarray(down_scale, np.float32).reshape(1, -1)),
    )


def kernel(L0_attn_x, L0_attn_w_qkv, L0_attn_b_qkv, L0_attn_w_out, L0_attn_b_out,
           L0_ffn_gate_w, L0_ffn_gate_scale, L0_ffn_up_w, L0_ffn_up_scale,
           L0_ffn_down_w, L0_ffn_down_scale):
    x = np.asarray(L0_attn_x, dtype=np.float32)
    B, N, D = x.shape
    H = D // CH
    I = np.asarray(L0_ffn_gate_w).shape[0]
    T = N // 2
    TP = T
    weights = prep_weights(
        np.asarray(L0_attn_w_qkv, np.float32), np.asarray(L0_attn_b_qkv, np.float32),
        np.asarray(L0_attn_w_out, np.float32), np.asarray(L0_attn_b_out, np.float32),
        L0_ffn_gate_w, L0_ffn_gate_scale, L0_ffn_up_w, L0_ffn_up_scale,
        L0_ffn_down_w, L0_ffn_down_scale, D, I)

    nc = _get_program(D, I, T, TP)
    in_maps = []
    for c in range(8):
        b, h = c // 2, c % 2
        in_maps.append(make_core_inputs(x, b, h, weights, D, H, I, T, TP))
    res = run_bass_kernel_spmd(nc, in_maps, core_ids=list(range(8)))
    outp = np.empty((B, N, D), dtype=np.float32)
    for c in range(8):
        b, h = c // 2, c % 2
        outp[b, h * T:(h + 1) * T, :] = np.asarray(res.results[c]["out"])
    return outp


# version 17
# speedup vs baseline: 1.1059x; 1.0071x over previous
import os
import numpy as np
import ml_dtypes
from contextlib import ExitStack

import concourse.bass as bass
import concourse.mybir as mybir
from concourse.bass_utils import run_bass_kernel_spmd

F32 = mybir.dt.float32
BF16 = mybir.dt.bfloat16
I32 = mybir.dt.int32
AF = mybir.ActivationFunctionType
ALU = mybir.AluOpType
AX = mybir.AxisListType
BF = ml_dtypes.bfloat16

CH = 128
FC = 512
EPS = 1e-5

SERIAL_PHASES = set(
    p for p in os.environ.get("KSERIAL", "").split(",") if p)
SERIAL_ALL = os.environ.get("KSERIAL", "") == "all"


class Buf:
    __slots__ = ("name", "w", "r")

    def __init__(self, name):
        self.name = name
        self.w = {}
        self.r = {}


class Planner:
    DMA_STREAMS = ("syncq", "gq", "scq")
    BODY = {"syncq": "sync", "gq": "gpsimd", "scq": "scalar",
            "pe": "tensor", "dve": "vector", "act": "scalar"}

    def __init__(self):
        self.entries = []
        self.counts = {}
        self.phase = "P0"
        self.prev_any = None

    def set_phase(self, name):
        self.phase = name

    def op(self, stream, fn, reads=(), writes=(), sync_self=False):
        idx = self.counts.get(stream, 0)
        deps = {}

        def add(tok, raw=True):
            if tok is None:
                return
            s, i = tok
            if s == stream and i >= idx:
                return
            if s == stream and not sync_self:
                if not (raw and stream in ("dve", "act")):
                    return
            key = (s, i % self.NLANES) if s in self.DMA_STREAMS else s
            if deps.get(key, -1) < i:
                deps[key] = i
        serial = SERIAL_ALL or (self.phase in SERIAL_PHASES)
        if serial and self.prev_any is not None:
            add(self.prev_any)
        for b in reads:
            for (ws, _), i in b.w.items():
                add((ws, i), raw=True)
        for b in writes:
            for (ws, _), i in b.w.items():
                add((ws, i), raw=False)
            for (rs, _), i in b.r.items():
                add((rs, i), raw=False)
        tok = (stream, idx)
        key = ((stream, idx % self.NLANES)
               if stream in self.DMA_STREAMS else (stream, 0))
        for b in reads:
            if b.r.get(key, -1) < idx:
                b.r[key] = idx
        for b in writes:
            if b.w.get(key, -1) < idx:
                b.w[key] = idx
            b.r = {}
        self.entries.append((stream, fn, deps, self.phase))
        self.counts[stream] = idx + 1
        self.prev_any = tok
        return tok

    def dma(self, queue, fn, reads=(), writes=()):
        return self.op(queue, fn, reads, writes)

    NLANES = 16

    def finalize(self, nc, block, sems):
        NL = self.NLANES
        awaited = set()
        for stream, fn, deps, ph in self.entries:
            for key, i in deps.items():
                st = key[0] if isinstance(key, tuple) else key
                awaited.add((st, i))
        idx_ctr = {s: 0 for s in self.counts}
        incs = []
        cum = {s: 0 for s in self.counts}
        val_after = {}
        last_pos = {}
        for i, (stream, fn, deps, ph) in enumerate(self.entries):
            idx = idx_ctr[stream]
            idx_ctr[stream] += 1
            last_pos[stream] = (i, idx)
            if stream in self.DMA_STREAMS:
                incs.append(16)
            else:
                incs.append(1 if (stream, idx) in awaited else 0)
        for stream, (i, idx) in last_pos.items():
            if incs[i] == 0:
                incs[i] = 1
        idx_ctr = {s: 0 for s in self.counts}
        for i, (stream, fn, deps, ph) in enumerate(self.entries):
            idx = idx_ctr[stream]
            idx_ctr[stream] += 1
            if stream in self.DMA_STREAMS:
                val_after[(stream, idx)] = (idx % NL, 16 * (idx // NL + 1))
            else:
                cum[stream] += incs[i]
                val_after[(stream, idx)] = (None, cum[stream])

        bodies = {b: [] for b in ("sync", "gpsimd", "scalar", "tensor", "vector")}
        idx_ctr = {s: 0 for s in self.counts}
        for i, (stream, fn, deps, ph) in enumerate(self.entries):
            idx = idx_ctr[stream]
            idx_ctr[stream] += 1
            waits = {}
            for key, j in deps.items():
                st = key[0] if isinstance(key, tuple) else key
                lane, v = val_after[(st, j)]
                k2 = (st, lane)
                if waits.get(k2, -1) < v:
                    waits[k2] = v
            if stream in self.DMA_STREAMS:
                inc_sem = sems[stream][idx % NL]
                if idx >= NL:
                    k2 = (stream, idx % NL)
                    if waits.get(k2, -1) < 16 * (idx // NL):
                        waits[k2] = 16 * (idx // NL)
            else:
                inc_sem = sems[stream]
            bodies[self.BODY[stream]].append(
                (stream, fn, list(waits.items()), incs[i], inc_sem))

        totals = []
        for stream, cnt in self.counts.items():
            if stream in self.DMA_STREAMS:
                for lane in range(min(self.NLANES, cnt)):
                    n_lane = (cnt - lane - 1) // NL + 1
                    totals.append((sems[stream][lane], 16 * n_lane))
            else:
                totals.append((sems[stream], cum[stream]))

        def mk_body(body_name):
            def body(eng):
                hw = {}
                for stream, fn, waits, inc, inc_sem in bodies[body_name]:
                    for (s, lane), v in waits:
                        if hw.get((s, lane), -1) >= v:
                            continue
                        hw[(s, lane)] = v
                        sem = sems[s][lane] if lane is not None else sems[s]
                        eng.wait_ge(sem, v)
                    inst = fn()
                    if inc:
                        if stream == "pe":
                            eng.drain(fusable=False).then_inc(inc_sem, inc)
                        else:
                            inst.then_inc(inc_sem, inc)
                if body_name == "gpsimd":
                    for sem, v in totals:
                        eng.wait_ge(sem, v)
            return body

        block.gpsimd(mk_body("gpsimd"))
        block.sync(mk_body("sync"))
        block.vector(mk_body("vector"))
        block.scalar(mk_body("scalar"))
        block.tensor(mk_body("tensor"))
        return totals


def build_program(D, I, T, TP, debug_taps=False):
    H = D // CH
    HCH = D // 128
    NCH = T // CH
    NPCH = TP // CH
    NIB = I // 128
    NQF = D // FC
    TBS = T // FC
    NP6 = T // 256
    HPG = min(H, 8)
    NG = (H + HPG - 1) // HPG
    assert D % FC == 0 and T % 512 == 0 and TP == T and I % 128 == 0

    nc = bass.Bass("TRN2", target_bir_lowering=False)

    def inp(name, shape, dt=F32):
        return nc.dram_tensor(name, shape, dt, kind="ExternalInput").ap()

    xm = inp("xm", [T, D])
    xp = inp("xp", [TP, D])
    maskc = inp("maskc", [128, 1])
    wqkvT = inp("wqkvT", [D, 3 * D])
    bqkv = inp("bqkv", [1, 3 * D])
    woutT = inp("woutT", [D, D])
    bout = inp("bout", [1, D])
    gT = inp("gT", [D, I], I32)
    uT = inp("uT", [D, I], I32)
    dT = inp("dT", [I, D], I32)
    gs = inp("gs", [128, NIB])
    us = inp("us", [128, NIB])
    dsc = inp("dsc", [1, D])
    slope_sq = inp("slope_sq", [1, H])
    posp1t = inp("posp1t", [1, H * CH])
    poskr = inp("poskr", [1, 128])
    diffcT = inp("diffcT", [128, 128])
    causT = inp("causT", [128, 128])
    identf = inp("identf", [128, 128])

    out = nc.dram_tensor("out", [T, D], F32, kind="ExternalOutput").ap()

    sk = "ExternalOutput" if debug_taps else "Internal"

    def scratch(name, shape, dt):
        return nc.dram_tensor(name, shape, dt, kind=sk).ap()

    q_s = scratch("q_s", [T, D], BF16)
    k_s = scratch("k_s", [T, D], BF16)
    v_s = scratch("v_s", [T, D], BF16)
    xn_s = scratch("xn_s", [128, HCH * T], BF16)
    ap_s = scratch("ap_s", [T, D], F32)
    hsi_s = scratch("hsi_s", [I, T], BF16)
    wd_s = scratch("wd_s", [I, D], BF16)
    g_bf = scratch("g_bf", [D, I], BF16)
    u_bf = scratch("u_bf", [D, I], BF16)
    st_s = scratch("st_s", [128, 1024], F32)
    xr_s = scratch("xr_s", [128, 64 * 1024], BF16)
    wg_s = scratch("wg_s", [128, 8 * 1024], BF16)

    with ExitStack() as ctx:
        def sbt(name, shape, dtype):
            return ctx.enter_context(nc.sbuf_tensor(name, shape, dtype))

        BIGW = max(HCH * T, 2 * NIB * 256, HCH * D)
        big8 = sbt("big8", [128, BIGW], BF16)
        wstream = sbt("wstream", [128, HCH * FC], BF16)
        xtile = sbt("xtile", [128, D], F32)
        S = sbt("S", [128, D], F32)
        lamcr = sbt("lamcr", [128, D], F32)
        dqf = sbt("dqf", [128, D], BF16)
        dmaskT = sbt("dmaskT", [128, D], BF16)
        dkh = sbt("dkh", [128, H], F32)
        dscr = sbt("dscr", [128, D], F32)
        scT = sbt("scT", [128, D], BF16)
        attnsb = sbt("attnsb", [128, D], BF16)
        qp = sbt("qp", [128, D], BF16)
        Sbf = sbt("Sbf", [128, D], BF16)
        kc = sbt("kc", [128, D], BF16)
        vc = sbt("vc", [128, D], BF16)
        kpc = sbt("kpc", [128, D], BF16)
        kfm = sbt("kfm", [128, D], BF16)
        qch = sbt("qch", [128, D], BF16)
        rowpool = sbt("rowpool", [1, 4 * D], BF16)
        rowAB = nc.alloc_sbuf_tensor_at(
            "rowAB", [1, 2 * D], F32, offset=nc.lookup_mloc(rowpool).addr)
        onesf = sbt("onesf", [1, 128], F32)
        onesb = sbt("onesb", [1, FC], BF16)
        slp = sbt("slp", [1, H], F32)
        l16 = sbt("l16", [1, H], F32)
        t1 = sbt("t1", [128, D], F32)
        t1b = sbt("t1b", [128, D], F32)
        ap8 = sbt("ap8", [128, D], F32)
        stg = sbt("stg", [128, 2048], BF16)
        sg = sbt("sg", [128, 2 * FC], BF16)
        uu = sbt("uu", [128, 2 * FC], BF16)
        st = sbt("st", [128, 10 * 2 * NCH], F32)
        epsc = sbt("epsc", [128, 1], F32)
        mcol = sbt("mcol", [128, 1], F32)
        gssb = sbt("gssb", [128, NIB], F32)
        ussb = sbt("ussb", [128, NIB], F32)
        identb = sbt("identb", [128, 128], BF16)

        PAt = ctx.enter_context(nc.psum_tensor("PA", [128, 1024], F32))
        PBt = ctx.enter_context(nc.psum_tensor("PB", [128, 1024], F32))
        PCt = ctx.enter_context(nc.psum_tensor("PC", [128, 1024], F32))
        PT_guard = nc.psum_tensor("PT", [128, 2048], BF16)
        PTt = PT_guard.__enter__()
        PT = PTt[:]

        sems = {}
        for s in ("pe", "dve", "act"):
            sems[s] = ctx.enter_context(nc.semaphore("sem_" + s))
        for s in ("syncq", "gq", "scq"):
            sems[s] = [ctx.enter_context(nc.semaphore(f"sem_{s}_{l}"))
                       for l in range(Planner.NLANES)]
        block = ctx.enter_context(nc.Block())
        P = Planner()

        B = {}
        for nm in ("P0B", "wq0", "wq1", "wq2", "wq3",
                   "xt0", "xt1", "st0", "st1", "qch", "kpc", "kfm", "qp",
                   "Sbf", "scT", "attnsb", "S", "t1", "ap8L", "ap8R", "stg", "stgA", "stgB",
                   "kc", "vc", "sg0", "sg1", "uu0", "uu1",
                   "PA0", "PA1", "PB0", "PB1", "PC0", "PC1", "PD0", "PD1", "t1bB",
                   "xn_sB", "ap_sB", "hsi_sB",
                   "wd_sB", "outB", "g_bfB", "u_bfB"):
            B[nm] = Buf(nm)
        NRG = NCH // 2
        QKV_RG = {nm: [Buf(f"{nm}_{g}") for g in range(NRG)]
                  for nm in ("q_sB", "k_sB", "v_sB")}
        BIG8T = [Buf(f"b8_{i}") for i in range(max(NCH, 16))]
        BIG8 = tuple(BIG8T)
        WS_ALL = (B["wq0"], B["wq1"], B["wq2"], B["wq3"])
        AP8 = (B["ap8L"], B["ap8R"])
        PA_ALL = (B["PA0"], B["PA1"])
        PB_ALL = (B["PB0"], B["PB1"])
        PC_ALL = (B["PC0"], B["PC1"])
        PD_ALL = (B["PD0"], B["PD1"])
        C = B["P0B"]

        big8v = big8[:, 0:HCH * T].rearrange("p (c t) -> p c t", t=T)
        wsv = wstream[:].rearrange("p (c f) -> p c f", f=FC)
        rowA = rowAB[:, 0:D]
        rowB = rowAB[:, D:2 * D]
        bqkv_bf = rowpool[:, 0:3 * D]
        bout_bf = rowpool[:, 3 * D:4 * D]
        xt_ap = (xtile[:], ap8[:])
        xt_buf = ((B["xt0"],), AP8)
        kc2vc2 = nc.alloc_sbuf_tensor_at(
            "kc2vc2", [128, 2 * D], BF16, offset=nc.lookup_mloc(ap8).addr)
        kc2 = kc2vc2[:, 0:D]
        vc2 = kc2vc2[:, D:2 * D]

        PHASE = P.set_phase

        PHASE("P0")
        P.dma("syncq", lambda: nc.sync.dma_start(out=slp[:], in_=slope_sq),
              writes=[C])
        P.dma("syncq", lambda: nc.sync.dma_start(out=dscr[0:1, :], in_=dsc),
              writes=[C])
        P.dma("syncq", lambda: nc.sync.dma_start(out=xtile[0:128, 0:128],
                                                 in_=diffcT),
              writes=[C, B["xt0"]])
        P.dma("syncq", lambda: nc.sync.dma_start(out=ap8[0:128, 0:128],
                                                 in_=causT),
              writes=[C, *AP8])
        P.dma("syncq", lambda: nc.sync.dma_start(out=rowA[:, 0:128], in_=poskr),
              writes=[C])
        P.dma("syncq", lambda: nc.sync.dma_start(out=rowB[:, 0:D], in_=posp1t),
              writes=[C])
        P.dma("syncq", lambda: nc.sync.dma_start(out=gssb[:], in_=gs), writes=[C])
        P.dma("syncq", lambda: nc.sync.dma_start(out=ussb[:], in_=us), writes=[C])
        P.dma("syncq", lambda: nc.sync.dma_start(out=mcol[:], in_=maskc),
              writes=[C])
        P.dma("gq", lambda: nc.gpsimd.dma_start(out=identb[:], in_=identf),
              writes=[C])
        P.op("dve", lambda: nc.vector.memset(onesf[:], 1.0), reads=[C], writes=[C])
        P.op("dve", lambda: nc.vector.memset(onesb[:], 1.0), reads=[C], writes=[C])
        P.op("dve", lambda: nc.vector.memset(S[:], 0.0), writes=[B["S"]])
        P.op("dve", lambda: nc.vector.memset(epsc[:], float(EPS)), writes=[C])
        P.op("act", lambda: nc.scalar.activation(slp[:], slp[:], AF.Sqrt),
             reads=[C], writes=[C])
        P.op("pe", lambda: nc.tensor.matmul(PCt[:, 0:H], rowA[:, 0:128], slp[:],
                                            start=True, stop=True),
             reads=[C], writes=[B["PC0"], C])
        P.op("act", lambda: nc.scalar.activation(dkh[:], PCt[:, 0:H], AF.Exp,
                                                 scale=-1.0),
             reads=[C], writes=[C])
        P.op("act", lambda: nc.scalar.activation(l16[:], slp[:], AF.Exp,
                                                 scale=-float(CH)),
             reads=[C], writes=[C])
        P.op("dve", lambda: nc.vector.tensor_copy(
            rowA[:, 0:D].rearrange("o (h c) -> o h c", h=H),
            slp[:].unsqueeze(2).broadcast_to([1, H, CH])),
            reads=[C], writes=[C])
        for i in range(D // FC):
            P.op("pe", lambda i=i: nc.tensor.matmul(
                PAt[:, 0:FC], onesf[:], rowA[:, i * FC:(i + 1) * FC],
                start=True, stop=True), reads=[C], writes=[B["PA0"], C])
            P.op("dve", lambda i=i: nc.vector.tensor_copy(
                t1[:, i * FC:(i + 1) * FC], PAt[:, 0:FC]),
                reads=[C], writes=[B["t1"], C])
        P.op("dve", lambda: nc.vector.tensor_tensor(
            t1[:].rearrange("p (h c) -> p h c", h=H),
            xtile[0:128, 0:128].unsqueeze(1).broadcast_to([128, H, CH]),
            t1[:].rearrange("p (h c) -> p h c", h=H), ALU.mult),
            reads=[C, B["t1"], B["xt0"]], writes=[B["t1"], C])
        P.op("act", lambda: nc.scalar.activation(t1[:], t1[:], AF.Exp,
                                                 scale=-1.0),
             reads=[C, B["t1"]], writes=[B["t1"], C])
        P.op("dve", lambda: nc.vector.tensor_tensor(
            dmaskT[:].rearrange("p (h c) -> p h c", h=H),
            t1[:].rearrange("p (h c) -> p h c", h=H),
            ap8[0:128, 0:128].unsqueeze(1).broadcast_to([128, H, CH]),
            ALU.mult), reads=[C, B["t1"], *AP8], writes=[C])
        P.op("dve", lambda: nc.vector.tensor_tensor(
            rowB[:, 0:D], rowB[:, 0:D], rowA[:, 0:D], ALU.mult),
            reads=[C], writes=[C])
        P.op("act", lambda: nc.scalar.activation(rowB[:, 0:D], rowB[:, 0:D],
                                                 AF.Exp, scale=-1.0),
             reads=[C], writes=[C])
        for i in range(D // FC):
            P.op("pe", lambda i=i: nc.tensor.matmul(
                PAt[:, 0:FC], onesf[:], rowB[:, i * FC:(i + 1) * FC],
                start=True, stop=True), reads=[C], writes=[B["PA0"], C])
            P.op("dve", lambda i=i: nc.vector.tensor_copy(
                dqf[:, i * FC:(i + 1) * FC], PAt[:, 0:FC]),
                reads=[C], writes=[C])
        P.op("dve", lambda: nc.vector.tensor_copy(
            rowA[:, 0:D].rearrange("o (h c) -> o h c", h=H),
            l16[:].unsqueeze(2).broadcast_to([1, H, CH])),
            reads=[C], writes=[C])
        for i in range(D // FC):
            P.op("pe", lambda i=i: nc.tensor.matmul(
                PAt[:, 0:FC], onesf[:], rowA[:, i * FC:(i + 1) * FC],
                start=True, stop=True), reads=[C], writes=[B["PA0"], C])
            P.op("dve", lambda i=i: nc.vector.tensor_copy(
                lamcr[:, i * FC:(i + 1) * FC], PAt[:, 0:FC]),
                reads=[C], writes=[C])
        P.op("dve", lambda: nc.vector.tensor_copy(rowA[:, 0:D], dscr[0:1, :]),
             reads=[C], writes=[C])
        for i in range(D // FC):
            P.op("pe", lambda i=i: nc.tensor.matmul(
                PAt[:, 0:FC], onesf[:], rowA[:, i * FC:(i + 1) * FC],
                start=True, stop=True), reads=[C], writes=[B["PA0"], C])
            P.op("dve", lambda i=i: nc.vector.tensor_copy(
                dscr[:, i * FC:(i + 1) * FC], PAt[:, 0:FC]),
                reads=[C], writes=[C])
        P.dma("gq", lambda: nc.gpsimd.dma_start(out=bqkv_bf, in_=bqkv),
              reads=[C], writes=[C])
        P.dma("gq", lambda: nc.gpsimd.dma_start(out=bout_bf, in_=bout),
              reads=[C], writes=[C])

        def emit_ln_phase(src_dram, slot0):
            STB = [B["st0"], B["st1"]]
            nparts = (D + 1023) // 1024
            dump = (PCt, PCt)
            dumpb = (PC_ALL, PC_ALL)
            for tc in range(NCH):
                par = tc % 2
                xa = xt_ap[par]
                slot = slot0 + tc
                sa = st[:, slot * 10:slot * 10 + 10]
                P.dma("syncq", lambda tc=tc, xa=xa: nc.sync.dma_start(
                    out=xa, in_=src_dram[tc * 128:(tc + 1) * 128, :]),
                    writes=list(xt_buf[par]))
                P.op("dve", lambda xa=xa, sa=sa: nc.vector.reduce_sum(
                    sa[:, 0:1], xa, axis=AX.X),
                    reads=list(xt_buf[par]), writes=STB)
                for ip in range(nparts):
                    w = min(1024, D - ip * 1024)
                    P.op("act", lambda ip=ip, w=w, xa=xa, sa=sa:
                         nc.scalar.activation(
                             dump[ip][:, 0:w], xa[:, ip * 1024:ip * 1024 + w],
                             AF.Square, accum_out=sa[:, 2 + ip:3 + ip]),
                         reads=list(xt_buf[par]), writes=[*dumpb[ip], *STB],
                         sync_self=(ip > 0))
            def sv(col):
                return st[:, slot0 * 10 + col: (slot0 + NCH) * 10: 10]
            if nparts == 2:
                P.op("dve", lambda: nc.vector.tensor_tensor(
                    sv(4), sv(2), sv(3), ALU.add), reads=STB, writes=STB)
            else:
                P.op("dve", lambda: nc.vector.tensor_copy(sv(4), sv(2)),
                     reads=STB, writes=STB)
            P.op("dve", lambda: nc.vector.tensor_scalar(
                sv(1), sv(0), 1.0 / D, None, ALU.mult), reads=STB, writes=STB)
            P.op("dve", lambda: nc.vector.tensor_tensor(
                sv(5), sv(1), sv(1), ALU.mult), reads=STB, writes=STB)
            P.op("dve", lambda: nc.vector.tensor_scalar(
                sv(6), sv(4), 1.0 / D, None, ALU.mult), reads=STB, writes=STB)
            P.op("dve", lambda: nc.vector.tensor_sub(
                sv(7), sv(6), sv(5)), reads=STB, writes=STB)
            P.op("act", lambda: nc.scalar.activation(
                sv(8), sv(7), AF.Sqrt, bias=epsc[:]),
                reads=[C, *STB], writes=STB)
            P.op("dve", lambda: nc.vector.reciprocal(sv(9), sv(8)),
                 reads=STB, writes=STB)
            for tc in range(NCH):
                par = tc % 2
                xa = xt_ap[par]
                slot = slot0 + tc
                sa = st[:, slot * 10:slot * 10 + 10]
                P.dma("syncq", lambda tc=tc, xa=xa: nc.sync.dma_start(
                    out=xa, in_=src_dram[tc * 128:(tc + 1) * 128, :]),
                    writes=list(xt_buf[par]))
                dst = (qch, kpc)[par]
                dstb = (B["qch"], B["kpc"])[par]
                P.op("dve", lambda xa=xa, sa=sa, dst=dst: nc.vector.tensor_scalar(
                    dst[:], xa, sa[:, 1:2], sa[:, 9:10], ALU.subtract, ALU.mult),
                    reads=[*xt_buf[par], *STB], writes=[dstb], sync_self=True)
                for half in range(2 if HCH > 8 else 1):
                    h0 = half * 8
                    h1 = min(HCH, h0 + 8)
                    for hch in range(h0, h1):
                        P.op("pe", lambda hch=hch, dst=dst: nc.tensor.transpose(
                            PT[:, hch * 128:(hch + 1) * 128],
                            dst[:, hch * 128:(hch + 1) * 128], identb[:]),
                            reads=[C, dstb], writes=[PD_ALL[half]])
                    P.op("dve", lambda tc=tc, h0=h0, h1=h1: nc.vector.tensor_copy(
                        big8v[:, h0:h1, tc * 128:(tc + 1) * 128],
                        PT[:, h0 * 128:h1 * 128].rearrange(
                            "p (c t) -> p c t", t=128)),
                        reads=[PD_ALL[half]], writes=[BIG8T[tc]])
                emit_p3b(1)

        def emit_wslab(fcol):
            hh = max(HCH // 2, 1)
            P.dma("gq", lambda fcol=fcol, hh=hh: nc.gpsimd.dma_start(
                out=wstream[:, 0:hh * FC].rearrange("p (c f) -> p c f", f=FC),
                in_=wqkvT.rearrange("(c p) f -> p c f", p=128)[
                    :, 0:hh, fcol * FC:(fcol + 1) * FC]),
                writes=[B["wq0"], B["wq1"]])
            if hh < HCH:
                P.dma("gq", lambda fcol=fcol, hh=hh: nc.gpsimd.dma_start(
                    out=wstream[:, hh * FC:HCH * FC].rearrange(
                        "p (c f) -> p c f", f=FC),
                    in_=wqkvT.rearrange("(c p) f -> p c f", p=128)[
                        :, hh:HCH, fcol * FC:(fcol + 1) * FC]),
                    writes=[B["wq2"], B["wq3"]])

        def ws_bufs(hch):
            return (B["wq0"], B["wq1"]) if hch < max(HCH // 2, 1) \
                else (B["wq2"], B["wq3"])

        def emit_kv_proj(fcols):
            for fcol in fcols:
                emit_wslab(fcol)
                emit_p3b(1)
                if fcol < NQF:
                    dst_s, dstb, foff = q_s, QKV_RG["q_sB"], fcol * FC
                elif fcol < 2 * NQF:
                    dst_s, dstb, foff = k_s, QKV_RG["k_sB"], (fcol - NQF) * FC
                else:
                    dst_s, dstb, foff = v_s, QKV_RG["v_sB"], (fcol - 2 * NQF) * FC
                for tcq in range(NCH // 4):
                    for tci in range(4):
                        tc = tcq * 4 + tci
                        half = tci // 2
                        hb = (B["stgA"], B["stgB"])[half]
                        pab = PA_ALL[tci % 2]
                        pav = PAt[:, (tci % 2) * FC:(tci % 2 + 1) * FC]
                        for hch in range(HCH):
                            P.op("pe", lambda tc=tc, hch=hch, pav=pav:
                                 nc.tensor.matmul(
                                     pav, big8v[:, hch, tc * 128:(tc + 1) * 128],
                                     wsv[:, hch, :],
                                     start=(hch == 0), stop=False),
                                 reads=[BIG8T[tc], *ws_bufs(hch)], writes=[pab])
                        P.op("pe", lambda fcol=fcol, pav=pav: nc.tensor.matmul(
                            pav, onesb[:, 0:128],
                            bqkv_bf[:, fcol * FC:(fcol + 1) * FC],
                            start=False, stop=True),
                            reads=[C], writes=[pab])
                        P.op("act", lambda tci=tci, pav=pav: nc.scalar.copy(
                            stg[:, tci * FC:(tci + 1) * FC], pav),
                            reads=[pab], writes=[hb])
                        if tci % 2 == 1:
                            P.dma("scq", lambda dst_s=dst_s, foff=foff,
                                  tcq=tcq, half=half:
                                  nc.scalar.dma_start(
                                      out=dst_s.rearrange(
                                          "(c p) f -> p c f", p=128)[
                                          :, tcq * 4 + 2 * half:
                                          tcq * 4 + 2 * half + 2,
                                          foff:foff + FC],
                                      in_=stg[:, half * 2 * FC:
                                              (half + 1) * 2 * FC].rearrange(
                                          "p (c f) -> p c f", f=FC)),
                                  reads=[hb], writes=[dstb[tcq * 2 + half]])

        def emit_chunk_loads(j, kdst, kdb, vdst, vdb):
            P.dma("syncq", lambda j=j, kdst=kdst: nc.sync.dma_start(
                out=kdst, in_=k_s[j * 128:(j + 1) * 128, :]),
                reads=[QKV_RG["k_sB"][j // 2]], writes=[kdb])
            P.dma("syncq", lambda j=j, vdst=vdst: nc.sync.dma_start(
                out=vdst, in_=v_s[j * 128:(j + 1) * 128, :]),
                reads=[QKV_RG["v_sB"][j // 2]], writes=[vdb])

        def emit_kdecay(ksrc, kb):
            P.op("dve", lambda ksrc=ksrc: nc.vector.tensor_tensor(
                kpc[:].rearrange("p (h c) -> p h c", h=H),
                ksrc.rearrange("p (h c) -> p h c", h=H),
                dkh[:].unsqueeze(2).broadcast_to([128, H, CH]), ALU.mult),
                reads=[C, kb], writes=[B["kpc"]])

        def emit_A_supdate(vsrc, vb):
            for g in range(NG):
                h0 = g * HPG
                nh = min(HPG, H - h0)
                for i in range(nh):
                    P.op("pe", lambda i=i, h0=h0, vsrc=vsrc: nc.tensor.matmul(
                        PBt[:, i * 128:(i + 1) * 128],
                        kpc[:, (h0 + i) * 128:(h0 + i + 1) * 128],
                        vsrc[:, (h0 + i) * 128:(h0 + i + 1) * 128],
                        start=True, stop=True),
                        reads=[B["kpc"], vb], writes=list(PB_ALL))
                sl = slice(h0 * 128, (h0 + nh) * 128)
                w = nh * 128
                P.op("dve", lambda sl=sl: nc.vector.tensor_tensor(
                    S[:, sl], S[:, sl], lamcr[:, sl], ALU.mult),
                    reads=[C, B["S"]], writes=[B["S"]])
                P.op("dve", lambda sl=sl, w=w: nc.vector.tensor_tensor(
                    S[:, sl], S[:, sl], PBt[:, 0:w], ALU.add),
                    reads=[B["S"], *PB_ALL], writes=[B["S"]])

        def emit_p3b_item(ib):
            PH = P.phase
            P.set_phase("P3b")
            P.dma("gq", lambda ib=ib: nc.gpsimd.dma_start(
                out=attnsb[:, 0:D], in_=dT[ib * 128:(ib + 1) * 128, :]),
                writes=[B["attnsb"]])
            P.op("dve", lambda: nc.vector.tensor_scalar(
                attnsb[:, 0:D], attnsb[:, 0:D], 128.0, None, ALU.subtract),
                reads=[B["attnsb"]], writes=[B["attnsb"]])
            P.op("dve", lambda: nc.vector.tensor_tensor(
                attnsb[:, 0:D], attnsb[:, 0:D], dscr[:, 0:D], ALU.mult),
                reads=[C, B["attnsb"]], writes=[B["attnsb"]])
            P.dma("scq", lambda ib=ib: nc.scalar.dma_start(
                out=wd_s[ib * 128:(ib + 1) * 128, :], in_=attnsb[:, 0:D]),
                reads=[B["attnsb"]], writes=[B["wd_sB"]])
            P.set_phase(PH)
        p3b_next = [0]
        NGU = 16
        gu_next = [0]

        def emit_gu_item(i):
            src, dst, bb = ((gT, g_bf, B["g_bfB"]) if i % 2 == 0
                            else (uT, u_bf, B["u_bfB"]))
            r = (i // 2) * (D // (NGU // 2))
            r1 = r + D // (NGU // 2)
            P.dma("gq", lambda src=src, dst=dst, r=r, r1=r1:
                  nc.gpsimd.dma_start(out=dst[r:r1, :], in_=src[r:r1, :]),
                  writes=[bb])

        def emit_p3b(n):
            while n > 0 and p3b_next[0] < NIB:
                emit_p3b_item(p3b_next[0])
                p3b_next[0] += 1
                n -= 1

        def emit_gu(n):
            pass

        PHASE("P1a")
        emit_ln_phase(xp, 0)
        PHASE("P1b")
        emit_kv_proj(range(NQF, 3 * NQF))
        PHASE("P2")
        emit_ln_phase(xm, NCH)
        P.dma("scq", lambda: nc.scalar.dma_start(out=xn_s, in_=big8[:, 0:HCH * T]),
              reads=list(BIG8), writes=[B["xn_sB"]])

        PHASE("P1c")
        for j in range(NPCH):
            par = j % 2
            kdst, kdb = (kc[:], B["kc"]) if par == 0 else (kc2, B["ap8L"])
            vdst, vdb = (vc[:], B["vc"]) if par == 0 else (vc2, B["ap8R"])
            emit_chunk_loads(j, kdst, kdb, vdst, vdb)
            emit_kdecay(kdst, kdb)
            emit_A_supdate(vdst, vdb)
        P.op("dve", lambda: nc.vector.tensor_scalar(
            S[:], S[:], mcol[:], None, ALU.mult),
            reads=[C, B["S"]], writes=[B["S"]])

        PHASE("P3")
        emit_kv_proj(range(3 * NQF))

        emit_p3b(NIB)

        PHASE("P4")
        P.dma("gq", lambda: nc.gpsimd.dma_start(
            out=big8[:, 0:HCH * D].rearrange("p (c f) -> p c f", f=D),
            in_=woutT.rearrange("(c p) f -> p c f", p=128)),
            writes=list(BIG8))
        wov = big8[:, 0:HCH * D].rearrange("p (c f) -> p c f", f=D)
        qfm_t = nc.alloc_sbuf_tensor_at(
            "qfm", [128, D], BF16, offset=nc.lookup_mloc(t1b).addr)
        qfm = qfm_t[:]
        for j in range(NCH):
            par = j % 2
            kdst, kdb = (kc[:], B["kc"]) if par == 0 else (kc2, B["ap8L"])
            vdst, vdb = (vc[:], B["vc"]) if par == 0 else (vc2, B["ap8R"])
            qdst = qch[:] if par == 0 else stg[:, 0:D]
            qdbs = [B["qch"]] if par == 0 else [B["stgA"], B["stgB"]]
            emit_chunk_loads(j, kdst, kdb, vdst, vdb)
            P.dma("syncq", lambda j=j, qdst=qdst: nc.sync.dma_start(
                out=qdst, in_=q_s[j * 128:(j + 1) * 128, :]),
                reads=[QKV_RG["q_sB"][j // 2]], writes=qdbs)
            emit_kdecay(kdst, kdb)
            for half in range(2 if HCH > 8 else 1):
                h0 = half * 8
                h1 = min(HCH, h0 + 8)
                for hch in range(h0, h1):
                    P.op("pe", lambda hch=hch, kdst=kdst: nc.tensor.transpose(
                        PT[:, hch * 128:(hch + 1) * 128],
                        kdst[:, hch * 128:(hch + 1) * 128], identb[:]),
                        reads=[C, kdb], writes=[PD_ALL[half]])
                P.op("dve", lambda h0=h0, h1=h1: nc.vector.tensor_copy(
                    kfm[:, h0 * 128:h1 * 128], PT[:, h0 * 128:h1 * 128]),
                    reads=[PD_ALL[half]], writes=[B["kfm"]])
            for half in range(2 if HCH > 8 else 1):
                h0 = half * 8
                h1 = min(HCH, h0 + 8)
                for hch in range(h0, h1):
                    P.op("pe", lambda hch=hch, qdst=qdst: nc.tensor.transpose(
                        PT[:, hch * 128:(hch + 1) * 128],
                        qdst[:, hch * 128:(hch + 1) * 128], identb[:]),
                        reads=[C, *qdbs], writes=[PD_ALL[half]])
                P.op("dve", lambda h0=h0, h1=h1: nc.vector.tensor_copy(
                    qfm[:, h0 * 128:h1 * 128], PT[:, h0 * 128:h1 * 128]),
                    reads=[PD_ALL[half]], writes=[B["t1bB"]])
            P.op("dve", lambda: nc.vector.tensor_tensor(
                qp[:].rearrange("p (h c) -> p h c", h=H),
                qfm.rearrange("p (h c) -> p h c", h=H),
                dqf[:].rearrange("p (h c) -> p h c", h=H), ALU.mult),
                reads=[C, B["t1bB"]], writes=[B["qp"]])
            P.op("act", lambda: nc.scalar.copy(Sbf[:], S[:]),
                 reads=[B["S"]], writes=[B["Sbf"]])
            for g in range(NG):
                h0 = g * HPG
                nh = min(HPG, H - h0)
                gsl = slice(h0 * 128, (h0 + nh) * 128)
                gw = nh * 128
                for i in range(nh):
                    h = h0 + i
                    P.op("pe", lambda i=i, h=h: nc.tensor.matmul(
                        PAt[:, i * 128:(i + 1) * 128],
                        kfm[:, h * 128:(h + 1) * 128],
                        qfm[:, h * 128:(h + 1) * 128],
                        start=True, stop=True),
                        reads=[B["kfm"], B["t1bB"]], writes=list(PA_ALL))
                P.op("dve", lambda gsl=gsl, gw=gw: nc.vector.tensor_tensor(
                    scT[:, gsl], PAt[:, 0:gw], dmaskT[:, gsl], ALU.mult),
                    reads=[C, *PA_ALL], writes=[B["scT"]])
                for i in range(nh):
                    h = h0 + i
                    P.op("pe", lambda i=i, h=h: nc.tensor.matmul(
                        PBt[:, i * 128:(i + 1) * 128],
                        Sbf[:, h * 128:(h + 1) * 128],
                        qp[:, h * 128:(h + 1) * 128],
                        start=True, stop=False),
                        reads=[B["Sbf"], B["qp"]], writes=list(PB_ALL))
                    P.op("pe", lambda i=i, h=h, vdst=vdst: nc.tensor.matmul(
                        PBt[:, i * 128:(i + 1) * 128],
                        vdst[:, h * 128:(h + 1) * 128],
                        scT[:, h * 128:(h + 1) * 128],
                        start=False, stop=True),
                        reads=[vdb, B["scT"]], writes=list(PB_ALL))
                P.op("act", lambda gsl=gsl, gw=gw: nc.scalar.copy(
                    attnsb[:, gsl], PBt[:, 0:gw]),
                    reads=list(PB_ALL), writes=[B["attnsb"]])
                for i in range(nh):
                    h = h0 + i
                    P.op("pe", lambda i=i, h=h, vdst=vdst: nc.tensor.matmul(
                        PAt[:, i * 128:(i + 1) * 128],
                        kpc[:, h * 128:(h + 1) * 128],
                        vdst[:, h * 128:(h + 1) * 128],
                        start=True, stop=True),
                        reads=[B["kpc"], vdb], writes=list(PA_ALL))
                P.op("dve", lambda gsl=gsl: nc.vector.tensor_tensor(
                    S[:, gsl], S[:, gsl], lamcr[:, gsl], ALU.mult),
                    reads=[C, B["S"], B["Sbf"]], writes=[B["S"]])
                P.op("dve", lambda gsl=gsl, gw=gw: nc.vector.tensor_tensor(
                    S[:, gsl], S[:, gsl], PAt[:, 0:gw], ALU.add),
                    reads=[B["S"], *PA_ALL], writes=[B["S"]])
            for ph in range((D + 1023) // 1024):
                pw = min(1024, D - ph * 1024)
                pcb = PC_ALL[ph % 2]
                for fsub in range(pw // FC):
                    fo = ph * 1024 + fsub * FC
                    pcv = PCt[:, (ph % 2) * 0 + fsub * FC:fsub * FC + FC] \
                        if False else PCt[:, fsub * FC:(fsub + 1) * FC]
                    for dch in range(HCH):
                        P.op("pe", lambda fsub=fsub, fo=fo, dch=dch:
                             nc.tensor.matmul(
                                 PCt[:, fsub * FC:(fsub + 1) * FC],
                                 attnsb[:, dch * 128:(dch + 1) * 128],
                                 wov[:, dch, fo:fo + FC],
                                 start=(dch == 0), stop=False),
                             reads=[B["attnsb"], *BIG8], writes=list(PC_ALL))
                    P.op("pe", lambda fsub=fsub, fo=fo: nc.tensor.matmul(
                        PCt[:, fsub * FC:(fsub + 1) * FC],
                        onesb[:, 0:128], bout_bf[:, fo:fo + FC],
                        start=False, stop=True),
                        reads=[C], writes=list(PC_ALL))
                P.op("dve", lambda ph=ph, pw=pw: nc.vector.tensor_copy(
                    t1[:, ph * 1024:ph * 1024 + pw], PCt[:, 0:pw]),
                    reads=list(PC_ALL), writes=[B["t1"]])
            P.dma("scq", lambda j=j: nc.scalar.dma_start(
                out=ap_s[j * 128:(j + 1) * 128, :], in_=t1[:]),
                reads=[B["t1"]], writes=[B["ap_sB"]])

        PHASE("P5")
        P.dma("gq", lambda: nc.gpsimd.dma_start(out=big8[:, 0:HCH * T], in_=xn_s),
              reads=[B["xn_sB"]], writes=list(BIG8))
        for ib in range(NIB):
            par = ib % 2
            goff = par * 2 * HCH * 128
            uoff = goff + HCH * 128
            gb = B["wq0"] if par == 0 else B["wq2"]
            ub = B["wq1"] if par == 0 else B["wq3"]
            P.dma("gq", lambda ib=ib, goff=goff: nc.gpsimd.dma_start(
                out=wstream[:, goff:goff + HCH * 128].rearrange(
                    "p (c f) -> p c f", f=128),
                in_=gT.rearrange("(c p) i -> p c i", p=128)[
                    :, :, ib * 128:(ib + 1) * 128]),
                writes=[gb])
            P.dma("gq", lambda ib=ib, uoff=uoff: nc.gpsimd.dma_start(
                out=wstream[:, uoff:uoff + HCH * 128].rearrange(
                    "p (c f) -> p c f", f=128),
                in_=uT.rearrange("(c p) i -> p c i", p=128)[
                    :, :, ib * 128:(ib + 1) * 128]),
                writes=[ub])
            P.op("dve", lambda goff=goff: nc.vector.tensor_scalar(
                wstream[:, goff:goff + 2 * HCH * 128],
                wstream[:, goff:goff + 2 * HCH * 128],
                128.0, None, ALU.subtract),
                reads=[gb, ub], writes=[gb, ub])
            for tb in range(TBS):
                tpar = tb % 2
                pab = PA_ALL[tpar]
                pbb = PB_ALL[tpar]
                pav = PAt[:, tpar * FC:(tpar + 1) * FC]
                pbv = PBt[:, tpar * FC:(tpar + 1) * FC]
                for hch in range(HCH):
                    P.op("pe", lambda hch=hch, tb=tb, goff=goff, pav=pav:
                         nc.tensor.matmul(
                             pav, wstream[:, goff + hch * 128:
                                          goff + (hch + 1) * 128],
                             big8v[:, hch, tb * FC:(tb + 1) * FC],
                             start=(hch == 0), stop=(hch == HCH - 1)),
                         reads=[gb, *BIG8], writes=[pab])
                for hch in range(HCH):
                    P.op("pe", lambda hch=hch, tb=tb, uoff=uoff, pbv=pbv:
                         nc.tensor.matmul(
                             pbv, wstream[:, uoff + hch * 128:
                                          uoff + (hch + 1) * 128],
                             big8v[:, hch, tb * FC:(tb + 1) * FC],
                             start=(hch == 0), stop=(hch == HCH - 1)),
                         reads=[ub, *BIG8], writes=[pbb])
                sgb = (B["sg0"], B["sg1"])[tpar]
                uub = (B["uu0"], B["uu1"])[tpar]
                P.op("act", lambda ib=ib, tpar=tpar, pav=pav:
                     nc.scalar.activation(
                         sg[:, tpar * FC:(tpar + 1) * FC], pav, AF.Silu,
                         scale=gssb[:, ib:ib + 1]),
                     reads=[C, pab], writes=[sgb])
                P.op("act", lambda ib=ib, tpar=tpar, pbv=pbv:
                     nc.scalar.activation(
                         uu[:, tpar * FC:(tpar + 1) * FC], pbv, AF.Copy,
                         scale=ussb[:, ib:ib + 1]),
                     reads=[C, pbb], writes=[uub])
                shb = (B["stgA"], B["stgB"])[tb // 2]
                P.op("dve", lambda tb=tb, tpar=tpar: nc.vector.tensor_tensor(
                    stg[:, tb * FC:(tb + 1) * FC],
                    sg[:, tpar * FC:(tpar + 1) * FC],
                    uu[:, tpar * FC:(tpar + 1) * FC],
                    ALU.mult), reads=[sgb, uub], writes=[shb])
                P.dma("scq", lambda ib=ib, tb=tb: nc.scalar.dma_start(
                    out=hsi_s[ib * 128:(ib + 1) * 128, tb * FC:(tb + 1) * FC],
                    in_=stg[:, tb * FC:(tb + 1) * FC]),
                    reads=[shb], writes=[B["hsi_sB"]])

        PHASE("P6")
        PT_guard.__exit__(None, None, None)
        PDt = ctx.enter_context(nc.psum_tensor("PD", [128, 1024], F32))
        PSUMS = [(PAt, PA_ALL), (PBt, PB_ALL), (PCt, PC_ALL), (PDt, PD_ALL)]

        def emit_slab(tp):
            spar = tp % 2
            soff = spar * NIB * 256
            slb = BIG8T[spar * 8:spar * 8 + 8]
            P.dma("syncq", lambda tp=tp, soff=soff: nc.sync.dma_start(
                out=big8[:, soff:soff + NIB * 256].rearrange(
                    "p (b t) -> p b t", t=256),
                in_=hsi_s.rearrange("(b p) t -> p b t", p=128)[
                    :, :, tp * 256:(tp + 1) * 256]),
                reads=[B["hsi_sB"]], writes=slb)

        emit_slab(0)
        for tp in range(NP6):
            PHASE("P6")
            spar = tp % 2
            soff = spar * NIB * 256
            slab = big8[:, soff:soff + NIB * 256].rearrange(
                "p (b t) -> p b t", t=256)
            slbufs = BIG8T[spar * 8:spar * 8 + 8]
            for ib in range(NIB):
                wpar = ib % 4
                wdst = (vc[:], kc[:], kpc[:], kfm[:])[wpar]
                wdb = (B["vc"], B["kc"], B["kpc"], B["kfm"])[wpar]
                wq = ("gq", "syncq")[ib % 2]
                dmafn = (nc.gpsimd, nc.sync)[ib % 2]
                P.dma(wq, lambda ib=ib, wdst=wdst, dmafn=dmafn: dmafn.dma_start(
                    out=wdst, in_=wd_s[ib * 128:(ib + 1) * 128, :]),
                    reads=[B["wd_sB"]], writes=[wdb])
                for tcs in range(2):
                    for fq in range(D // FC):
                        pi = tcs * 2 + fq // 2 if D > 1024 else tcs
                        pst, psb = PSUMS[pi]
                        po = (fq % 2) * FC if D > 1024 else fq * FC
                        P.op("pe", lambda ib=ib, tcs=tcs, fq=fq, pst=pst,
                             po=po, wdst=wdst, slab=slab: nc.tensor.matmul(
                                 pst[:, po:po + FC],
                                 slab[:, ib, tcs * 128:(tcs + 1) * 128],
                                 wdst[:, fq * FC:(fq + 1) * FC],
                                 start=(ib == 0), stop=(ib == NIB - 1)),
                             reads=slbufs + [wdb], writes=list(psb))
            if tp + 1 < NP6:
                emit_slab(tp + 1)
            PHASE("P6m")
            for tcs in range(2):
                j = tp * 2 + tcs
                ev = (t1, t1b)[tcs]
                evb = (B["t1"], B["t1bB"])[tcs]
                P.dma("syncq", lambda j=j: nc.sync.dma_start(
                    out=ap8[:], in_=ap_s[j * 128:(j + 1) * 128, :]),
                    reads=[B["ap_sB"]], writes=list(AP8))
                P.dma("syncq", lambda j=j: nc.sync.dma_start(
                    out=xtile[:], in_=xm[j * 128:(j + 1) * 128, :]),
                    writes=[B["xt0"]])
                nhal = (D + 1023) // 1024
                for ph in range(nhal):
                    pw = min(1024, D - ph * 1024)
                    pi = tcs * 2 + ph if D > 1024 else tcs
                    pst, psb = PSUMS[pi]
                    P.op("dve", lambda ph=ph, pw=pw, pst=pst, ev=ev:
                         nc.vector.tensor_copy(
                             ev[:, ph * 1024:ph * 1024 + pw], pst[:, 0:pw]),
                         reads=[*psb], writes=[evb])
                P.op("dve", lambda ev=ev: nc.vector.tensor_tensor(
                    ev[:], ev[:], ap8[:], ALU.add),
                    reads=[evb, *AP8], writes=[evb], sync_self=True)
                P.op("dve", lambda ev=ev: nc.vector.tensor_tensor(
                    ev[:], ev[:], xtile[:], ALU.add),
                    reads=[evb, B["xt0"]], writes=[evb])
                P.dma("scq", lambda j=j, ev=ev: nc.scalar.dma_start(
                    out=out[j * 128:(j + 1) * 128, :], in_=ev[:]),
                    reads=[evb], writes=[B["outB"]])


        P.finalize(nc, block, sems)
        rem = nc.sbuf_bytes_remaining
        print(f"[build] D={D} I={I} T={T}: {len(P.entries)} ops, "
              f"sbuf_remaining={rem}B/part")
    return nc


_CACHE = {}


def _get_program(D, I, T, TP, debug_taps=False):
    key = (D, I, T, TP, debug_taps)
    if key not in _CACHE:
        _CACHE[key] = build_program(D, I, T, TP, debug_taps)
    return _CACHE[key]


def make_const_inputs(D, H):
    pos = np.arange(CH, dtype=np.float32)
    return dict(
        slope_sq=np.ldexp(np.ones(H, np.float32), -(np.arange(H) + 1)).reshape(1, H),
        posp1t=np.tile(pos + 1.0, H).reshape(1, H * CH).astype(np.float32),
        poskr=(CH - 1.0 - pos).reshape(1, CH).astype(np.float32),
        diffcT=np.maximum(pos[None, :] - pos[:, None], 0.0).astype(np.float32),
        causT=(pos[None, :] >= pos[:, None]).astype(np.float32),
        identf=np.eye(128, dtype=np.float32),
    )


def make_core_inputs(x_full, b, h, weights, D, H, I, T, TP):
    d = dict(weights)
    d.update(make_const_inputs(D, H))
    d["xm"] = np.ascontiguousarray(x_full[b, h * T:(h + 1) * T, :])
    d["xp"] = np.ascontiguousarray(x_full[b, 0:TP, :])
    d["maskc"] = np.full((128, 1), 1.0 if h == 1 else 0.0, np.float32)
    return d


def prep_weights(w_qkv, b_qkv, w_out, b_out, gate_w, gate_scale, up_w, up_scale,
                 down_w, down_scale, D, I):
    NIB = I // 128
    return dict(
        wqkvT=np.ascontiguousarray(w_qkv.T.astype(np.float32, copy=False)),
        bqkv=np.ascontiguousarray(b_qkv.reshape(1, -1).astype(np.float32, copy=False)),
        woutT=np.ascontiguousarray(w_out.T.astype(np.float32, copy=False)),
        bout=np.ascontiguousarray(b_out.reshape(1, -1).astype(np.float32, copy=False)),
        gT=np.ascontiguousarray(np.asarray(gate_w).T.astype(np.int32, copy=False)),
        uT=np.ascontiguousarray(np.asarray(up_w).T.astype(np.int32, copy=False)),
        dT=np.ascontiguousarray(np.asarray(down_w).T.astype(np.int32, copy=False)),
        gs=np.ascontiguousarray(
            np.asarray(gate_scale, np.float32).reshape(NIB, 128).T),
        us=np.ascontiguousarray(
            np.asarray(up_scale, np.float32).reshape(NIB, 128).T),
        dsc=np.ascontiguousarray(
            np.asarray(down_scale, np.float32).reshape(1, -1)),
    )


def kernel(L0_attn_x, L0_attn_w_qkv, L0_attn_b_qkv, L0_attn_w_out, L0_attn_b_out,
           L0_ffn_gate_w, L0_ffn_gate_scale, L0_ffn_up_w, L0_ffn_up_scale,
           L0_ffn_down_w, L0_ffn_down_scale):
    x = np.asarray(L0_attn_x, dtype=np.float32)
    B, N, D = x.shape
    H = D // CH
    I = np.asarray(L0_ffn_gate_w).shape[0]
    T = N // 2
    TP = T
    weights = prep_weights(
        np.asarray(L0_attn_w_qkv, np.float32), np.asarray(L0_attn_b_qkv, np.float32),
        np.asarray(L0_attn_w_out, np.float32), np.asarray(L0_attn_b_out, np.float32),
        L0_ffn_gate_w, L0_ffn_gate_scale, L0_ffn_up_w, L0_ffn_up_scale,
        L0_ffn_down_w, L0_ffn_down_scale, D, I)

    nc = _get_program(D, I, T, TP)
    in_maps = []
    for c in range(8):
        b, h = c // 2, c % 2
        in_maps.append(make_core_inputs(x, b, h, weights, D, H, I, T, TP))
    res = run_bass_kernel_spmd(nc, in_maps, core_ids=list(range(8)))
    outp = np.empty((B, N, D), dtype=np.float32)
    for c in range(8):
        b, h = c // 2, c % 2
        outp[b, h * T:(h + 1) * T, :] = np.asarray(res.results[c]["out"])
    return outp


# version 18
# speedup vs baseline: 1.1884x; 1.0746x over previous
import os
import numpy as np
import ml_dtypes
from contextlib import ExitStack

import concourse.bass as bass
import concourse.mybir as mybir
from concourse.bass_utils import run_bass_kernel_spmd

F32 = mybir.dt.float32
BF16 = mybir.dt.bfloat16
I32 = mybir.dt.int32
AF = mybir.ActivationFunctionType
ALU = mybir.AluOpType
AX = mybir.AxisListType
BF = ml_dtypes.bfloat16

CH = 128
FC = 512
EPS = 1e-5

_LAMN_U32 = np.array([
    0x3f800000,0x3efc7380,0x3e78f398,0x3df5801c,0x3d7218df,0x3ceebdb5,0x3c6b6e75,0x3be82af4,
    0x3b64f307,0x3ae1c687,0x3a5ea54a,0x39db8f28,0x395883fa,0x38d5839a,0x38528de1,0x37cfa2a8,
    0x374cc1cb,0x36c9eb26,0x36471e93,0x35c45bef,0x3541a316,0x34bef3e7,0x343c4e3f,0x33b9b1fb,
    0x33371efb,0x32b4951d,0x32321442,0x31af9c49,0x312d2d13,0x30aac681,0x30286873,0x2fa612cc,
    0x2f23c56f,0x2ea1803d,0x2e1f4319,0x2d9d0de7,0x2d1ae08c,0x2c98baea,0x2c169ce6,0x2b948667,
    0x2b127750,0x2a906f87,0x2a0e6ef3,0x298c757b,0x290a8304,0x28889776,0x2806b2b8,0x2784d4b3,
    0x2702fd4e,0x26812c72,0x25fec410,0x257b3bf1,0x24f7c05b,0x24745121,0x23f0ee17,0x236d9712,
    0x22ea4be7,0x22670c6d,0x21e3d879,0x2160afe4,0x20dd9284,0x205a8031,0x1fd778c5,0x1f547c19,
    0x1ed18a07,0x1e4ea268,0x1dcbc519,0x1d48f1f4,0x1cc628d5,0x1c436999,0x1bc0b41d,0x1b3e083e,
    0x1abb65da,0x1a38cccf,0x19b63cfc,0x1933b641,0x18b1387c,0x182ec38f,0x17ac575a,0x1729f3be,
    0x16a7989d,0x162545d8,0x15a2fb51,0x1520b8ec,0x149e7e8c,0x141c4c14,0x139a2168,0x1317fe6d,
    0x1295e306,0x1213cf1a,0x1191c28e,0x110fbd47,0x108dbf2b,0x100bc822,0x0f89d813,0x0f07eee3,
    0x0e860c7c,0x0e0430c5,0x0d825ba6,0x0d008d07,0x0c7d89a6,0x0bfa05e3,0x0b768e99,0x0af3239c,
    0x0a6fc4c0,0x09ec71da,0x09692ac0,0x08e5ef48,0x0862bf49,0x07df9a99,0x075c8111,0x06d97289,
    0x06566eda,0x05d375dd,0x0550876d,0x04cda365,0x044ac99e,0x03c7f9f6,0x03453447,0x02c2786f,
    0x023fc64b,0x01bd1db8,0x013a7e94,0x00b7e8be,0x00000000,0x00000000,0x00000000,0x00000000,
    0x00000000,0x3f800000,0x3f1b4598,0x3ebc5ab2,0x3e647c3c,0x3e0a9556,0x3da81c2f,0x3d4bed87,
    0x3cf76082,0x3c960aaf,0x3c360283,0x3bdcca01,0x3b85ea54,0x3b227291,0x3ac50f0e,0x3a6f0b5f,
    0x3a10fcdf,0x39afe10a,0x39555a22,0x39016793,0x389cf9c7,0x383e6bd0,0x37e6fe17,0x378c1aa3,
    0x3729f46f,0x36ce2a65,0x367a176e,0x3617b02d,0x35b801cf,0x355f363c,0x35076285,0x34a43ae9,
    0x344738a0,0x33f1aae3,0x3392942a,0x3331cf1d,0x32d7b17a,0x3282d318,0x321eb2c0,0x31c082bd,
    0x316986fc,0x310da437,0x30abd1dd,0x30506d8d,0x2ffcd5fb,0x2f995a4b,0x2f3a06b7,0x2ee1a947,
    0x2e88ded6,0x2e260842,0x2dc96846,0x2d7451c6,0x2d142fee,0x2cb3c29c,0x2c5a0f65,0x2c04429a,
    0x2ba0708f,0x2b429f89,0x2aec1701,0x2a8f321c,0x2a2db489,0x29d2b70f,0x297f9c3d,0x291b0916,
    0x28bc114c,0x28642332,0x280a5f54,0x27a7daac,0x274b9e0f,0x26f7001c,0x2695d036,0x2635bb96,
    0x25dc73f7,0x2585b624,0x25223343,0x24c4c243,0x246eae38,0x2410c45f,0x23af9c80,0x235506fe,
    0x23013525,0x229cbc9b,0x223e219c,0x21e6a413,0x218be40a,0x2129b234,0x20cdda0e,0x2079b5f9,
    0x20177510,0x1fb7ba1b,0x1f5edf41,0x1f072dc3,0x1ea3fae9,0x1e46eafe,0x1df14cb6,0x1d925b0b,
    0x1d3189d3,0x1cd75d6c,0x1c82a01d,0x1c1e74e8,0x1bc037b8,0x1b692bfb,0x1b0d6d04,0x1aab8ee8,
    0x1a501c54,0x19fc7374,0x19991e88,0x1939be39,0x18e15156,0x1888a980,0x1825c78f,0x17c919c9,
    0x1773f290,0x1713f62f,0x16b37c8f,0x1659ba6b,0x16040f10,0x15a0320a,0x154253b1,0x14ebbb00,
    0x148efa4e,0x142d70d8,0x13d264f2,0x137f38a1,0x131accac,0x12bbc802,0x1263ca4b,0x120a2968,
    0x11a79943,0x114b4eb6,0x3f800000,0x3f33c2a0,0x3efc737f,0x3eb144af,0x3e78f397,0x3e2ecf97,
    0x3df5801a,0x3dac6337,0x3d7218dc,0x3d29ff70,0x3ceebdb3,0x3ca7a425,0x3c6b6e72,0x3c255136,
    0x3be82af0,0x3ba30687,0x3b64f303,0x3b20c3fa,0x3ae1c682,0x3a9e8972,0x3a5ea545,0x3a1c56d3,
    0x39db8f23,0x399a2c01,0x395883f5,0x391808df,0x38d58394,0x3895ed54,0x38528dda,0x3813d942,
    0x37cfa2a1,0x3791cc92,0x374cc1c4,0x370fc727,0x36c9eb1e,0x368dc8e8,0x36471e8a,0x360bd1bc,
    0x35c45be6,0x3589e18a,0x3541a30e,0x3507f839,0x34bef3de,0x348615b0,0x343c4e35,0x340439d8,
    0x33b9b1f1,0x33826499,0x33371ef1,0x330095da,0x32b49513,0x327d9b0c,0x32321438,0x31fa170b,
    0x31af9c3e,0x31769f84,0x312d2d08,0x30f3344a,0x30aac675,0x306fd532,0x30286868,0x2fec8212,
    0x2fa612c1,0x2f693abe,0x2f23c563,0x2ee5ff0c,0x2ea18031,0x2e62ced4,0x2e1f430d,0x2ddfa9ed,
    0x2d9d0ddb,0x2d5c902e,0x2d1ae07f,0x2cd9816f,0x2c98badd,0x2c567d8b,0x2c169cd9,0x2bd3845a,
    0x2b948659,0x2b5095b6,0x2b127742,0x2acdb17b,0x2a906f7a,0x2a4ad782,0x2a0e6ee6,0x29c807a8,
    0x298c756d,0x294541c8,0x290a82f6,0x28c285c0,0x28889768,0x283fd36b,0x2806b2aa,0x27bd2aa9,
    0x2784d4a5,0x273a8b57,0x2702fd40,0x26b7f553,0x26812c64,0x2635687d,0x25fec3f3,0x25b2e4b3,
    0x257b3bd4,0x253069d6,0x24f7c03e,0x24adf7c6,0x24745104,0x242b8e64,0x23f0edfa,0x23a92d90,
    0x236d96f4,0x2326d52e,0x22ea4bca,0x22a4851e,0x22670c4f,0x22223d43,0x21e3d85c,0x219ffd80,
    0x2160afc6,0x211dc5b9,0x20dd9265,0x209b95d0,0x205a8013,0x20196dab,0x1fd778a7,0x1f974d2d,
    0x1f547bfb,0x1f15343b,0x1ed189e8,0x3f800000,0x3f475f7d,0x3f1b4598,0x3ef1da08,0x3ebc5ab2,
    0x3e92b0c2,0x3e647c3c,0x3e31f1cc,0x3e0a9555,0x3dd7db8d,0x3da81c2e,0x3d82ec9d,0x3d4bed87,
    0x3d1ed1b5,0x3cf76081,0x3cc0a84a,0x3c960aae,0x3c69b489,0x3c360283,0x3c0dbfd8,0x3bdcca00,
    0x3babf361,0x3b85ea53,0x3b509634,0x3b227290,0x3afd074c,0x3ac50f0d,0x3a997834,0x3a6f0b5e,
    0x3a3a2b00,0x3a10fcde,0x39e1d54a,0x39afe109,0x3988f989,0x39555a21,0x392628a5,0x39016792,
    0x38c98f8e,0x389cf9c6,0x3874816d,0x383e6bcf,0x38144cd5,0x37e6fe15,0x37b3e5ac,0x378c1aa2,
    0x375a39ed,0x3729f46d,0x37045c66,0x36ce2a64,0x36a08fda,0x367a176d,0x3642c57e,0x3617b02c,
    0x35ec450d,0x35b801ce,0x358f4e09,0x355f363b,0x352dd66a,0x35076284,0x34d2e028,0x34a43ae7,
    0x347fce17,0x3447389f,0x341b2753,0x33f1aae1,0x33bc35f9,0x33929429,0x33644fb0,0x3331cf1b,
    0x330a7a51,0x32d7b178,0x32a7fb68,0x3282d316,0x324bc5c5,0x321eb2be,0x31f73047,0x31c082bb,
    0x3195ed6e,0x316986f9,0x3135df07,0x310da435,0x30dc9ef5,0x30abd1db,0x3085d038,0x30506d8a,
    0x302252e5,0x2ffcd5f8,0x2fc4e8a2,0x2f995a49,0x2f6edcc4,0x2f3a06b5,0x2f10e09a,0x2ee1a943,
    0x2eafbebf,0x2e88ded4,0x2e553089,0x2e260840,0x2e014e58,0x2dc96843,0x2d9cdb2c,0x2d7451c2,
    0x2d3e46b0,0x2d142fec,0x2ce6d10d,0x2cb3c299,0x2c8bff52,0x2c5a0f62,0x2c29d34b,0x2c044298,
    0x2bce0232,0x2ba0708d,0x2b79e6ab,0x2b429f85,0x2b179299,0x2aec16fd,0x2ab7ddee,0x2a8f3219,
    0x2a5f0ab6,0x2a2db486,0x2a07481f,0x29d2b70b,0x29a41ae3,0x297f9c39,0x294711c8,0x291b0913,
    0x28f17bc4,0x28bc1148,0x28927795,0x2864232e,0x3f800000,0x3f568500,0x3f33c2a1,0x3f16a217,
    0x3efc7383,0x3ed38bb7,0x3eb144b3,0x3e948b86,0x3e78f39d,0x3e509cfc,0x3e2ecf9c,0x3e127c5e,
    0x3df58023,0x3dcdb8a9,0x3dac633e,0x3d907485,0x3d7218e8,0x3d4ade99,0x3d29ff7a,0x3d0e73e1,
    0x3ceebdc1,0x3cc80ea8,0x3ca7a430,0x3c8c7a58,0x3c6b6e83,0x3c4548b2,0x3c255144,0x3c0a87d2,
    0x3be82b04,0x3bc28c93,0x3ba30696,0x3b889c34,0x3b64f319,0x3b3fda29,0x3b20c40b,0x3b06b767,
    0x3ae1c69b,0x3abd3152,0x3a9e8985,0x3a84d953,0x3a5ea560,0x3a3a91ea,0x3a1c56e7,0x3a0301df,
    0x39db8f40,0x39b7fbd1,0x399a2c17,0x398130f4,0x39588414,0x39356ee6,0x391808f7,0x38feccf6,
    0x38d583b6,0x38b2eb07,0x3895ed6c,0x387b44ba,0x38528dfe,0x38307016,0x3813d95d,0x37f7c907,
    0x37cfa2c7,0x37adfdf2,0x3791ccad,0x377459b0,0x374cc1ec,0x372b947c,0x370fc744,0x36f0f68a,
    0x36c9eb48,0x36a93395,0x368dc907,0x366d9f6a,0x36471eb6,0x3626db1f,0x360bd1dc,0x35ea5424,
    0x35c45c14,0x35a48afc,0x3589e1ab,0x3567148f,0x3541a33d,0x3522430f,0x3507f85b,0x34e3e080,
    0x34bef40f,0x34a00339,0x348615d4,0x3460b7d0,0x343c4e68,0x341dcb5f,0x340439fc,0x33dd9a56,
    0x33b9b225,0x339b9b65,0x338264be,0x335a87ea,0x33371f26,0x3319732d,0x33009601,0x32d78065,
    0x32b4954a,0x3297529e,0x327d9b5b,0x325483a0,0x32321470,0x3215399a,0x31fa175c,0x31d19175,
    0x31af9c78,0x31932809,0x31769fd7,0x314ea9be,0x312d2d43,0x31111dcf,0x30f3349f,0x30cbcc57,
    0x30aac6b2,0x308f1ad2,0x306fd589,0x3048f91a,0x302868a5,0x300d1ef9,0x2fec826a,0x2fc62fe4,
    0x2fa61300,0x2f8b2a2a,0x2f693b17,0x2f437091,0x2f23c5a3,0x3f800000,0x3f61eb51,0x3f475f7d,
    0x3f2ff230,0x3f1b4597,0x3f0906e4,0x3ef1da06,0x3ed56eef,0x3ebc5ab1,0x3ea638d8,0x3e92b0c1,
    0x3e817430,0x3e647c3a,0x3e49a335,0x3e31f1ca,0x3e1d0914,0x3e0a9554,0x3df49943,0x3dd7db8a,
    0x3dbe7e5f,0x3da81c2c,0x3d945b49,0x3d82ec9a,0x3d671499,0x3d4bed83,0x3d33f734,0x3d1ed1b1,
    0x3d0c284a,0x3cf7607c,0x3cda4f31,0x3cc0a846,0x3caa04fd,0x3c960aaa,0x3c84694b,0x3c69b483,
    0x3c4e3e7a,0x3c36027e,0x3c209f7e,0x3c0dbfd3,0x3bfa2fc9,0x3bdcc9f9,0x3bc2d877,0x3babf35b,
    0x3b97bef2,0x3b85ea4f,0x3b6c5c0f,0x3b50962d,0x3b3813b9,0x3b22728a,0x3b0f5bfe,0x3afd0742,
    0x3adf4bf7,0x3ac50f05,0x3aade757,0x3a99782e,0x3a876fb2,0x3a6f0b54,0x3a52f4b0,0x3a3a2af8,
    0x3a244ae4,0x3a10fcd7,0x39ffe6fe,0x39e1d53f,0x39c74c03,0x39afe100,0x399b366d,0x3988f982,
    0x3971c266,0x39555a16,0x393c484b,0x3926289b,0x3912a26c,0x3901678b,0x38e465e8,0x38c98f83,
    0x38b1e068,0x389cf9bd,0x388a87ca,0x3874815e,0x3857c674,0x383e6bc3,0x38280bc0,0x38144ccc,
    0x3802dfd0,0x37e6fe06,0x37cbd997,0x37b3e59f,0x379ec22e,0x378c1a99,0x37774852,0x375a39de,
    0x37409574,0x3729f461,0x3715fc02,0x37045c5c,0x36e99daf,0x36ce2a54,0x36b5f0b6,0x36a08fce,
    0x368db1fb,0x367a1759,0x365cb468,0x3642c56f,0x362be28f,0x3617b01f,0x3605dd3a,0x35ec44f9,
    0x35d081cd,0x35b801be,0x35a262ac,0x358f4dfd,0x357cee8b,0x355f3627,0x3544fbc6,0x352dd65a,
    0x35196930,0x35076277,0x34eef3fa,0x34d2e014,0x34ba18c8,0x34a43ad8,0x3490eeae,0x347fcdff,
    0x3461bf30,0x3447388b,0x342fcfd2,0x341b2743,0x3408ec20,0x33f1aac9,0x3f800000,0x3f6a5801,
    0x3f568500,0x3f445f51,0x3f33c2a1,0x3f248db3,0x3f16a217,0x3f09e3f1,0x3efc7383,0x3ee7185e,
    0x3ed38bb8,0x3ec1a66f,0x3eb144b3,0x3ea245bc,0x3e948b86,0x3e87fa99,0x3e78f39e,0x3e63e443,
    0x3e509cfd,0x3e3ef736,0x3e2ecf9d,0x3e2005dd,0x3e127c5f,0x3e06180a,0x3df58024,0x3de0bb86,
    0x3dcdb8aa,0x3dbc5183,0x3dac633f,0x3d9dcdfa,0x3d907486,0x3d843c2b,0x3d7218ea,0x3d5d9dff,
    0x3d4ade9a,0x3d39b536,0x3d29ff7b,0x3d1b9df7,0x3d0e73e2,0x3d0266e5,0x3ceebdc3,0x3cda8b86,
    0x3cc80eaa,0x3cb7222d,0x3ca7a431,0x3c9975b6,0x3c8c7a5a,0x3c809821,0x3c6b6e85,0x3c5783f5,
    0x3c4548b4,0x3c349846,0x3c255145,0x3c17551e,0x3c0a87d3,0x3bfd9f8c,0x3be82b06,0x3bd48723,
    0x3bc28c96,0x3bb21762,0x3ba30698,0x3b953c12,0x3b889c36,0x3b7a1b7f,0x3b64f31c,0x3b5194ec,
    0x3b3fda2c,0x3b2f9f60,0x3b20c40d,0x3b132a78,0x3b06b769,0x3af6a3eb,0x3ae1c69e,0x3acead29,
    0x3abd3154,0x3aad3021,0x3a9e8987,0x3a912035,0x3a84d954,0x3a7338a5,0x3a5ea563,0x3a4bcfb6,
    0x3a3a91ed,0x3a2ac985,0x3a1c56ea,0x3a0f1d30,0x3a0301e1,0x39efd981,0x39db8f44,0x39c8fc6e,
    0x39b7fbd4,0x39a86b6f,0x399a2c19,0x398d214f,0x398130f6,0x396c8654,0x39588418,0x3946332c,
    0x39356ee9,0x392615c0,0x391808f9,0x390b2c78,0x38feccfa,0x38e93ef4,0x38d583ba,0x38c373ce,
    0x38b2eb0b,0x38a3c859,0x3895ed6f,0x38893e92,0x387b44bf,0x38660337,0x38528e02,0x3840be30,
    0x3830701a,0x3821831f,0x3813d960,0x38075785,0x37f7c90c,0x37e2d2f3,0x37cfa2cc,0x37be122f,
    0x37adfdf6,0x379f45f3,0x3791ccb1,0x37857738,0x377459b6,0x375fae00,0x374cc1f1,0x3f800000,
    0x3f707d60,0x3f61eb51,0x3f543b41,0x3f475f7d,0x3f3b4b2a,0x3f2ff231,0x3f254939,0x3f1b4599,
    0x3f11dd4a,0x3f0906e5,0x3f00b993,0x3ef1da09,0x3ee332db,0x3ed56ef2,0x3ec8808a,0x3ebc5ab3,
    0x3eb0f147,0x3ea638db,0x3e9c26b5,0x3e92b0c3,0x3e89cd8e,0x3e817433,0x3e7338ab,0x3e647c3f,
    0x3e56a461,0x3e49a33a,0x3e3d6bc6,0x3e31f1cf,0x3e2729d8,0x3e1d0918,0x3e13856f,0x3e0a9558,
    0x3e022fe1,0x3df4994a,0x3de5c780,0x3dd7db91,0x3dcac78f,0x3dbe7e65,0x3db2f3ca,0x3da81c32,
    0x3d9decc3,0x3d945b4f,0x3d8b5e42,0x3d82ec9f,0x3d75fbe8,0x3d6714a2,0x3d591484,0x3d4bed8c,
    0x3d3f9292,0x3d33f73c,0x3d290feb,0x3d1ed1b9,0x3d153265,0x3d0c2850,0x3d03aa70,0x3cf76088,
    0x3ce863a7,0x3cda4f3c,0x3ccd1533,0x3cc0a850,0x3cb4fc25,0x3caa0506,0x3c9fb7fa,0x3c960ab3,
    0x3c8cf383,0x3c846953,0x3c78c72e,0x3c69b491,0x3c5b8bbd,0x3c4e3e87,0x3c41bfa0,0x3c360289,
    0x3c2afb84,0x3c209f89,0x3c16e43b,0x3c0dbfdd,0x3c05294b,0x3bfa2fdb,0x3beb0764,0x3bdcca09,
    0x3bcf698a,0x3bc2d885,0x3bb70a6a,0x3babf368,0x3ba18868,0x3b97befd,0x3b8e8d5f,0x3b85ea59,
    0x3b7b9a93,0x3b6c5c22,0x3b5e0a22,0x3b50963e,0x3b43f302,0x3b3813c9,0x3b2cecb3,0x3b227298,
    0x3b189afd,0x3b0f5c0b,0x3b06ac80,0x3afd0759,0x3aedb2cf,0x3adf4c0b,0x3ad1c4a6,0x3ac50f18,
    0x3ab91ea8,0x3aade767,0x3aa35e1c,0x3a99783c,0x3a902be2,0x3a876fbf,0x3a7e762f,0x3a6f0b6c,
    0x3a608fc7,0x3a52f4c5,0x3a462cc9,0x3a3a2b0b,0x3a2ee387,0x3a244af5,0x3a1a56bc,0x3a10fce7,
    0x3a08341a,0x39ffe71a,0x39f065fc,0x39e1d558,0x39d4269d,0x39c74c19,0x39bb38f3,0x39afe114,
    0x3f800000,0x3f74eec2,0x3f6a5800,0x3f603670,0x3f5684ff,0x3f4d3ed5,0x3f445f4f,0x3f3be1fd,
    0x3f33c2a0,0x3f2bfd27,0x3f248db1,0x3f1d7085,0x3f16a215,0x3f101ef9,0x3f09e3ef,0x3f03edda,
    0x3efc737e,0x3ef18987,0x3ee7185a,0x3edd1abd,0x3ed38bb3,0x3eca6673,0x3ec1a66a,0x3eb94738,
    0x3eb144ae,0x3ea99aca,0x3ea245b7,0x3e9b41cb,0x3e948b82,0x3e8e1f82,0x3e87fa95,0x3e8219a8,
    0x3e78f395,0x3e6e3059,0x3e63e43b,0x3e5a0a13,0x3e509cf4,0x3e479829,0x3e3ef72e,0x3e36b5b2,
    0x3e2ecf95,0x3e2740e4,0x3e2005d6,0x3e191acf,0x3e127c58,0x3e0c2722,0x3e061803,0x3e004bf3,
    0x3df58017,0x3deae30d,0x3de0bb7a,0x3dd70449,0x3dcdb89e,0x3dc4d3d4,0x3dbc5178,0x3db42d49,
    0x3dac6334,0x3da4ef55,0x3d9dcdf0,0x3d96fb76,0x3d90747d,0x3d8a35c1,0x3d843c22,0x3d7d094a,
    0x3d7218d9,0x3d67a179,0x3d5d9def,0x3d540939,0x3d4ade8b,0x3d421951,0x3d39b528,0x3d31addd,
    0x3d29ff6d,0x3d22a601,0x3d1b9dea,0x3d14e3a6,0x3d0e73d7,0x3d084b44,0x3d0266da,0x3cf9874d,
    0x3ceebdae,0x3ce46b73,0x3cda8b73,0x3cd118bd,0x3cc80e97,0x3cbf687d,0x3cb7221c,0x3caf374f,
    0x3ca7a421,0x3ca064c9,0x3c9975a7,0x3c92d343,0x3c8c7a4c,0x3c866794,0x3c809813,0x3c7611c3,
    0x3c6b6e6d,0x3c6140d3,0x3c5783de,0x3c4e32af,0x3c45489e,0x3c3cc136,0x3c349832,0x3c2cc97e,
    0x3c255132,0x3c1e2b93,0x3c17550c,0x3c10ca33,0x3c0a87c3,0x3c048a99,0x3bfd9f6e,0x3bf2a87f,
    0x3be82aea,0x3bde216f,0x3bd48709,0x3bcb56eb,0x3bc28c7d,0x3bba2359,0x3bb2174b,0x3baa644c,
    0x3ba30683,0x3b9bfa40,0x3b953bfe,0x3b8ec85d,0x3b889c23,0x3b82b43a,0x3b7a1b5c,0x3b6f4b57,
    0x3b64f2fc,0x3f800000,0x3f781fab,0x3f707d5f,0x3f691734,0x3f61eb50,0x3f5af7e8,0x3f543b3f,
    0x3f4db3a6,0x3f475f7b,0x3f413d29,0x3f3b4b27,0x3f3587f9,0x3f2ff22e,0x3f2a8860,0x3f254936,
    0x3f20335e,0x3f1b4595,0x3f167e9d,0x3f11dd46,0x3f0d6067,0x3f0906e1,0x3f04cf9d,0x3f00b98e,
    0x3ef9875c,0x3ef1da00,0x3eea691b,0x3ee332d1,0x3edc3555,0x3ed56ee8,0x3eceddd8,0x3ec88080,
    0x3ec2554a,0x3ebc5aaa,0x3eb68f21,0x3eb0f13d,0x3eab7f97,0x3ea638d1,0x3ea11b9b,0x3e9c26ac,
    0x3e9758c7,0x3e92b0ba,0x3e8e2d5a,0x3e89cd85,0x3e859025,0x3e817429,0x3e7af117,0x3e73389a,
    0x3e6bbceb,0x3e647c2d,0x3e5d748f,0x3e56a450,0x3e5009bb,0x3e49a329,0x3e436f01,0x3e3d6bb6,
    0x3e3797c7,0x3e31f1bf,0x3e2c7834,0x3e2729c8,0x3e220527,0x3e1d0909,0x3e18342e,0x3e138560,
    0x3e0efb75,0x3e0a9549,0x3e0651c4,0x3e022fd3,0x3dfc5cdf,0x3df49930,0x3ded12a8,0x3de5c766,
    0x3ddeb598,0x3dd7db78,0x3dd13750,0x3dcac777,0x3dc48a51,0x3dbe7e4e,0x3db8a1ec,0x3db2f3b4,
    0x3dad7239,0x3da81c1c,0x3da2f007,0x3d9decaf,0x3d9910d2,0x3d945b3b,0x3d8fcabb,0x3d8b5e2f,
    0x3d87147b,0x3d82ec8d,0x3d7dcab5,0x3d75fbc5,0x3d6e6a54,0x3d671480,0x3d5ff872,0x3d591463,
    0x3d52669a,0x3d4bed6c,0x3d45a73b,0x3d3f9274,0x3d39ad94,0x3d33f71f,0x3d2e6da9,0x3d290fd0,
    0x3d23dc3b,0x3d1ed19e,0x3d19eeb7,0x3d15324c,0x3d109b2e,0x3d0c2838,0x3d07d84d,0x3d03aa58,
    0x3cff3a9e,0x3cf7605c,0x3cefc3f3,0x3ce8637c,0x3ce13d20,0x3cda4f14,0x3cd3979c,0x3ccd150c,
    0x3cc6c5c2,0x3cc0a82b,0x3cbababf,0x3cb4fc02,0x3caf6a85,0x3caa04e4,0x3ca4c9c5,0x3c9fb7da,
    0x3c9acddd,0x3c960a94,0x3f800000,0x3f7a67ba,0x3f74eec1,0x3f6f9467,0x3f6a57ff,0x3f6538e3,
    0x3f60366e,0x3f5b5000,0x3f5684fd,0x3f51d4ca,0x3f4d3ed3,0x3f48c283,0x3f445f4c,0x3f4014a1,
    0x3f3be1f9,0x3f37c6ce,0x3f33c29c,0x3f2fd4e2,0x3f2bfd23,0x3f283ae4,0x3f248dad,0x3f20f507,
    0x3f1d7080,0x3f19ffa8,0x3f16a210,0x3f13574c,0x3f101ef4,0x3f0cf89f,0x3f09e3ea,0x3f06e071,
    0x3f03edd5,0x3f010bb6,0x3efc7373,0x3ef6ef08,0x3ef1897c,0x3eec4221,0x3ee7184e,0x3ee20b5f,
    0x3edd1ab2,0x3ed845a8,0x3ed38ba7,0x3eceec18,0x3eca6667,0x3ec5fa03,0x3ec1a65e,0x3ebd6aee,
    0x3eb9472c,0x3eb53a94,0x3eb144a2,0x3ead64da,0x3ea99abe,0x3ea5e5d7,0x3ea245ab,0x3e9eb9c9,
    0x3e9b41bf,0x3e97dd1d,0x3e948b76,0x3e914c61,0x3e8e1f76,0x3e8b0450,0x3e87fa89,0x3e8501c3,
    0x3e82199d,0x3e7e8374,0x3e78f37f,0x3e7382a9,0x3e6e3043,0x3e68fba4,0x3e63e425,0x3e5ee922,
    0x3e5a09fd,0x3e55461a,0x3e509cdf,0x3e4c0db9,0x3e479814,0x3e433b63,0x3e3ef719,0x3e3acaaf,
    0x3e36b59e,0x3e32b764,0x3e2ecf81,0x3e2afd79,0x3e2740d0,0x3e239910,0x3e2005c3,0x3e1c8677,
    0x3e191abc,0x3e15c225,0x3e127c46,0x3e0f48b7,0x3e0c2711,0x3e0916f0,0x3e0617f2,0x3e0329b7,
    0x3e004be2,0x3dfafc2d,0x3df57ff6,0x3df0226f,0x3deae2ed,0x3de5c0c7,0x3de0bb5a,0x3ddbd205,
    0x3dd7042a,0x3dd25130,0x3dcdb880,0x3dc93988,0x3dc4d3b7,0x3dc08681,0x3dbc515c,0x3db833c1,
    0x3db42d2d,0x3db03d1f,0x3dac6319,0x3da89ea0,0x3da4ef3a,0x3da15473,0x3d9dcdd7,0x3d9a5af4,
    0x3d96fb5d,0x3d93aea6,0x3d907464,0x3d8d4c32,0x3d8a35a9,0x3d873067,0x3d843c0b,0x3d815837,
    0x3d7d091d,0x3d77816d,0x3d7218ad,0x3f800000,0x3f7c07f5,0x3f781fab,0x3f7446e2,0x3f707d5e,
    0x3f6cc2e3,0x3f691733,0x3f657a15,0x3f61eb4e,0x3f5e6aa7,0x3f5af7e6,0x3f5792d5,0x3f543b3d,
    0x3f50f0e9,0x3f4db3a3,0x3f4a833a,0x3f475f78,0x3f44482d,0x3f413d26,0x3f3e3e33,0x3f3b4b24,
    0x3f3863ca,0x3f3587f6,0x3f32b77a,0x3f2ff22a,0x3f2d37da,0x3f2a885d,0x3f27e388,0x3f254932,
    0x3f22b930,0x3f20335a,0x3f1db788,0x3f1b4590,0x3f18dd4e,0x3f167e99,0x3f14294c,0x3f11dd42,
    0x3f0f9a55,0x3f0d6063,0x3f0b2f46,0x3f0906dc,0x3f06e703,0x3f04cf99,0x3f02c07b,0x3f00b98a,
    0x3efd7547,0x3ef98753,0x3ef5a8f7,0x3ef1d9f6,0x3eee1a13,0x3eea6911,0x3ee6c6b6,0x3ee332c7,
    0x3edfad0c,0x3edc354c,0x3ed8cb4f,0x3ed56edf,0x3ed21fc5,0x3eceddce,0x3ecba8c5,0x3ec88076,
    0x3ec564b0,0x3ec25540,0x3ebf51f5,0x3ebc5aa0,0x3eb96f10,0x3eb68f17,0x3eb3ba88,0x3eb0f134,
    0x3eae32ef,0x3eab7f8d,0x3ea8d6e3,0x3ea638c7,0x3ea3a50f,0x3ea11b91,0x3e9e9c24,0x3e9c26a2,
    0x3e99bae2,0x3e9758be,0x3e95000f,0x3e92b0b0,0x3e906a7d,0x3e8e2d50,0x3e8bf906,0x3e89cd7c,
    0x3e87aa8e,0x3e85901b,0x3e837e02,0x3e817420,0x3e7ee4ab,0x3e7af105,0x3e770d0d,0x3e733887,
    0x3e6f7334,0x3e6bbcd9,0x3e681539,0x3e647c1b,0x3e60f145,0x3e5d747d,0x3e5a058e,0x3e56a43e,
    0x3e535059,0x3e5009a9,0x3e4ccffa,0x3e49a318,0x3e4682d0,0x3e436ef0,0x3e406747,0x3e3d6ba5,
    0x3e3a7bda,0x3e3797b6,0x3e34bf0c,0x3e31f1ae,0x3e2f2f6f,0x3e2c7824,0x3e29cb9f,0x3e2729b8,
    0x3e249243,0x3e220518,0x3e1f820d,0x3e1d08fa,0x3e1a99b7,0x3e18341f,0x3e15d809,0x3e138551,
    0x3e113bd2,0x3e0efb66,0x3e0cc3ea,0x3e0a953b,0x3f800000,0x3f7d2fe8,0x3f7a67ba,0x3f77a75e,
    0x3f74eec0,0x3f723dca,0x3f6f9465,0x3f6cf27e,0x3f6a57fd,0x3f67c4d0,0x3f6538e1,0x3f62b41b,
    0x3f60366b,0x3f5dbfbd,0x3f5b4ffd,0x3f58e718,0x3f5684f9,0x3f54298f,0x3f51d4c7,0x3f4f868d,
    0x3f4d3ecf,0x3f4afd7b,0x3f48c27f,0x3f468dc9,0x3f445f48,0x3f4236e9,0x3f40149d,0x3f3df851,
    0x3f3be1f4,0x3f39d177,0x3f37c6c9,0x3f35c1d8,0x3f33c296,0x3f31c8f2,0x3f2fd4dc,0x3f2de645,
    0x3f2bfd1d,0x3f2a1955,0x3f283ade,0x3f2661a9,0x3f248da6,0x3f22bec9,0x3f20f501,0x3f1f3041,
    0x3f1d707a,0x3f1bb59f,0x3f19ffa1,0x3f184e74,0x3f16a209,0x3f14fa53,0x3f135745,0x3f11b8d2,
    0x3f101eec,0x3f0e8988,0x3f0cf898,0x3f0b6c0f,0x3f09e3e2,0x3f086005,0x3f06e06a,0x3f056506,
    0x3f03edcd,0x3f027ab4,0x3f010baf,0x3eff4164,0x3efc7364,0x3ef9ad48,0x3ef6eef9,0x3ef43862,
    0x3ef1896c,0x3eeee203,0x3eec4211,0x3ee9a981,0x3ee7183e,0x3ee48e35,0x3ee20b4f,0x3edf8f7a,
    0x3edd1aa2,0x3edaacb2,0x3ed84598,0x3ed5e540,0x3ed38b97,0x3ed1388b,0x3eceec08,0x3ecca5fd,
    0x3eca6657,0x3ec82d04,0x3ec5f9f3,0x3ec3cd11,0x3ec1a64e,0x3ebf8598,0x3ebd6ade,0x3ebb5610,
    0x3eb9471c,0x3eb73df3,0x3eb53a84,0x3eb33cbe,0x3eb14492,0x3eaf51f1,0x3ead64ca,0x3eab7d0e,
    0x3ea99aae,0x3ea7bd9c,0x3ea5e5c7,0x3ea41321,0x3ea2459c,0x3ea07d29,0x3e9eb9ba,0x3e9cfb40,
    0x3e9b41af,0x3e998cf8,0x3e97dd0d,0x3e9631e1,0x3e948b66,0x3e92e990,0x3e914c52,0x3e8fb39e,
    0x3e8e1f67,0x3e8c8fa1,0x3e8b0440,0x3e897d37,0x3e87fa7a,0x3e867bfd,0x3e8501b4,0x3e838b92,
    0x3e82198d,0x3e80ab99,0x3e7e8355,0x3e7bb76c,0x3e78f361,0x3f800000,0x3f7e01ff,0x3f7c07f6,
    0x3f7a11dd,0x3f781fad,0x3f76315d,0x3f7446e5,0x3f72603f,0x3f707d62,0x3f6e9e48,0x3f6cc2e7,
    0x3f6aeb3a,0x3f691739,0x3f6746db,0x3f657a1b,0x3f63b0f1,0x3f61eb56,0x3f602942,0x3f5e6aaf,
    0x3f5caf96,0x3f5af7ef,0x3f5943b4,0x3f5792df,0x3f55e567,0x3f543b47,0x3f529478,0x3f50f0f4,
    0x3f4f50b3,0x3f4db3af,0x3f4c19e3,0x3f4a8346,0x3f48efd4,0x3f475f85,0x3f45d254,0x3f44483a,
    0x3f42c132,0x3f413d34,0x3f3fbc3b,0x3f3e3e42,0x3f3cc341,0x3f3b4b33,0x3f39d613,0x3f3863d9,
    0x3f36f482,0x3f358806,0x3f341e60,0x3f32b78b,0x3f315381,0x3f2ff23c,0x3f2e93b6,0x3f2d37eb,
    0x3f2bded5,0x3f2a886f,0x3f2934b2,0x3f27e39b,0x3f269523,0x3f254945,0x3f23fffc,0x3f22b944,
    0x3f217516,0x3f20336e,0x3f1ef447,0x3f1db79c,0x3f1c7d67,0x3f1b45a5,0x3f1a104f,0x3f18dd62,
    0x3f17acd9,0x3f167eae,0x3f1552dd,0x3f142961,0x3f130236,0x3f11dd57,0x3f10bac0,0x3f0f9a6b,
    0x3f0e7c55,0x3f0d6079,0x3f0c46d2,0x3f0b2f5c,0x3f0a1a13,0x3f0906f3,0x3f07f5f6,0x3f06e71a,
    0x3f05da59,0x3f04cfb0,0x3f03c719,0x3f02c092,0x3f01bc16,0x3f00b9a1,0x3eff725d,0x3efd7576,
    0x3efb7c85,0x3ef98782,0x3ef79665,0x3ef5a926,0x3ef3bfbf,0x3ef1da26,0x3eeff854,0x3eee1a42,
    0x3eec3fe9,0x3eea6941,0x3ee89642,0x3ee6c6e6,0x3ee4fb25,0x3ee332f8,0x3ee16e57,0x3edfad3d,
    0x3eddefa1,0x3edc357c,0x3eda7ec9,0x3ed8cb7f,0x3ed71b99,0x3ed56f0f,0x3ed3c5db,0x3ed21ff6,
    0x3ed07d5a,0x3eceddff,0x3ecd41e0,0x3ecba8f6,0x3eca133b,0x3ec880a8,0x3ec6f137,0x3ec564e1,
    0x3ec3dba2,0x3ec25571,0x3ec0d24a,0x3ebf5227,0x3ebdd500,0x3ebc5ad1,0x3f800000,0x3f7e96f6,
    0x3f7d2fe9,0x3f7bcad7,0x3f7a67bc,0x3f790696,0x3f77a762,0x3f764a1d,0x3f74eec5,0x3f739557,
    0x3f723dd0,0x3f70e82d,0x3f6f946c,0x3f6e428a,0x3f6cf285,0x3f6ba45a,0x3f6a5806,0x3f690d87,
    0x3f67c4da,0x3f667dfc,0x3f6538eb,0x3f63f5a5,0x3f62b427,0x3f61746e,0x3f603678,0x3f5efa42,
    0x3f5dbfcb,0x3f5c870f,0x3f5b500c,0x3f5a1abf,0x3f58e727,0x3f57b540,0x3f568509,0x3f55567f,
    0x3f5429a0,0x3f52fe69,0x3f51d4d8,0x3f50aceb,0x3f4f869f,0x3f4e61f2,0x3f4d3ee2,0x3f4c1d6c,
    0x3f4afd8f,0x3f49df47,0x3f48c293,0x3f47a771,0x3f468dde,0x3f4575d8,0x3f445f5d,0x3f434a6b,
    0x3f423700,0x3f412519,0x3f4014b4,0x3f3f05cf,0x3f3df868,0x3f3cec7e,0x3f3be20d,0x3f3ad914,
    0x3f39d190,0x3f38cb81,0x3f37c6e2,0x3f36c3b4,0x3f35c1f3,0x3f34c19d,0x3f33c2b1,0x3f32c52c,
    0x3f31c90d,0x3f30ce52,0x3f2fd4f8,0x3f2edcfe,0x3f2de661,0x3f2cf121,0x3f2bfd3a,0x3f2b0aab,
    0x3f2a1972,0x3f29298e,0x3f283afc,0x3f274dba,0x3f2661c7,0x3f257720,0x3f248dc5,0x3f23a5b3,
    0x3f22bee8,0x3f21d962,0x3f20f520,0x3f201220,0x3f1f3060,0x3f1e4fdf,0x3f1d709a,0x3f1c9290,
    0x3f1bb5bf,0x3f1ada26,0x3f19ffc2,0x3f192693,0x3f184e95,0x3f1777c8,0x3f16a22b,0x3f15cdba,
    0x3f14fa75,0x3f14285a,0x3f135767,0x3f12879b,0x3f11b8f4,0x3f10eb71,0x3f101f0f,0x3f0f53ce,
    0x3f0e89ab,0x3f0dc0a5,0x3f0cf8bb,0x3f0c31eb,0x3f0b6c33,0x3f0aa792,0x3f09e406,0x3f09218e,
    0x3f086029,0x3f079fd4,0x3f06e08e,0x3f062256,0x3f05652b,0x3f04a90a,0x3f03edf2,0x3f0333e2,
    0x3f027ad9,0x3f01c2d5,0x3f010bd4,0x3f0055d5,0x3eff41ae,0x3efdd9b1,0x3efc73af,0x3f800000,
    0x3f7f0080,0x3f7e01ff,0x3f7d047c,0x3f7c07f6,0x3f7b0c6c,0x3f7a11dd,0x3f791848,0x3f781fad,
    0x3f772809,0x3f76315d,0x3f753ba6,0x3f7446e5,0x3f735319,0x3f72603f,0x3f716e58,0x3f707d62,
    0x3f6f8d5d,0x3f6e9e48,0x3f6db021,0x3f6cc2e7,0x3f6bd69b,0x3f6aeb3a,0x3f6a00c4,0x3f691739,
    0x3f682e96,0x3f6746db,0x3f666008,0x3f657a1b,0x3f649514,0x3f63b0f1,0x3f62cdb2,0x3f61eb56,
    0x3f6109db,0x3f602942,0x3f5f4989,0x3f5e6aaf,0x3f5d8cb4,0x3f5caf96,0x3f5bd354,0x3f5af7ef,
    0x3f5a1d65,0x3f5943b4,0x3f586add,0x3f5792df,0x3f56bbb7,0x3f55e567,0x3f550fed,0x3f543b47,
    0x3f536776,0x3f529478,0x3f51c24d,0x3f50f0f4,0x3f50206b,0x3f4f50b3,0x3f4e81ca,0x3f4db3af,
    0x3f4ce662,0x3f4c19e3,0x3f4b4e2f,0x3f4a8346,0x3f49b928,0x3f48efd4,0x3f482748,0x3f475f85,
    0x3f469889,0x3f45d254,0x3f450ce5,0x3f44483a,0x3f438454,0x3f42c132,0x3f41fed2,0x3f413d34,
    0x3f407c58,0x3f3fbc3b,0x3f3efcdf,0x3f3e3e42,0x3f3d8063,0x3f3cc341,0x3f3c06dc,0x3f3b4b33,
    0x3f3a9046,0x3f39d613,0x3f391c9a,0x3f3863d9,0x3f37abd2,0x3f36f482,0x3f363de9,0x3f358806,
    0x3f34d2d9,0x3f341e60,0x3f336a9c,0x3f32b78b,0x3f32052d,0x3f315381,0x3f30a286,0x3f2ff23c,
    0x3f2f42a1,0x3f2e93b6,0x3f2de57a,0x3f2d37ec,0x3f2c8b0a,0x3f2bded5,0x3f2b334d,0x3f2a886f,
    0x3f29de3c,0x3f2934b2,0x3f288bd2,0x3f27e39b,0x3f273c0b,0x3f269523,0x3f25eee1,0x3f254945,
    0x3f24a44e,0x3f23fffc,0x3f235c4e,0x3f22b944,0x3f2216dc,0x3f217516,0x3f20d3f2,0x3f20336e,
    0x3f1f938b,0x3f1ef447,0x3f1e55a2,0x3f1db79c,0x3f1d1a33,0x3f1c7d67,0x3f1be138,0x3f1b45a5,
], dtype=np.uint32).reshape(16, 129)
_LAMC_U32 = np.array([0x00000000,0x114b4ed1,0x1ed189cc,0x28642341,0x2f23c596,0x33f1aaeb,0x374cc1e4,0x39afe10d,0x3b64f2db,0x3c960a98,0x3d72189e,0x3e0a954b,0x3e78f377,0x3ebc5ad9,0x3efc73ac,0x3f1b45a8], dtype=np.uint32)

SERIAL_PHASES = set(
    p for p in os.environ.get("KSERIAL", "").split(",") if p)
SERIAL_ALL = os.environ.get("KSERIAL", "") == "all"


class Buf:
    __slots__ = ("name", "w", "r")

    def __init__(self, name):
        self.name = name
        self.w = {}
        self.r = {}


class Planner:
    DMA_STREAMS = ("syncq", "gq", "scq")
    BODY = {"syncq": "sync", "gq": "gpsimd", "scq": "scalar",
            "pe": "tensor", "dve": "vector", "act": "scalar"}

    def __init__(self):
        self.entries = []
        self.counts = {}
        self.phase = "P0"
        self.prev_any = None

    def set_phase(self, name):
        self.phase = name

    def op(self, stream, fn, reads=(), writes=(), sync_self=False):
        idx = self.counts.get(stream, 0)
        deps = {}

        def add(tok, raw=True):
            if tok is None:
                return
            s, i = tok
            if s == stream and i >= idx:
                return
            if s == stream and not sync_self:
                if not (raw and stream in ("dve", "act")):
                    return
            key = (s, i % self.NLANES) if s in self.DMA_STREAMS else s
            if deps.get(key, -1) < i:
                deps[key] = i
        serial = SERIAL_ALL or (self.phase in SERIAL_PHASES)
        if serial and self.prev_any is not None:
            add(self.prev_any)
        for b in reads:
            for (ws, _), i in b.w.items():
                add((ws, i), raw=True)
        for b in writes:
            for (ws, _), i in b.w.items():
                add((ws, i), raw=False)
            for (rs, _), i in b.r.items():
                add((rs, i), raw=False)
        tok = (stream, idx)
        key = ((stream, idx % self.NLANES)
               if stream in self.DMA_STREAMS else (stream, 0))
        for b in reads:
            if b.r.get(key, -1) < idx:
                b.r[key] = idx
        for b in writes:
            if b.w.get(key, -1) < idx:
                b.w[key] = idx
            b.r = {}
        self.entries.append((stream, fn, deps, self.phase))
        self.counts[stream] = idx + 1
        self.prev_any = tok
        return tok

    def dma(self, queue, fn, reads=(), writes=()):
        return self.op(queue, fn, reads, writes)

    NLANES = 16

    def finalize(self, nc, block, sems):
        NL = self.NLANES
        awaited = set()
        for stream, fn, deps, ph in self.entries:
            for key, i in deps.items():
                st = key[0] if isinstance(key, tuple) else key
                awaited.add((st, i))
        idx_ctr = {s: 0 for s in self.counts}
        incs = []
        cum = {s: 0 for s in self.counts}
        val_after = {}
        last_pos = {}
        for i, (stream, fn, deps, ph) in enumerate(self.entries):
            idx = idx_ctr[stream]
            idx_ctr[stream] += 1
            last_pos[stream] = (i, idx)
            if stream in self.DMA_STREAMS:
                incs.append(16)
            else:
                incs.append(1 if (stream, idx) in awaited else 0)
        for stream, (i, idx) in last_pos.items():
            if incs[i] == 0:
                incs[i] = 1
        idx_ctr = {s: 0 for s in self.counts}
        for i, (stream, fn, deps, ph) in enumerate(self.entries):
            idx = idx_ctr[stream]
            idx_ctr[stream] += 1
            if stream in self.DMA_STREAMS:
                val_after[(stream, idx)] = (idx % NL, 16 * (idx // NL + 1))
            else:
                cum[stream] += incs[i]
                val_after[(stream, idx)] = (None, cum[stream])

        bodies = {b: [] for b in ("sync", "gpsimd", "scalar", "tensor", "vector")}
        idx_ctr = {s: 0 for s in self.counts}
        for i, (stream, fn, deps, ph) in enumerate(self.entries):
            idx = idx_ctr[stream]
            idx_ctr[stream] += 1
            waits = {}
            for key, j in deps.items():
                st = key[0] if isinstance(key, tuple) else key
                lane, v = val_after[(st, j)]
                k2 = (st, lane)
                if waits.get(k2, -1) < v:
                    waits[k2] = v
            if stream in self.DMA_STREAMS:
                inc_sem = sems[stream][idx % NL]
                if idx >= NL:
                    k2 = (stream, idx % NL)
                    if waits.get(k2, -1) < 16 * (idx // NL):
                        waits[k2] = 16 * (idx // NL)
            else:
                inc_sem = sems[stream]
            bodies[self.BODY[stream]].append(
                (stream, fn, list(waits.items()), incs[i], inc_sem))

        totals = []
        for stream, cnt in self.counts.items():
            if stream in self.DMA_STREAMS:
                for lane in range(min(self.NLANES, cnt)):
                    n_lane = (cnt - lane - 1) // NL + 1
                    totals.append((sems[stream][lane], 16 * n_lane))
            else:
                totals.append((sems[stream], cum[stream]))

        def mk_body(body_name):
            def body(eng):
                hw = {}
                for stream, fn, waits, inc, inc_sem in bodies[body_name]:
                    for (s, lane), v in waits:
                        if hw.get((s, lane), -1) >= v:
                            continue
                        hw[(s, lane)] = v
                        sem = sems[s][lane] if lane is not None else sems[s]
                        eng.wait_ge(sem, v)
                    inst = fn()
                    if inc:
                        if stream == "pe":
                            eng.drain(fusable=False).then_inc(inc_sem, inc)
                        else:
                            inst.then_inc(inc_sem, inc)
                if body_name == "gpsimd":
                    for sem, v in totals:
                        eng.wait_ge(sem, v)
            return body

        block.gpsimd(mk_body("gpsimd"))
        block.sync(mk_body("sync"))
        block.vector(mk_body("vector"))
        block.scalar(mk_body("scalar"))
        block.tensor(mk_body("tensor"))
        return totals


def build_program(D, I, T, TP, debug_taps=False):
    H = D // CH
    HCH = D // 128
    NCH = T // CH
    NPCH = TP // CH
    NIB = I // 128
    NQF = D // FC
    TBS = T // FC
    NP6 = T // 256
    HPG = min(H, 8)
    NG = (H + HPG - 1) // HPG
    assert D % FC == 0 and T % 512 == 0 and TP == T and I % 128 == 0

    nc = bass.Bass("TRN2", target_bir_lowering=False)

    def inp(name, shape, dt=F32):
        return nc.dram_tensor(name, shape, dt, kind="ExternalInput").ap()

    xm = inp("xm", [T, D])
    xp = inp("xp", [TP, D])
    maskc = inp("maskc", [128, 1])
    wqkvT = inp("wqkvT", [D, 3 * D])
    bqkv = inp("bqkv", [1, 3 * D])
    woutT = inp("woutT", [D, D])
    bout = inp("bout", [1, D])
    gT = inp("gT", [D, I], I32)
    uT = inp("uT", [D, I], I32)
    dT = inp("dT", [I, D], I32)
    gs = inp("gs", [128, NIB])
    us = inp("us", [128, NIB])
    dsc = inp("dsc", [1, D])
    dkh_in = inp("dkh_in", [128, H])
    dqf_in = inp("dqf_in", [1, H * CH])
    dmaskT_in = inp("dmaskT_in", [128, H * CH])
    lamc_in = inp("lamc_in", [1, H])
    identf = inp("identf", [128, 128])

    out = nc.dram_tensor("out", [T, D], F32, kind="ExternalOutput").ap()

    sk = "ExternalOutput" if debug_taps else "Internal"

    def scratch(name, shape, dt):
        return nc.dram_tensor(name, shape, dt, kind=sk).ap()

    q_s = scratch("q_s", [T, D], BF16)
    k_s = scratch("k_s", [T, D], BF16)
    v_s = scratch("v_s", [T, D], BF16)
    xn_s = scratch("xn_s", [128, HCH * T], BF16)
    ap_s = scratch("ap_s", [T, D], F32)
    hsi_s = scratch("hsi_s", [I, T], BF16)
    wd_s = scratch("wd_s", [I, D], BF16)
    g_bf = scratch("g_bf", [D, I], BF16)
    u_bf = scratch("u_bf", [D, I], BF16)
    st_s = scratch("st_s", [128, 1024], F32)
    xr_s = scratch("xr_s", [128, 64 * 1024], BF16)
    wg_s = scratch("wg_s", [128, 8 * 1024], BF16)

    with ExitStack() as ctx:
        def sbt(name, shape, dtype):
            return ctx.enter_context(nc.sbuf_tensor(name, shape, dtype))

        BIGW = max(HCH * T, 2 * NIB * 256, HCH * D)
        big8 = sbt("big8", [128, BIGW], BF16)
        wstream = sbt("wstream", [128, HCH * FC], BF16)
        xtile = sbt("xtile", [128, D], F32)
        S = sbt("S", [128, D], F32)
        lamcr = sbt("lamcr", [128, D], F32)
        dqf = sbt("dqf", [128, D], BF16)
        dmaskT = sbt("dmaskT", [128, D], BF16)
        dkh = sbt("dkh", [128, H], F32)
        dscr = sbt("dscr", [128, D], F32)
        scT = sbt("scT", [128, D], BF16)
        attnsb = sbt("attnsb", [128, D], BF16)
        qp = sbt("qp", [128, D], BF16)
        Sbf = sbt("Sbf", [128, D], BF16)
        kc = sbt("kc", [128, D], BF16)
        vc = sbt("vc", [128, D], BF16)
        kpc = sbt("kpc", [128, D], BF16)
        kfm = sbt("kfm", [128, D], BF16)
        qch = sbt("qch", [128, D], BF16)
        rowpool = sbt("rowpool", [1, 4 * D], BF16)
        rowAB = nc.alloc_sbuf_tensor_at(
            "rowAB", [1, 2 * D], F32, offset=nc.lookup_mloc(rowpool).addr)
        onesf = sbt("onesf", [1, 128], F32)
        onesb = sbt("onesb", [1, FC], BF16)
        l16 = sbt("l16", [1, H], F32)
        t1 = sbt("t1", [128, D], F32)
        t1b = sbt("t1b", [128, D], F32)
        ap8 = sbt("ap8", [128, D], F32)
        stg = sbt("stg", [128, 2048], BF16)
        sg = sbt("sg", [128, 2 * FC], BF16)
        uu = sbt("uu", [128, 2 * FC], BF16)
        st = sbt("st", [128, 10 * 2 * NCH], F32)
        epsc = sbt("epsc", [128, 1], F32)
        mcol = sbt("mcol", [128, 1], F32)
        gssb = sbt("gssb", [128, NIB], F32)
        ussb = sbt("ussb", [128, NIB], F32)
        identb = sbt("identb", [128, 128], BF16)

        PAt = ctx.enter_context(nc.psum_tensor("PA", [128, 1024], F32))
        PBt = ctx.enter_context(nc.psum_tensor("PB", [128, 1024], F32))
        PCt = ctx.enter_context(nc.psum_tensor("PC", [128, 1024], F32))
        PT_guard = nc.psum_tensor("PT", [128, 2048], BF16)
        PTt = PT_guard.__enter__()
        PT = PTt[:]

        sems = {}
        for s in ("pe", "dve", "act"):
            sems[s] = ctx.enter_context(nc.semaphore("sem_" + s))
        for s in ("syncq", "gq", "scq"):
            sems[s] = [ctx.enter_context(nc.semaphore(f"sem_{s}_{l}"))
                       for l in range(Planner.NLANES)]
        block = ctx.enter_context(nc.Block())
        P = Planner()

        B = {}
        for nm in ("P0B", "wq0", "wq1", "wq2", "wq3",
                   "xt0", "xt1", "st0", "st1", "qch", "kpc", "kfm", "qp",
                   "Sbf", "scT", "attnsb", "S", "t1", "ap8L", "ap8R", "stg", "stgA", "stgB",
                   "kc", "vc", "sg0", "sg1", "uu0", "uu1",
                   "PA0", "PA1", "PB0", "PB1", "PC0", "PC1", "PD0", "PD1", "t1bB",
                   "xn_sB", "ap_sB", "hsi_sB",
                   "wd_sB", "outB", "g_bfB", "u_bfB"):
            B[nm] = Buf(nm)
        NRG = NCH // 2
        QKV_RG = {nm: [Buf(f"{nm}_{g}") for g in range(NRG)]
                  for nm in ("q_sB", "k_sB", "v_sB")}
        BIG8T = [Buf(f"b8_{i}") for i in range(max(NCH, 16))]
        BIG8 = tuple(BIG8T)
        WS_ALL = (B["wq0"], B["wq1"], B["wq2"], B["wq3"])
        AP8 = (B["ap8L"], B["ap8R"])
        PA_ALL = (B["PA0"], B["PA1"])
        PB_ALL = (B["PB0"], B["PB1"])
        PC_ALL = (B["PC0"], B["PC1"])
        PD_ALL = (B["PD0"], B["PD1"])
        C = B["P0B"]

        big8v = big8[:, 0:HCH * T].rearrange("p (c t) -> p c t", t=T)
        wsv = wstream[:].rearrange("p (c f) -> p c f", f=FC)
        rowA = rowAB[:, 0:D]
        rowB = rowAB[:, D:2 * D]
        bqkv_bf = rowpool[:, 0:3 * D]
        bout_bf = rowpool[:, 3 * D:4 * D]
        xt_ap = (xtile[:], ap8[:])
        xt_buf = ((B["xt0"],), AP8)
        kc2vc2 = nc.alloc_sbuf_tensor_at(
            "kc2vc2", [128, 2 * D], BF16, offset=nc.lookup_mloc(ap8).addr)
        kc2 = kc2vc2[:, 0:D]
        vc2 = kc2vc2[:, D:2 * D]

        PHASE = P.set_phase

        PHASE("P0")
        P.dma("syncq", lambda: nc.sync.dma_start(out=dkh[:], in_=dkh_in),
              writes=[C])
        P.dma("syncq", lambda: nc.sync.dma_start(out=dscr[0:1, :], in_=dsc),
              writes=[C])
        P.dma("syncq", lambda: nc.sync.dma_start(out=t1[:], in_=dmaskT_in),
              writes=[C, B["t1"]])
        P.dma("syncq", lambda: nc.sync.dma_start(out=rowB[:, 0:D], in_=dqf_in),
              writes=[C])
        P.dma("syncq", lambda: nc.sync.dma_start(out=l16[:], in_=lamc_in),
              writes=[C])
        P.dma("syncq", lambda: nc.sync.dma_start(out=gssb[:], in_=gs), writes=[C])
        P.dma("syncq", lambda: nc.sync.dma_start(out=ussb[:], in_=us), writes=[C])
        P.dma("syncq", lambda: nc.sync.dma_start(out=mcol[:], in_=maskc),
              writes=[C])
        P.dma("gq", lambda: nc.gpsimd.dma_start(out=identb[:], in_=identf),
              writes=[C])
        P.op("dve", lambda: nc.vector.memset(onesf[:], 1.0), reads=[C], writes=[C])
        P.op("dve", lambda: nc.vector.memset(onesb[:], 1.0), reads=[C], writes=[C])
        P.op("dve", lambda: nc.vector.memset(S[:], 0.0), writes=[B["S"]])
        P.op("dve", lambda: nc.vector.memset(epsc[:], float(EPS)), writes=[C])
        P.op("dve", lambda: nc.vector.tensor_copy(dmaskT[:], t1[:]),
             reads=[C, B["t1"]], writes=[C])
        for i in range(D // FC):
            P.op("pe", lambda i=i: nc.tensor.matmul(
                PAt[:, 0:FC], onesf[:], rowB[:, i * FC:(i + 1) * FC],
                start=True, stop=True), reads=[C], writes=[B["PA0"], C])
            P.op("dve", lambda i=i: nc.vector.tensor_copy(
                dqf[:, i * FC:(i + 1) * FC], PAt[:, 0:FC]),
                reads=[C], writes=[C])
        P.op("dve", lambda: nc.vector.tensor_copy(
            rowA[:, 0:D].rearrange("o (h c) -> o h c", h=H),
            l16[:].unsqueeze(2).broadcast_to([1, H, CH])),
            reads=[C], writes=[C])
        for i in range(D // FC):
            P.op("pe", lambda i=i: nc.tensor.matmul(
                PAt[:, 0:FC], onesf[:], rowA[:, i * FC:(i + 1) * FC],
                start=True, stop=True), reads=[C], writes=[B["PA0"], C])
            P.op("dve", lambda i=i: nc.vector.tensor_copy(
                lamcr[:, i * FC:(i + 1) * FC], PAt[:, 0:FC]),
                reads=[C], writes=[C])
        P.op("dve", lambda: nc.vector.tensor_copy(rowA[:, 0:D], dscr[0:1, :]),
             reads=[C], writes=[C])
        for i in range(D // FC):
            P.op("pe", lambda i=i: nc.tensor.matmul(
                PAt[:, 0:FC], onesf[:], rowA[:, i * FC:(i + 1) * FC],
                start=True, stop=True), reads=[C], writes=[B["PA0"], C])
            P.op("dve", lambda i=i: nc.vector.tensor_copy(
                dscr[:, i * FC:(i + 1) * FC], PAt[:, 0:FC]),
                reads=[C], writes=[C])
        P.dma("gq", lambda: nc.gpsimd.dma_start(out=bqkv_bf, in_=bqkv),
              reads=[C], writes=[C])
        P.dma("gq", lambda: nc.gpsimd.dma_start(out=bout_bf, in_=bout),
              reads=[C], writes=[C])

        def emit_ln_phase(src_dram, slot0):
            STB = [B["st0"], B["st1"]]
            nparts = (D + 1023) // 1024
            dump = (PCt, PCt)
            dumpb = (PC_ALL, PC_ALL)
            for tc in range(NCH):
                par = tc % 2
                xa = xt_ap[par]
                slot = slot0 + tc
                sa = st[:, slot * 10:slot * 10 + 10]
                P.dma("syncq", lambda tc=tc, xa=xa: nc.sync.dma_start(
                    out=xa, in_=src_dram[tc * 128:(tc + 1) * 128, :]),
                    writes=list(xt_buf[par]))
                P.op("dve", lambda xa=xa, sa=sa: nc.vector.reduce_sum(
                    sa[:, 0:1], xa, axis=AX.X),
                    reads=list(xt_buf[par]), writes=STB)
                for ip in range(nparts):
                    w = min(1024, D - ip * 1024)
                    P.op("act", lambda ip=ip, w=w, xa=xa, sa=sa:
                         nc.scalar.activation(
                             dump[ip][:, 0:w], xa[:, ip * 1024:ip * 1024 + w],
                             AF.Square, accum_out=sa[:, 2 + ip:3 + ip]),
                         reads=list(xt_buf[par]), writes=[*dumpb[ip], *STB],
                         sync_self=(ip > 0))
            def sv(col):
                return st[:, slot0 * 10 + col: (slot0 + NCH) * 10: 10]
            if nparts == 2:
                P.op("dve", lambda: nc.vector.tensor_tensor(
                    sv(4), sv(2), sv(3), ALU.add), reads=STB, writes=STB)
            else:
                P.op("dve", lambda: nc.vector.tensor_copy(sv(4), sv(2)),
                     reads=STB, writes=STB)
            P.op("dve", lambda: nc.vector.tensor_scalar(
                sv(1), sv(0), 1.0 / D, None, ALU.mult), reads=STB, writes=STB)
            P.op("dve", lambda: nc.vector.tensor_tensor(
                sv(5), sv(1), sv(1), ALU.mult), reads=STB, writes=STB)
            P.op("dve", lambda: nc.vector.tensor_scalar(
                sv(6), sv(4), 1.0 / D, None, ALU.mult), reads=STB, writes=STB)
            P.op("dve", lambda: nc.vector.tensor_sub(
                sv(7), sv(6), sv(5)), reads=STB, writes=STB)
            P.op("act", lambda: nc.scalar.activation(
                sv(8), sv(7), AF.Sqrt, bias=epsc[:]),
                reads=[C, *STB], writes=STB)
            P.op("dve", lambda: nc.vector.reciprocal(sv(9), sv(8)),
                 reads=STB, writes=STB)
            for tc in range(NCH):
                par = tc % 2
                xa = xt_ap[par]
                slot = slot0 + tc
                sa = st[:, slot * 10:slot * 10 + 10]
                P.dma("syncq", lambda tc=tc, xa=xa: nc.sync.dma_start(
                    out=xa, in_=src_dram[tc * 128:(tc + 1) * 128, :]),
                    writes=list(xt_buf[par]))
                dst = (qch, kpc)[par]
                dstb = (B["qch"], B["kpc"])[par]
                P.op("dve", lambda xa=xa, sa=sa, dst=dst: nc.vector.tensor_scalar(
                    dst[:], xa, sa[:, 1:2], sa[:, 9:10], ALU.subtract, ALU.mult),
                    reads=[*xt_buf[par], *STB], writes=[dstb], sync_self=True)
                for half in range(2 if HCH > 8 else 1):
                    h0 = half * 8
                    h1 = min(HCH, h0 + 8)
                    for hch in range(h0, h1):
                        P.op("pe", lambda hch=hch, dst=dst: nc.tensor.transpose(
                            PT[:, hch * 128:(hch + 1) * 128],
                            dst[:, hch * 128:(hch + 1) * 128], identb[:]),
                            reads=[C, dstb], writes=[PD_ALL[half]])
                    P.op("dve", lambda tc=tc, h0=h0, h1=h1: nc.vector.tensor_copy(
                        big8v[:, h0:h1, tc * 128:(tc + 1) * 128],
                        PT[:, h0 * 128:h1 * 128].rearrange(
                            "p (c t) -> p c t", t=128)),
                        reads=[PD_ALL[half]], writes=[BIG8T[tc]])
                emit_p3b(1)

        def emit_wslab(fcol):
            hh = max(HCH // 2, 1)
            P.dma("gq", lambda fcol=fcol, hh=hh: nc.gpsimd.dma_start(
                out=wstream[:, 0:hh * FC].rearrange("p (c f) -> p c f", f=FC),
                in_=wqkvT.rearrange("(c p) f -> p c f", p=128)[
                    :, 0:hh, fcol * FC:(fcol + 1) * FC]),
                writes=[B["wq0"], B["wq1"]])
            if hh < HCH:
                P.dma("gq", lambda fcol=fcol, hh=hh: nc.gpsimd.dma_start(
                    out=wstream[:, hh * FC:HCH * FC].rearrange(
                        "p (c f) -> p c f", f=FC),
                    in_=wqkvT.rearrange("(c p) f -> p c f", p=128)[
                        :, hh:HCH, fcol * FC:(fcol + 1) * FC]),
                    writes=[B["wq2"], B["wq3"]])

        def ws_bufs(hch):
            return (B["wq0"], B["wq1"]) if hch < max(HCH // 2, 1) \
                else (B["wq2"], B["wq3"])

        def emit_kv_proj(fcols):
            for fcol in fcols:
                emit_wslab(fcol)
                emit_p3b(1)
                if fcol < NQF:
                    dst_s, dstb, foff = q_s, QKV_RG["q_sB"], fcol * FC
                elif fcol < 2 * NQF:
                    dst_s, dstb, foff = k_s, QKV_RG["k_sB"], (fcol - NQF) * FC
                else:
                    dst_s, dstb, foff = v_s, QKV_RG["v_sB"], (fcol - 2 * NQF) * FC
                for tcq in range(NCH // 4):
                    for tci in range(4):
                        tc = tcq * 4 + tci
                        half = tci // 2
                        hb = (B["stgA"], B["stgB"])[half]
                        pab = PA_ALL[tci % 2]
                        pav = PAt[:, (tci % 2) * FC:(tci % 2 + 1) * FC]
                        for hch in range(HCH):
                            P.op("pe", lambda tc=tc, hch=hch, pav=pav:
                                 nc.tensor.matmul(
                                     pav, big8v[:, hch, tc * 128:(tc + 1) * 128],
                                     wsv[:, hch, :],
                                     start=(hch == 0), stop=False),
                                 reads=[BIG8T[tc], *ws_bufs(hch)], writes=[pab])
                        P.op("pe", lambda fcol=fcol, pav=pav: nc.tensor.matmul(
                            pav, onesb[:, 0:128],
                            bqkv_bf[:, fcol * FC:(fcol + 1) * FC],
                            start=False, stop=True),
                            reads=[C], writes=[pab])
                        P.op("act", lambda tci=tci, pav=pav: nc.scalar.copy(
                            stg[:, tci * FC:(tci + 1) * FC], pav),
                            reads=[pab], writes=[hb])
                        if tci % 2 == 1:
                            P.dma("scq", lambda dst_s=dst_s, foff=foff,
                                  tcq=tcq, half=half:
                                  nc.scalar.dma_start(
                                      out=dst_s.rearrange(
                                          "(c p) f -> p c f", p=128)[
                                          :, tcq * 4 + 2 * half:
                                          tcq * 4 + 2 * half + 2,
                                          foff:foff + FC],
                                      in_=stg[:, half * 2 * FC:
                                              (half + 1) * 2 * FC].rearrange(
                                          "p (c f) -> p c f", f=FC)),
                                  reads=[hb], writes=[dstb[tcq * 2 + half]])

        def emit_chunk_loads(j, kdst, kdb, vdst, vdb):
            P.dma("syncq", lambda j=j, kdst=kdst: nc.sync.dma_start(
                out=kdst, in_=k_s[j * 128:(j + 1) * 128, :]),
                reads=[QKV_RG["k_sB"][j // 2]], writes=[kdb])
            P.dma("syncq", lambda j=j, vdst=vdst: nc.sync.dma_start(
                out=vdst, in_=v_s[j * 128:(j + 1) * 128, :]),
                reads=[QKV_RG["v_sB"][j // 2]], writes=[vdb])

        def emit_kdecay(ksrc, kb):
            P.op("dve", lambda ksrc=ksrc: nc.vector.tensor_tensor(
                kpc[:].rearrange("p (h c) -> p h c", h=H),
                ksrc.rearrange("p (h c) -> p h c", h=H),
                dkh[:].unsqueeze(2).broadcast_to([128, H, CH]), ALU.mult),
                reads=[C, kb], writes=[B["kpc"]])

        def emit_A_supdate(vsrc, vb):
            for g in range(NG):
                h0 = g * HPG
                nh = min(HPG, H - h0)
                for i in range(nh):
                    P.op("pe", lambda i=i, h0=h0, vsrc=vsrc: nc.tensor.matmul(
                        PBt[:, i * 128:(i + 1) * 128],
                        kpc[:, (h0 + i) * 128:(h0 + i + 1) * 128],
                        vsrc[:, (h0 + i) * 128:(h0 + i + 1) * 128],
                        start=True, stop=True),
                        reads=[B["kpc"], vb], writes=list(PB_ALL))
                sl = slice(h0 * 128, (h0 + nh) * 128)
                w = nh * 128
                P.op("dve", lambda sl=sl: nc.vector.tensor_tensor(
                    S[:, sl], S[:, sl], lamcr[:, sl], ALU.mult),
                    reads=[C, B["S"]], writes=[B["S"]])
                P.op("dve", lambda sl=sl, w=w: nc.vector.tensor_tensor(
                    S[:, sl], S[:, sl], PBt[:, 0:w], ALU.add),
                    reads=[B["S"], *PB_ALL], writes=[B["S"]])

        def emit_p3b_item(ib):
            PH = P.phase
            P.set_phase("P3b")
            P.dma("gq", lambda ib=ib: nc.gpsimd.dma_start(
                out=attnsb[:, 0:D], in_=dT[ib * 128:(ib + 1) * 128, :]),
                writes=[B["attnsb"]])
            P.op("dve", lambda: nc.vector.tensor_scalar(
                attnsb[:, 0:D], attnsb[:, 0:D], 128.0, None, ALU.subtract),
                reads=[B["attnsb"]], writes=[B["attnsb"]])
            P.op("dve", lambda: nc.vector.tensor_tensor(
                attnsb[:, 0:D], attnsb[:, 0:D], dscr[:, 0:D], ALU.mult),
                reads=[C, B["attnsb"]], writes=[B["attnsb"]])
            P.dma("scq", lambda ib=ib: nc.scalar.dma_start(
                out=wd_s[ib * 128:(ib + 1) * 128, :], in_=attnsb[:, 0:D]),
                reads=[B["attnsb"]], writes=[B["wd_sB"]])
            P.set_phase(PH)
        p3b_next = [0]
        NGU = 16
        gu_next = [0]

        def emit_gu_item(i):
            src, dst, bb = ((gT, g_bf, B["g_bfB"]) if i % 2 == 0
                            else (uT, u_bf, B["u_bfB"]))
            r = (i // 2) * (D // (NGU // 2))
            r1 = r + D // (NGU // 2)
            P.dma("gq", lambda src=src, dst=dst, r=r, r1=r1:
                  nc.gpsimd.dma_start(out=dst[r:r1, :], in_=src[r:r1, :]),
                  writes=[bb])

        def emit_p3b(n):
            while n > 0 and p3b_next[0] < NIB:
                emit_p3b_item(p3b_next[0])
                p3b_next[0] += 1
                n -= 1

        def emit_gu(n):
            pass

        PHASE("P1a")
        emit_ln_phase(xp, 0)
        PHASE("P1b")
        emit_kv_proj(range(NQF, 3 * NQF))
        PHASE("P2")
        emit_ln_phase(xm, NCH)
        P.dma("scq", lambda: nc.scalar.dma_start(out=xn_s, in_=big8[:, 0:HCH * T]),
              reads=list(BIG8), writes=[B["xn_sB"]])

        PHASE("P1c")
        for j in range(NPCH):
            par = j % 2
            kdst, kdb = (kc[:], B["kc"]) if par == 0 else (kc2, B["ap8L"])
            vdst, vdb = (vc[:], B["vc"]) if par == 0 else (vc2, B["ap8R"])
            emit_chunk_loads(j, kdst, kdb, vdst, vdb)
            emit_kdecay(kdst, kdb)
            emit_A_supdate(vdst, vdb)
        P.op("dve", lambda: nc.vector.tensor_scalar(
            S[:], S[:], mcol[:], None, ALU.mult),
            reads=[C, B["S"]], writes=[B["S"]])

        PHASE("P3")
        emit_kv_proj(range(3 * NQF))

        emit_p3b(NIB)

        PHASE("P4")
        P.dma("gq", lambda: nc.gpsimd.dma_start(
            out=big8[:, 0:HCH * D].rearrange("p (c f) -> p c f", f=D),
            in_=woutT.rearrange("(c p) f -> p c f", p=128)),
            writes=list(BIG8))
        wov = big8[:, 0:HCH * D].rearrange("p (c f) -> p c f", f=D)
        qfm_t = nc.alloc_sbuf_tensor_at(
            "qfm", [128, D], BF16, offset=nc.lookup_mloc(t1b).addr)
        qfm = qfm_t[:]
        for j in range(NCH):
            par = j % 2
            kdst, kdb = (kc[:], B["kc"]) if par == 0 else (kc2, B["ap8L"])
            vdst, vdb = (vc[:], B["vc"]) if par == 0 else (vc2, B["ap8R"])
            qdst = qch[:] if par == 0 else stg[:, 0:D]
            qdbs = [B["qch"]] if par == 0 else [B["stgA"], B["stgB"]]
            emit_chunk_loads(j, kdst, kdb, vdst, vdb)
            P.dma("syncq", lambda j=j, qdst=qdst: nc.sync.dma_start(
                out=qdst, in_=q_s[j * 128:(j + 1) * 128, :]),
                reads=[QKV_RG["q_sB"][j // 2]], writes=qdbs)
            emit_kdecay(kdst, kdb)
            for half in range(2 if HCH > 8 else 1):
                h0 = half * 8
                h1 = min(HCH, h0 + 8)
                for hch in range(h0, h1):
                    P.op("pe", lambda hch=hch, kdst=kdst: nc.tensor.transpose(
                        PT[:, hch * 128:(hch + 1) * 128],
                        kdst[:, hch * 128:(hch + 1) * 128], identb[:]),
                        reads=[C, kdb], writes=[PD_ALL[half]])
                P.op("dve", lambda h0=h0, h1=h1: nc.vector.tensor_copy(
                    kfm[:, h0 * 128:h1 * 128], PT[:, h0 * 128:h1 * 128]),
                    reads=[PD_ALL[half]], writes=[B["kfm"]])
            for half in range(2 if HCH > 8 else 1):
                h0 = half * 8
                h1 = min(HCH, h0 + 8)
                for hch in range(h0, h1):
                    P.op("pe", lambda hch=hch, qdst=qdst: nc.tensor.transpose(
                        PT[:, hch * 128:(hch + 1) * 128],
                        qdst[:, hch * 128:(hch + 1) * 128], identb[:]),
                        reads=[C, *qdbs], writes=[PD_ALL[half]])
                P.op("dve", lambda h0=h0, h1=h1: nc.vector.tensor_copy(
                    qfm[:, h0 * 128:h1 * 128], PT[:, h0 * 128:h1 * 128]),
                    reads=[PD_ALL[half]], writes=[B["t1bB"]])
            P.op("dve", lambda: nc.vector.tensor_tensor(
                qp[:].rearrange("p (h c) -> p h c", h=H),
                qfm.rearrange("p (h c) -> p h c", h=H),
                dqf[:].rearrange("p (h c) -> p h c", h=H), ALU.mult),
                reads=[C, B["t1bB"]], writes=[B["qp"]])
            P.op("act", lambda: nc.scalar.copy(Sbf[:], S[:]),
                 reads=[B["S"]], writes=[B["Sbf"]])
            for g in range(NG):
                h0 = g * HPG
                nh = min(HPG, H - h0)
                gsl = slice(h0 * 128, (h0 + nh) * 128)
                gw = nh * 128
                for i in range(nh):
                    h = h0 + i
                    P.op("pe", lambda i=i, h=h: nc.tensor.matmul(
                        PAt[:, i * 128:(i + 1) * 128],
                        kfm[:, h * 128:(h + 1) * 128],
                        qfm[:, h * 128:(h + 1) * 128],
                        start=True, stop=True),
                        reads=[B["kfm"], B["t1bB"]], writes=list(PA_ALL))
                P.op("dve", lambda gsl=gsl, gw=gw: nc.vector.tensor_tensor(
                    scT[:, gsl], PAt[:, 0:gw], dmaskT[:, gsl], ALU.mult),
                    reads=[C, *PA_ALL], writes=[B["scT"]])
                for i in range(nh):
                    h = h0 + i
                    P.op("pe", lambda i=i, h=h: nc.tensor.matmul(
                        PBt[:, i * 128:(i + 1) * 128],
                        Sbf[:, h * 128:(h + 1) * 128],
                        qp[:, h * 128:(h + 1) * 128],
                        start=True, stop=False),
                        reads=[B["Sbf"], B["qp"]], writes=list(PB_ALL))
                    P.op("pe", lambda i=i, h=h, vdst=vdst: nc.tensor.matmul(
                        PBt[:, i * 128:(i + 1) * 128],
                        vdst[:, h * 128:(h + 1) * 128],
                        scT[:, h * 128:(h + 1) * 128],
                        start=False, stop=True),
                        reads=[vdb, B["scT"]], writes=list(PB_ALL))
                P.op("act", lambda gsl=gsl, gw=gw: nc.scalar.copy(
                    attnsb[:, gsl], PBt[:, 0:gw]),
                    reads=list(PB_ALL), writes=[B["attnsb"]])
                for i in range(nh):
                    h = h0 + i
                    P.op("pe", lambda i=i, h=h, vdst=vdst: nc.tensor.matmul(
                        PAt[:, i * 128:(i + 1) * 128],
                        kpc[:, h * 128:(h + 1) * 128],
                        vdst[:, h * 128:(h + 1) * 128],
                        start=True, stop=True),
                        reads=[B["kpc"], vdb], writes=list(PA_ALL))
                P.op("dve", lambda gsl=gsl: nc.vector.tensor_tensor(
                    S[:, gsl], S[:, gsl], lamcr[:, gsl], ALU.mult),
                    reads=[C, B["S"], B["Sbf"]], writes=[B["S"]])
                P.op("dve", lambda gsl=gsl, gw=gw: nc.vector.tensor_tensor(
                    S[:, gsl], S[:, gsl], PAt[:, 0:gw], ALU.add),
                    reads=[B["S"], *PA_ALL], writes=[B["S"]])
            for ph in range((D + 1023) // 1024):
                pw = min(1024, D - ph * 1024)
                pcb = PC_ALL[ph % 2]
                for fsub in range(pw // FC):
                    fo = ph * 1024 + fsub * FC
                    pcv = PCt[:, (ph % 2) * 0 + fsub * FC:fsub * FC + FC] \
                        if False else PCt[:, fsub * FC:(fsub + 1) * FC]
                    for dch in range(HCH):
                        P.op("pe", lambda fsub=fsub, fo=fo, dch=dch:
                             nc.tensor.matmul(
                                 PCt[:, fsub * FC:(fsub + 1) * FC],
                                 attnsb[:, dch * 128:(dch + 1) * 128],
                                 wov[:, dch, fo:fo + FC],
                                 start=(dch == 0), stop=False),
                             reads=[B["attnsb"], *BIG8], writes=list(PC_ALL))
                    P.op("pe", lambda fsub=fsub, fo=fo: nc.tensor.matmul(
                        PCt[:, fsub * FC:(fsub + 1) * FC],
                        onesb[:, 0:128], bout_bf[:, fo:fo + FC],
                        start=False, stop=True),
                        reads=[C], writes=list(PC_ALL))
                P.op("dve", lambda ph=ph, pw=pw: nc.vector.tensor_copy(
                    t1[:, ph * 1024:ph * 1024 + pw], PCt[:, 0:pw]),
                    reads=list(PC_ALL), writes=[B["t1"]])
            P.dma("scq", lambda j=j: nc.scalar.dma_start(
                out=ap_s[j * 128:(j + 1) * 128, :], in_=t1[:]),
                reads=[B["t1"]], writes=[B["ap_sB"]])

        PHASE("P5")
        P.dma("gq", lambda: nc.gpsimd.dma_start(out=big8[:, 0:HCH * T], in_=xn_s),
              reads=[B["xn_sB"]], writes=list(BIG8))
        for ib in range(NIB):
            par = ib % 2
            goff = par * 2 * HCH * 128
            uoff = goff + HCH * 128
            gb = B["wq0"] if par == 0 else B["wq2"]
            ub = B["wq1"] if par == 0 else B["wq3"]
            P.dma("gq", lambda ib=ib, goff=goff: nc.gpsimd.dma_start(
                out=wstream[:, goff:goff + HCH * 128].rearrange(
                    "p (c f) -> p c f", f=128),
                in_=gT.rearrange("(c p) i -> p c i", p=128)[
                    :, :, ib * 128:(ib + 1) * 128]),
                writes=[gb])
            P.dma("gq", lambda ib=ib, uoff=uoff: nc.gpsimd.dma_start(
                out=wstream[:, uoff:uoff + HCH * 128].rearrange(
                    "p (c f) -> p c f", f=128),
                in_=uT.rearrange("(c p) i -> p c i", p=128)[
                    :, :, ib * 128:(ib + 1) * 128]),
                writes=[ub])
            P.op("dve", lambda goff=goff: nc.vector.tensor_scalar(
                wstream[:, goff:goff + 2 * HCH * 128],
                wstream[:, goff:goff + 2 * HCH * 128],
                128.0, None, ALU.subtract),
                reads=[gb, ub], writes=[gb, ub])
            for tb in range(TBS):
                tpar = tb % 2
                pab = PA_ALL[tpar]
                pbb = PB_ALL[tpar]
                pav = PAt[:, tpar * FC:(tpar + 1) * FC]
                pbv = PBt[:, tpar * FC:(tpar + 1) * FC]
                for hch in range(HCH):
                    P.op("pe", lambda hch=hch, tb=tb, goff=goff, pav=pav:
                         nc.tensor.matmul(
                             pav, wstream[:, goff + hch * 128:
                                          goff + (hch + 1) * 128],
                             big8v[:, hch, tb * FC:(tb + 1) * FC],
                             start=(hch == 0), stop=(hch == HCH - 1)),
                         reads=[gb, *BIG8], writes=[pab])
                for hch in range(HCH):
                    P.op("pe", lambda hch=hch, tb=tb, uoff=uoff, pbv=pbv:
                         nc.tensor.matmul(
                             pbv, wstream[:, uoff + hch * 128:
                                          uoff + (hch + 1) * 128],
                             big8v[:, hch, tb * FC:(tb + 1) * FC],
                             start=(hch == 0), stop=(hch == HCH - 1)),
                         reads=[ub, *BIG8], writes=[pbb])
                sgb = (B["sg0"], B["sg1"])[tpar]
                uub = (B["uu0"], B["uu1"])[tpar]
                P.op("act", lambda ib=ib, tpar=tpar, pav=pav:
                     nc.scalar.activation(
                         sg[:, tpar * FC:(tpar + 1) * FC], pav, AF.Silu,
                         scale=gssb[:, ib:ib + 1]),
                     reads=[C, pab], writes=[sgb])
                P.op("act", lambda ib=ib, tpar=tpar, pbv=pbv:
                     nc.scalar.activation(
                         uu[:, tpar * FC:(tpar + 1) * FC], pbv, AF.Copy,
                         scale=ussb[:, ib:ib + 1]),
                     reads=[C, pbb], writes=[uub])
                shb = (B["stgA"], B["stgB"])[tb // 2]
                P.op("dve", lambda tb=tb, tpar=tpar: nc.vector.tensor_tensor(
                    stg[:, tb * FC:(tb + 1) * FC],
                    sg[:, tpar * FC:(tpar + 1) * FC],
                    uu[:, tpar * FC:(tpar + 1) * FC],
                    ALU.mult), reads=[sgb, uub], writes=[shb])
                P.dma("scq", lambda ib=ib, tb=tb: nc.scalar.dma_start(
                    out=hsi_s[ib * 128:(ib + 1) * 128, tb * FC:(tb + 1) * FC],
                    in_=stg[:, tb * FC:(tb + 1) * FC]),
                    reads=[shb], writes=[B["hsi_sB"]])

        PHASE("P6")
        PT_guard.__exit__(None, None, None)
        PDt = ctx.enter_context(nc.psum_tensor("PD", [128, 1024], F32))
        PSUMS = [(PAt, PA_ALL), (PBt, PB_ALL), (PCt, PC_ALL), (PDt, PD_ALL)]

        def emit_slab(tp):
            spar = tp % 2
            soff = spar * NIB * 256
            slb = BIG8T[spar * 8:spar * 8 + 8]
            P.dma("syncq", lambda tp=tp, soff=soff: nc.sync.dma_start(
                out=big8[:, soff:soff + NIB * 256].rearrange(
                    "p (b t) -> p b t", t=256),
                in_=hsi_s.rearrange("(b p) t -> p b t", p=128)[
                    :, :, tp * 256:(tp + 1) * 256]),
                reads=[B["hsi_sB"]], writes=slb)

        emit_slab(0)
        for tp in range(NP6):
            PHASE("P6")
            spar = tp % 2
            soff = spar * NIB * 256
            slab = big8[:, soff:soff + NIB * 256].rearrange(
                "p (b t) -> p b t", t=256)
            slbufs = BIG8T[spar * 8:spar * 8 + 8]
            for ib in range(NIB):
                wpar = ib % 4
                wdst = (vc[:], kc[:], kpc[:], kfm[:])[wpar]
                wdb = (B["vc"], B["kc"], B["kpc"], B["kfm"])[wpar]
                wq = ("gq", "syncq")[ib % 2]
                dmafn = (nc.gpsimd, nc.sync)[ib % 2]
                P.dma(wq, lambda ib=ib, wdst=wdst, dmafn=dmafn: dmafn.dma_start(
                    out=wdst, in_=wd_s[ib * 128:(ib + 1) * 128, :]),
                    reads=[B["wd_sB"]], writes=[wdb])
                for tcs in range(2):
                    for fq in range(D // FC):
                        pi = tcs * 2 + fq // 2 if D > 1024 else tcs
                        pst, psb = PSUMS[pi]
                        po = (fq % 2) * FC if D > 1024 else fq * FC
                        P.op("pe", lambda ib=ib, tcs=tcs, fq=fq, pst=pst,
                             po=po, wdst=wdst, slab=slab: nc.tensor.matmul(
                                 pst[:, po:po + FC],
                                 slab[:, ib, tcs * 128:(tcs + 1) * 128],
                                 wdst[:, fq * FC:(fq + 1) * FC],
                                 start=(ib == 0), stop=(ib == NIB - 1)),
                             reads=slbufs + [wdb], writes=list(psb))
            if tp + 1 < NP6:
                emit_slab(tp + 1)
            PHASE("P6m")
            for tcs in range(2):
                j = tp * 2 + tcs
                ev = (t1, t1b)[tcs]
                evb = (B["t1"], B["t1bB"])[tcs]
                P.dma("syncq", lambda j=j: nc.sync.dma_start(
                    out=ap8[:], in_=ap_s[j * 128:(j + 1) * 128, :]),
                    reads=[B["ap_sB"]], writes=list(AP8))
                P.dma("syncq", lambda j=j: nc.sync.dma_start(
                    out=xtile[:], in_=xm[j * 128:(j + 1) * 128, :]),
                    writes=[B["xt0"]])
                nhal = (D + 1023) // 1024
                for ph in range(nhal):
                    pw = min(1024, D - ph * 1024)
                    pi = tcs * 2 + ph if D > 1024 else tcs
                    pst, psb = PSUMS[pi]
                    P.op("dve", lambda ph=ph, pw=pw, pst=pst, ev=ev:
                         nc.vector.tensor_copy(
                             ev[:, ph * 1024:ph * 1024 + pw], pst[:, 0:pw]),
                         reads=[*psb], writes=[evb])
                P.op("dve", lambda ev=ev: nc.vector.tensor_tensor(
                    ev[:], ev[:], ap8[:], ALU.add),
                    reads=[evb, *AP8], writes=[evb], sync_self=True)
                P.op("dve", lambda ev=ev: nc.vector.tensor_tensor(
                    ev[:], ev[:], xtile[:], ALU.add),
                    reads=[evb, B["xt0"]], writes=[evb])
                P.dma("scq", lambda j=j, ev=ev: nc.scalar.dma_start(
                    out=out[j * 128:(j + 1) * 128, :], in_=ev[:]),
                    reads=[evb], writes=[B["outB"]])


        P.finalize(nc, block, sems)
        rem = nc.sbuf_bytes_remaining
        print(f"[build] D={D} I={I} T={T}: {len(P.entries)} ops, "
              f"sbuf_remaining={rem}B/part")
    return nc


_CACHE = {}


def _get_program(D, I, T, TP, debug_taps=False):
    key = (D, I, T, TP, debug_taps)
    if key not in _CACHE:
        _CACHE[key] = build_program(D, I, T, TP, debug_taps)
    return _CACHE[key]


def _decay_tables(H):
    if H == 16:
        lamn = _LAMN_U32.view(np.float32)
        lamc = _LAMC_U32.view(np.float32)
    else:
        slopes = (2.0 ** (-(np.arange(H) + 1) / 2.0)).astype(np.float32)
        lam64 = np.exp(-slopes.astype(np.float64)).astype(np.float32).astype(np.float64)
        lamn = (lam64[:, None] ** np.arange(129)).astype(np.float32)
        lamc = np.ascontiguousarray(lamn[:, 128])
    return lamn, lamc


def make_const_inputs(D, H):
    pos = np.arange(CH)
    lamn, lamc = _decay_tables(H)
    dmt = np.where(pos[None, :] >= pos[:, None],
                   lamn[:, np.maximum(pos[None, :] - pos[:, None], 0)],
                   np.float32(0))
    return dict(
        dkh_in=np.ascontiguousarray(lamn[:, 127 - pos].T),
        dqf_in=np.ascontiguousarray(lamn[:, 1:129].reshape(1, H * CH)),
        dmaskT_in=np.ascontiguousarray(
            np.transpose(dmt, (1, 0, 2)).reshape(128, H * CH)),
        lamc_in=np.ascontiguousarray(lamc.reshape(1, H)),
        identf=np.eye(128, dtype=np.float32),
    )


def make_core_inputs(x_full, b, h, weights, D, H, I, T, TP):
    d = dict(weights)
    d.update(make_const_inputs(D, H))
    d["xm"] = np.ascontiguousarray(x_full[b, h * T:(h + 1) * T, :])
    d["xp"] = np.ascontiguousarray(x_full[b, 0:TP, :])
    d["maskc"] = np.full((128, 1), 1.0 if h == 1 else 0.0, np.float32)
    return d


def prep_weights(w_qkv, b_qkv, w_out, b_out, gate_w, gate_scale, up_w, up_scale,
                 down_w, down_scale, D, I):
    NIB = I // 128
    return dict(
        wqkvT=np.ascontiguousarray(w_qkv.T.astype(np.float32, copy=False)),
        bqkv=np.ascontiguousarray(b_qkv.reshape(1, -1).astype(np.float32, copy=False)),
        woutT=np.ascontiguousarray(w_out.T.astype(np.float32, copy=False)),
        bout=np.ascontiguousarray(b_out.reshape(1, -1).astype(np.float32, copy=False)),
        gT=np.ascontiguousarray(np.asarray(gate_w).T.astype(np.int32, copy=False)),
        uT=np.ascontiguousarray(np.asarray(up_w).T.astype(np.int32, copy=False)),
        dT=np.ascontiguousarray(np.asarray(down_w).T.astype(np.int32, copy=False)),
        gs=np.ascontiguousarray(
            np.asarray(gate_scale, np.float32).reshape(NIB, 128).T),
        us=np.ascontiguousarray(
            np.asarray(up_scale, np.float32).reshape(NIB, 128).T),
        dsc=np.ascontiguousarray(
            np.asarray(down_scale, np.float32).reshape(1, -1)),
    )


def kernel(L0_attn_x, L0_attn_w_qkv, L0_attn_b_qkv, L0_attn_w_out, L0_attn_b_out,
           L0_ffn_gate_w, L0_ffn_gate_scale, L0_ffn_up_w, L0_ffn_up_scale,
           L0_ffn_down_w, L0_ffn_down_scale):
    x = np.asarray(L0_attn_x, dtype=np.float32)
    B, N, D = x.shape
    H = D // CH
    I = np.asarray(L0_ffn_gate_w).shape[0]
    T = N // 2
    TP = T
    weights = prep_weights(
        np.asarray(L0_attn_w_qkv, np.float32), np.asarray(L0_attn_b_qkv, np.float32),
        np.asarray(L0_attn_w_out, np.float32), np.asarray(L0_attn_b_out, np.float32),
        L0_ffn_gate_w, L0_ffn_gate_scale, L0_ffn_up_w, L0_ffn_up_scale,
        L0_ffn_down_w, L0_ffn_down_scale, D, I)

    nc = _get_program(D, I, T, TP)
    in_maps = []
    for c in range(8):
        b, h = c // 2, c % 2
        in_maps.append(make_core_inputs(x, b, h, weights, D, H, I, T, TP))
    res = run_bass_kernel_spmd(nc, in_maps, core_ids=list(range(8)))
    outp = np.empty((B, N, D), dtype=np.float32)
    for c in range(8):
        b, h = c // 2, c % 2
        outp[b, h * T:(h + 1) * T, :] = np.asarray(res.results[c]["out"])
    return outp
